# version 21
# baseline (speedup 1.0000x reference)
import contextlib
import numpy as np

import concourse.bass as bass
import concourse.mybir as mybir
from concourse.bass_utils import run_bass_kernel_spmd
from concourse.aot_env import allow_ici_all_to_all

F32 = mybir.dt.float32
BF16 = mybir.dt.bfloat16
AF = mybir.ActivationFunctionType
ALU = mybir.AluOpType
AX = mybir.AxisListType

ENGS = ("te", "ve", "sc", "gp", "sy")
_LASTPROG = None


def real_cfg():
    return dict(B=4, T=2048, H=4096, DIN=8192, NST=16, KCV=4, R=256, NC=8,
                TAU=0.1, ITERS=50, NSTR=4,
                EPS_LN=1e-5, EPS_RMS_A=1e-6, EPS_RMS_F=1.1920929e-07)


def derived(c):
    d = dict(c)
    d["DSL"] = c["DIN"] // c["NC"]
    d["NT_D"] = d["DSL"] // 128
    d["KH"] = c["H"] // 128
    d["KD"] = c["DIN"] // 128
    d["MODC"] = 6 * c["H"] // c["NC"]
    d["QT"] = c["T"] // 4
    d["NT_Q"] = d["QT"] // 128
    d["TC"] = 256
    d["NTC"] = c["T"] // d["TC"]
    d["THL"] = c["T"] // 2
    d["DTBLK"] = min(4, d["NT_D"])
    d["NBLK"] = max(1, d["NT_D"] // d["DTBLK"])
    d["E"] = c["R"] + 2 * c["NST"]
    d["RT"] = (c["R"] + 127) // 128
    d["RP"] = min(c["R"], 128)
    d["CD"] = c["H"] // c["NSTR"]
    d["KCD"] = d["CD"] // 128
    d["NCP"] = 2 * d["DSL"] // 256
    d["G"] = 2 * d["NT_Q"]
    d["WBN"] = min(512, d["CD"])
    return d


class Buf:
    __slots__ = ("ready", "users")

    def __init__(self):
        self.ready = None
        self.users = []


class Prog:
    NROT = {"dma": 16, "cc": 8}

    def __init__(self):
        self.stream = {e: [] for e in ENGS}
        self.cnt = {}
        self.bufs = {}
        self.qcnt = {}
        self.evwm = {}

    def B(self, key):
        b = self.bufs.get(key)
        if b is None:
            b = self.bufs[key] = Buf()
        return b

    def op(self, eng, fn, reads=(), writes=(), after=(), kind="compute"):
        deps = [a for a in after if a is not None]
        if kind == "compute":
            sem, inc = eng, 1
            self.cnt[sem] = self.cnt.get(sem, 0) + inc
        else:
            if kind == "dma":
                qbase, inc = {"sy": "syd", "gp": "gpd", "sc": "scd"}[eng], 16
            else:
                qbase, inc = "cc", 1
            nrot = self.NROT[kind]
            idx = self.qcnt.get(qbase, 0)
            self.qcnt[qbase] = idx + 1
            sem = f"{qbase}{idx % nrot}"
            prev = self.cnt.get(sem, 0)
            if prev > 0:
                deps.append((sem, prev))
            self.cnt[sem] = prev + inc
        ev = (sem, self.cnt[sem])
        for b in reads:
            if b.ready is not None:
                deps.append(b.ready)
        for b in writes:
            deps.extend(b.users)
            if b.ready is not None:
                deps.append(b.ready)
        wm = {}
        for s, v in deps:
            if wm.get(s, 0) < v:
                wm[s] = v
        for b in reads:
            b.users.append(ev)
        for b in writes:
            b.ready = ev
            b.users = []
        self.stream[eng].append((tuple(sorted(wm.items())), fn, sem, inc))
        return ev

    def check_deadlock(self):
        ptr = {e: 0 for e in ENGS}
        val = {}
        progressed = True
        while progressed:
            progressed = False
            for e in ENGS:
                while ptr[e] < len(self.stream[e]):
                    waits, _, sem, inc = self.stream[e][ptr[e]]
                    if any(val.get(s, 0) < v for s, v in waits):
                        break
                    val[sem] = val.get(sem, 0) + inc
                    ptr[e] += 1
                    progressed = True
        stuck = {e: (ptr[e], len(self.stream[e])) for e in ENGS if ptr[e] < len(self.stream[e])}
        if stuck:
            msgs = []
            for e, (p, n) in stuck.items():
                waits = self.stream[e][p][0]
                bad = [(s, v, val.get(s, 0)) for s, v in waits if val.get(s, 0) < v]
                msgs.append(f"{e}@{p}/{n} unmet={bad[:4]}")
            raise RuntimeError("deadlock: " + "; ".join(msgs))

    def realize(self, sems, block):
        def mk(e):
            def body(eng):
                maxw = {}
                for waits, fn, sem, inc in self.stream[e]:
                    for s, v in waits:
                        if maxw.get(s, 0) < v:
                            eng.wait_ge(sems[s], v)
                            maxw[s] = v
                    fn(eng).then_inc(sems[sem], inc)
            return body
        block.sync(mk("sy"))
        block.tensor(mk("te"))
        block.vector(mk("ve"))
        block.scalar(mk("sc"))
        block.gpsimd(mk("gp"))


class Tile:

    def __init__(self, P, t, name, cols):
        self.P = P
        self.t = t
        self.name = name
        self.cols = cols
        self.ngrid = (cols + 511) // 512

    def bufs(self, lo, hi):
        return [self.P.B((self.name, i)) for i in range(lo // 512, (hi + 511) // 512)]

    def f(self, lo, hi):
        return self.t[:, lo:hi], self.bufs(lo, hi)

    def bf(self, lo, hi_bf):
        ap = self.t[:, lo:self.cols].bitcast(BF16)[:, 0:hi_bf]
        return ap, self.bufs(lo, lo + (hi_bf + 1) // 2)


def build_nc(cfg):
    c = derived(cfg)
    B, T, H, DIN, NST, KCV, R, NC = (c["B"], c["T"], c["H"], c["DIN"], c["NST"],
                                     c["KCV"], c["R"], c["NC"])
    DSL, NT_D, KH, KD, MODC = c["DSL"], c["NT_D"], c["KH"], c["KD"], c["MODC"]
    QT, NT_Q, TC, NTC, THL = c["QT"], c["NT_Q"], c["TC"], c["NTC"], c["THL"]
    DTBLK, NBLK, E, RT, RP = c["DTBLK"], c["NBLK"], c["E"], c["RT"], c["RP"]
    CD, KCD, NCP, G, WBN = c["CD"], c["KCD"], c["NCP"], c["G"], c["WBN"]
    NSTR, TAU, ITERS = c["NSTR"], c["TAU"], c["ITERS"]
    NQB = THL // QT

    nc = bass.Bass("TRN2", target_bir_lowering=False, detect_race_conditions=False)
    P = Prog()
    bB = P.B

    def din(name, shape, dtype=F32):
        return nc.dram_tensor(name, list(shape), dtype, kind="ExternalInput")

    def dint(name, shape, dtype=F32):
        return nc.dram_tensor(name, list(shape), dtype, kind="Internal")

    x_my = din("x_my", (2, QT, H))
    cT_in = din("cT_in", (H, B))
    wmod_sl = din("wmod_sl", (H, MODC))
    win_t = din("win_t", (NCP, KH, 128, 256))
    xpwT_sl = din("xpwT_sl", (DSL, E))
    dtpwT_sl = din("dtpwT_sl", (RT, RP, DSL))
    dtb_sl = din("dtb_sl", (DSL,))
    A_sl = din("A_sl", (DSL, NST))
    D_sl = din("D_sl", (DSL,))
    convw_sl = din("convw_sl", (DSL, KCV))
    convb_sl = din("convb_sl", (DSL,))
    wout_sl = din("wout_sl", (DSL, H))
    dt_ln = din("dt_ln", (R,))
    B_ln = din("B_ln", (NST,))
    C_ln = din("C_ln", (NST,))
    wres_t = din("wres_t", (H, 24))
    base_cat = din("base_cat", (24,))
    wb_in = din("wb_in", (CD, CD))
    gate_in = din("gate_in", (1, 1))
    eye_f = din("eye_f", (128, 128))
    eye4_in = din("eye4_in", (NST,))

    out_my = nc.dram_tensor("out_my", [2, QT, H], F32, kind="ExternalOutput")

    win_bf = dint("win_bf", (NCP, KH, 128, 256), BF16)
    wout_bf = dint("wout_bf", (DSL, H), BF16)
    wout_all = dint("wout_all", (NC, DSL, H), BF16)
    wmod_bf = dint("wmod_bf", (H, MODC), BF16)
    wb_bf = dint("wb_bf", (CD, CD), BF16)
    xpwT_bf = dint("xpwT_bf", (DSL, E), BF16)
    dtpwT_bf = dint("dtpwT_bf", (RT, RP, DSL), BF16)
    wres_bf = dint("wres_bf", (H, 24), BF16)
    mod_part = dint("mod_part", (NC, 2, MODC))
    mod_my = dint("mod_my", (NC, 2, MODC))
    hT_my = [dint(f"hT_my{p}", (H, QT), BF16) for p in range(2)]
    hT_all = [dint(f"hT_all{p}", (NC, H, QT), BF16) for p in range(2)]
    ssm_in = [dint(f"ssm_in{b}", (T, E)) for b in range(B)]
    ssm_out = [dint(f"ssm_out{b}", (T, E)) for b in range(B)]
    x_sp = dint("x_sp", (B, DSL, T))
    z_sp = dint("z_sp", (B, DSL, T))
    xc_sp = dint("xc_sp", (B, DSL, T))
    xcbf_sp = dint("xcbf_sp", (B, DSL, T), BF16)
    dt_sp = dint("dt_sp", (B, DSL, T))
    Bt_i = dint("Bt_i", (B, NST, T))
    Ct_i = dint("Ct_i", (B, NST, T))
    g_dr = dint("g_dr", (1, 1))
    y_in = [dint(f"y_in{p}", (NC, DSL, QT), BF16) for p in range(2)]
    y_my = [dint(f"y_my{p}", (NC, DSL, QT), BF16) for p in range(2)]
    x2_sp = dint("x2_sp", (2, QT, H))
    h2_sp = dint("h2_sp", (2, QT, H))

    ctx = contextlib.ExitStack()

    def TT(name, cols):
        t = ctx.enter_context(nc.sbuf_tensor(name, [128, cols], F32))
        return Tile(P, t, name, cols)

    phA = TT("phA", 4096)
    phB = TT("phB", 4096)
    phB2 = TT("phB2", 4096)
    phC = TT("phC", 4096)
    phD = TT("phD", 4096)
    phE = TT("phE", 4096)
    phF = TT("phF", 3072)
    phG = TT("phG", 4096)
    phH = TT("phH", 2048)
    phI = TT("phI", 2048)
    phJ = TT("phJ", 2048)
    phK = TT("phK", 3072)
    phL = TT("phL", 2048)
    phM = TT("phM", 3072)
    smT = TT("sm", 5120)
    dtpwT_t = ctx.enter_context(nc.sbuf_tensor("dtpwT_sb", [RP, RT * DSL], BF16))
    eyef_t = ctx.enter_context(nc.sbuf_tensor("eyef_sb", [128, 128], F32))

    _smoff = [0]

    def sml(name, cols):
        off = _smoff[0]
        _smoff[0] += cols
        assert _smoff[0] <= 5120, f"sm overflow at {name}"
        return smT.t[:, off:off + cols], [bB(("sm", name))]

    A_sb, bA_sb = sml("A", NT_D * NST)
    D_sb, bD_sb = sml("D", NT_D)
    dtb_sb, bdtb = sml("dtb", NT_D)
    cvb_sb, bcvb = sml("cvb", NT_D)
    cvw_sb0, bcvw = sml("cvw", NT_D * KCV)
    cvw_sb = cvw_sb0.rearrange("p (d k) -> p d k", k=KCV)
    dtln_bc, bdtln = sml("dtln", R)
    Bln_bc, bBln = sml("Bln", NST)
    Cln_bc, bCln = sml("Cln", NST)
    base_bc, bbase = sml("base", 24)
    eye4_bc, beye4 = sml("eye4", NST)
    g_bc, bg = sml("g", 1)
    gm1_bc, bgm1 = sml("gm1", 1)
    s1_t, bs1 = sml("s1", 1)
    s2_t, bs2 = sml("s2", 1)
    mu_t, bmu = sml("mu", 1)
    var_t, bvar = sml("var", 1)
    rstd_t, brstd = sml("rstd", 1)
    rho_t, brho = sml("rho", 3)
    sstate, bsst = sml("sstate", NT_D * NST)
    ct_f, bctf = sml("ct_f", KH * B)
    ct_b0, bctb = sml("ct_b", (KH * B + 1) // 2)
    ct_bf = ct_b0.bitcast(BF16)[:, 0:KH * B]
    btct, bbtct = sml("btct", 128)
    xcb_t0, bxcb0 = sml("xcb0", 128)
    xcb_t1, bxcb1 = sml("xcb1", 128)
    xcb_t2, bxcb2 = sml("xcb2", 128)
    xcb_t3, bxcb3 = sml("xcb3", 128)
    logits_sb, blg = sml("logits", G * 24)
    Z_sb, bZ = sml("Z", G * NST)
    ZT_sb, bZT = sml("ZT", G * NST)
    u_sb, bu = sml("u", G * 4)
    v_sb, bv = sml("v", G * 4)
    mx_sb, bmx = sml("mx", G * 4)
    t4_sb, bt4 = sml("t4", 2 * G)
    Hres_sb, bHr = sml("Hres", G * NST)
    Hpre_sb, bHp = sml("Hpre", G * 4)
    Hpost_sb, bHq = sml("Hpost", G * 4)
    snk_sb, bsnk = sml("snk", G * NST)
    ev0_sb, bev0 = sml("ev0", 288)
    ev1_sb, bev1 = sml("ev1", 288)
    wrs0, bwrs = sml("wrs", (KH * 24 + 1) // 2)
    wres_tiles = wrs0.bitcast(BF16)[:, 0:KH * 24].rearrange("p (c e) -> p c e", e=24)
    xpw0, bxpw = sml("xpw", (NT_D * E + 1) // 2)
    xpwT_sb = xpw0.bitcast(BF16)[:, 0:NT_D * E].rearrange("p (c e) -> p c e", e=E)
    gt_sb, bgt = sml("gt", 4)
    junk_sb, bjunk = sml("junk", 260)

    pb = [ctx.enter_context(nc.psum_tensor(f"pb{i}", [128, 512], F32)) for i in range(8)]
    bpb = [bB(("pb", i)) for i in range(8)]
    beyef = bB("eyef")
    bdtpw = bB("dtpw_sb")

    def TE(fn, **kw):
        return P.op("te", fn, **kw)

    def VE(fn, **kw):
        return P.op("ve", fn, **kw)

    def SC(fn, **kw):
        return P.op("sc", fn, **kw)

    def GP(fn, **kw):
        return P.op("gp", fn, **kw)

    def DMA(fn, **kw):
        return P.op("sy", fn, kind="dma", **kw)

    def SDMA(fn, **kw):
        return P.op("sc", fn, kind="dma", **kw)

    def GDMA(fn, **kw):
        return P.op("gp", fn, kind="dma", **kw)

    def CC(fn, **kw):
        return P.op("gp", fn, kind="cc", **kw)

    ev_win = []
    for i in range(NCP):
        ev_win.append(GDMA(lambda g, i=i: g.dma_start(out=win_bf.ap()[i], in_=win_t.ap()[i]),
                           writes=[bB(("win_bf", i))]))
    nwo = 4
    ev_wout_c = []
    for i in range(nwo):
        r0, r1 = i * DSL // nwo, (i + 1) * DSL // nwo
        ev_wout_c.append(GDMA(lambda g, r0=r0, r1=r1: g.dma_start(
            out=wout_bf.ap()[r0:r1, :], in_=wout_sl.ap()[r0:r1, :]), writes=[bB("wout_bf")]))
    nwm = 8
    ev_wmod_c = []
    for i in range(nwm):
        r0, r1 = i * H // nwm, (i + 1) * H // nwm
        ev_wmod_c.append(GDMA(lambda g, r0=r0, r1=r1: g.dma_start(
            out=wmod_bf.ap()[r0:r1, :], in_=wmod_sl.ap()[r0:r1, :]), writes=[bB("wmod_bf")]))
    ev_wb = GDMA(lambda g: g.dma_start(out=wb_bf.ap(), in_=wb_in.ap()), writes=[bB("wb_bf")])
    ev_xpwc = GDMA(lambda g: g.dma_start(out=xpwT_bf.ap(), in_=xpwT_sl.ap()), writes=[bB("xpw_bf")])
    ev_dtpwc = GDMA(lambda g: g.dma_start(out=dtpwT_bf.ap(), in_=dtpwT_sl.ap()),
                    writes=[bB("dtpw_bf")])
    ev_wresc = GDMA(lambda g: g.dma_start(out=wres_bf.ap(), in_=wres_t.ap()), writes=[bB("wres_bf")])

    def ld(dst, src, writes, after=(), slow=False):
        return DMA(lambda s, d=dst, i=src, slow=slow: s.dma_start(
            out=d, in_=i, allow_slow_non_contiguous=slow), writes=writes, after=after)

    ev_A0 = ld(A_sb.rearrange("p (c n) -> p c n", n=NST),
               A_sl.ap().rearrange("(c p) n -> p c n", p=128), bA_sb)
    ev_A1 = SC(lambda s: s.activation(A_sb, A_sb, AF.Exp), reads=bA_sb, writes=bA_sb)
    ev_A = VE(lambda v: v.tensor_scalar_mul(A_sb, A_sb, -1.0), reads=bA_sb, writes=bA_sb)
    ev_D = ld(D_sb, D_sl.ap().rearrange("(c p) -> p c", p=128), bD_sb, slow=True)
    ev_dtb = ld(dtb_sb, dtb_sl.ap().rearrange("(c p) -> p c", p=128), bdtb, slow=True)
    ev_cvb = ld(cvb_sb, convb_sl.ap().rearrange("(c p) -> p c", p=128), bcvb, slow=True)
    ev_cvw = ld(cvw_sb0.rearrange("p (c k) -> p c k", k=KCV),
                convw_sl.ap().rearrange("(c p) k -> p c k", p=128), bcvw)
    ev_dtln = ld(dtln_bc, dt_ln.ap().unsqueeze(0).broadcast_to([128, R]), bdtln)
    ev_Bln = ld(Bln_bc, B_ln.ap().unsqueeze(0).broadcast_to([128, NST]), bBln)
    ev_Cln = ld(Cln_bc, C_ln.ap().unsqueeze(0).broadcast_to([128, NST]), bCln)
    ev_base = ld(base_bc, base_cat.ap().unsqueeze(0).broadcast_to([128, 24]), bbase)
    ev_eye4 = ld(eye4_bc, eye4_in.ap().unsqueeze(0).broadcast_to([128, NST]), beye4)
    ev_eyef = ld(eyef_t[:, :], eye_f.ap(), [beyef])
    ev_dtpws = ld(dtpwT_t[:, :].rearrange("p (r d) -> p r d", d=DSL),
                  dtpwT_bf.ap().rearrange("r p d -> p r d"), [bdtpw],
                  after=[ev_dtpwc])
    ev_xpws = ld(xpw0.bitcast(BF16)[:, 0:NT_D * E].rearrange("p (cc e) -> p cc e", e=E),
                 xpwT_bf.ap().rearrange("(cc p) e -> p cc e", p=128), bxpw, after=[ev_xpwc])
    ev_wres_ld = ld(wrs0.bitcast(BF16)[:, 0:KH * 24].rearrange("p (cc e) -> p cc e", e=24),
                    wres_bf.ap().rearrange("(cc p) e -> p cc e", p=128), bwrs,
                    after=[ev_wresc])

    ev_g0 = ld(gt_sb[:, 0:1][0:1, :], gate_in.ap(), bgt)
    ev_g1 = SC(lambda s: s.activation(gt_sb[0:1, 1:2], gt_sb[0:1, 0:1], AF.Exp, scale=-1.0),
               reads=bgt, writes=bgt)
    ev_g2 = VE(lambda v: v.tensor_scalar_add(gt_sb[0:1, 2:3], gt_sb[0:1, 1:2], 1.0),
               reads=bgt, writes=bgt)
    ev_g3 = VE(lambda v: v.reciprocal(gt_sb[0:1, 3:4], gt_sb[0:1, 2:3]), reads=bgt, writes=bgt)
    ev_g4 = DMA(lambda s: s.dma_start(out=g_dr.ap(), in_=gt_sb[0:1, 3:4]),
                reads=bgt, writes=[bB("g_dr")])
    ev_g5 = DMA(lambda s: s.dma_start(out=g_bc, in_=g_dr.ap().broadcast_to([128, 1])),
                reads=[bB("g_dr")], writes=bg)
    ev_g6 = VE(lambda v: v.tensor_scalar(gm1_bc, g_bc, -1.0, 1.0, ALU.mult, ALU.add),
               reads=bg, writes=bgm1)

    ev_ct0 = ld(ct_f.rearrange("p (cc b) -> p cc b", b=B),
                cT_in.ap().rearrange("(cc p) b -> p cc b", p=128), bctf)
    ev_ct1 = SC(lambda s: s.activation(ct_f, ct_f, AF.Silu), reads=bctf, writes=bctf)
    ev_ct2 = VE(lambda v: v.tensor_copy(ct_bf, ct_f), reads=bctf, writes=bctb)
    ct_tiles = ct_bf.rearrange("p (cc b) -> p cc b", b=B)

    stg_ap, stg_b = phA.f(0, MODC)
    mod_stage = stg_ap[0:B, :]
    n_mch = (MODC + 255) // 256
    for i in range(n_mch):
        w0 = i * 256
        wid = min(256, MODC - w0)
        rhs_ap, rhs_b = phB.bf(0, KH * wid)
        rhs3 = rhs_ap.rearrange("p (cc w) -> p cc w", w=wid)
        ev_l = DMA(lambda s, w0=w0, wid=wid, r=rhs3: s.dma_start(
            out=r, in_=wmod_bf.ap()[:, w0:w0 + wid].rearrange("(cc p) w -> p cc w", p=128)),
            reads=[bB("wmod_bf")], writes=rhs_b, after=ev_wmod_c)

        def mmod(te, rhs3=rhs3, wid=wid):
            for k in range(KH):
                inst = te.matmul(pb[0][0:B, 0:wid], ct_tiles[:, k, :], rhs3[:, k, :],
                                 start=(k == 0), stop=(k == KH - 1))
            return inst
        ev_m = TE(mmod, reads=rhs_b + bctb, writes=[bpb[0]], after=[ev_ct2, ev_l])
        VE(lambda v, w0=w0, wid=wid: v.tensor_copy(mod_stage[:, w0:w0 + wid], pb[0][0:B, 0:wid]),
           reads=[bpb[0]], writes=stg_b, after=[ev_m])
    for half in range(2):
        for bb in range(2):
            b_src = half * 2 + bb
            for rr in range(NC // 2):
                DMA(lambda s, half=half, bb=bb, b_src=b_src, rr=rr: s.dma_start(
                    out=mod_part.ap()[bb * (NC // 2) + rr, half, :].unsqueeze(0),
                    in_=mod_stage[b_src:b_src + 1, :]),
                    reads=stg_b, writes=[bB("mod_part")])
    ev_a2a_mod = CC(lambda g: g.collective_compute(
        "AllToAll", ALU.bypass, replica_groups=[list(range(NC))],
        ins=[mod_part.ap().opt()], outs=[mod_my.ap().opt()]),
        reads=[bB("mod_part")], writes=[bB("mod_my")])
    def emit_ag_wout():
        return CC(lambda g: g.collective_compute(
            "AllGather", ALU.bypass, replica_groups=[list(range(NC))],
            ins=[wout_bf.ap().opt()], outs=[wout_all.ap().opt()]),
            reads=[bB("wout_bf")], writes=[bB("wout_all")], after=ev_wout_c)

    mm_flat = mod_my.ap().rearrange("j r m -> (j r m)")

    def bcast_mod(tile, lo_col, ph, vec, lo, hi):
        dst_ap, dst_b = tile.f(lo_col, lo_col + (hi - lo))
        evs = None
        coff = 0
        pos = vec * H + lo
        end = vec * H + hi
        while pos < end:
            j = pos // MODC
            ib = pos % MODC
            take = min(MODC - ib, end - pos)
            src = mm_flat[j * (2 * MODC) + ph * MODC + ib:
                          j * (2 * MODC) + ph * MODC + ib + take]
            evs = DMA(lambda s, d=dst_ap[:, coff:coff + take],
                      i=src.unsqueeze(0).broadcast_to([128, take]): s.dma_start(out=d, in_=i),
                      reads=[bB("mod_my")], writes=dst_b)
            coff += take
            pos += take
        return evs

    for ph in range(2):
        ev_ca = bcast_mod(phC, 0, ph, 1, 0, H)
        ca_ap, ca_b = phC.f(0, H)
        VE(lambda v: v.tensor_scalar_add(ca_ap, ca_ap, 1.0), reads=ca_b, writes=ca_b)
        ev_sa = bcast_mod(phD, 0, ph, 0, 0, H)
        sa_ap, sa_b = phD.f(0, H)
        for tt in range(NT_Q):
            xt_ap, xt_b = phA.f(0, H)
            DMA(lambda s, ph=ph, tt=tt, d=xt_ap: s.dma_start(
                out=d, in_=x_my.ap()[ph, tt * 128:(tt + 1) * 128, :]), writes=xt_b)
            VE(lambda v, i=xt_ap: v.reduce_sum(s1_t, i, AX.X), reads=xt_b, writes=bs1)
            dum_ap, dum_b = phE.f(0, H)
            SC(lambda s, i=xt_ap, d=dum_ap: s.activation(d, i, AF.Square, accum_out=s2_t),
               reads=xt_b, writes=bs2 + dum_b)
            VE(lambda v: v.tensor_scalar_mul(mu_t, s1_t, 1.0 / H), reads=bs1, writes=bmu)
            VE(lambda v: v.tensor_tensor(var_t, mu_t, mu_t, ALU.mult), reads=bmu, writes=bvar)
            VE(lambda v: v.tensor_scalar(junk_sb[:, 0:1], s2_t, 1.0 / H, None, ALU.mult),
               reads=bs2, writes=bjunk)
            VE(lambda v: v.tensor_sub(var_t, junk_sb[:, 0:1], var_t),
               reads=bjunk + bvar, writes=bvar)
            VE(lambda v: v.tensor_scalar_add(var_t, var_t, c["EPS_LN"]),
               reads=bvar, writes=bvar)
            SC(lambda s: s.activation(rstd_t, var_t, AF.Ln),
               reads=bvar, writes=brstd)
            SC(lambda s: s.activation(rstd_t, rstd_t, AF.Exp, scale=-0.5),
               reads=brstd, writes=brstd)
            hn1_ap, hn1_b = phB.f(0, H)
            VE(lambda v, i=xt_ap, o=hn1_ap: v.tensor_scalar(o, i, mu_t, rstd_t,
                                                            ALU.subtract, ALU.mult),
               reads=xt_b + bmu + brstd, writes=hn1_b)
            t1_ap, t1_b = phE.f(0, H)
            VE(lambda v, i=hn1_ap, o=t1_ap, cav=ca_ap: v.tensor_tensor(o, i, cav, ALU.mult),
               reads=hn1_b + ca_b, writes=t1_b)
            h_ap, h_b = phB2.f(0, H)
            VE(lambda v, i=t1_ap, o=h_ap, sav=sa_ap: v.tensor_tensor(o, i, sav, ALU.add),
               reads=t1_b + sa_b, writes=h_b, after=[ev_sa])
            hstg_ap, hstg_b = phG.bf(0, H)
            hstg3 = hstg_ap.rearrange("p (cc q) -> p cc q", q=128)
            for kc in range(KH):
                ev_tp = TE(lambda te, kc=kc, i=h_ap: te.transpose(
                    pb[7][:, 0:128], i[:, kc * 128:(kc + 1) * 128], eyef_t[:, :]),
                    reads=h_b + [beyef], writes=[bpb[7]], after=[ev_eyef])
                SC(lambda s, kc=kc, o=hstg3: s.copy(o[:, kc, :], pb[7][:, 0:128]),
                   reads=[bpb[7]], writes=hstg_b, after=[ev_tp])
            DMA(lambda s, ph=ph, tt=tt: s.dma_start(
                out=hT_my[ph].ap()[:, tt * 128:(tt + 1) * 128]
                .rearrange("(cc p) q -> p cc q", p=128), in_=hstg3),
                reads=hstg_b, writes=[bB(("hT_my", ph))])
    ev_ag_hT = [CC(lambda g, p=p: g.collective_compute(
        "AllGather", ALU.bypass, replica_groups=[list(range(NC))],
        ins=[hT_my[p].ap().opt()], outs=[hT_all[p].ap().opt()]),
        reads=[bB(("hT_my", p))], writes=[bB(("hT_all", p))]) for p in range(2)]

    def h_block(b, t0):
        return 4 * (b % 2) + (t0 // QT), t0 % QT

    def emit_P2(b):
        for tc in range(NTC):
            t0 = tc * TC
            rr, cc0 = h_block(b, t0)
            rhs_ap, rhs_b = phA.bf(0, KH * TC)
            rhs3 = rhs_ap.rearrange("p (cc w) -> p cc w", w=TC)
            ev_rhs = DMA(lambda s, rr=rr, cc0=cc0, ph=b // 2, r=rhs3: s.dma_start(
                out=r, in_=hT_all[ph].ap()[rr, :, cc0:cc0 + TC]
                .rearrange("(cc p) w -> p cc w", p=128)),
                reads=[bB(("hT_all", b // 2))], writes=rhs_b, after=[ev_ag_hT[b // 2]])
            for ctp in range(NCP):
                wtile = phB if ctp % 2 == 0 else phB2
                wt_ap, wt_b = wtile.bf(0, KH * 256)
                wt3 = wt_ap.rearrange("p (cc w) -> p cc w", w=256)
                ev_wt = DMA(lambda s, ctp=ctp, w=wt3: s.dma_start(
                    out=w, in_=win_bf.ap()[ctp].rearrange("cc p w -> p cc w")),
                    reads=[bB(("win_bf", ctp))], writes=wt_b, after=[ev_win[ctp]])
                for hh in range(2):
                    ct = ctp * 2 + hh
                    bank = ct % 6

                    def mmip(te, wt3=wt3, hh=hh, bank=bank, rhs3=rhs3):
                        for k in range(KH):
                            inst = te.matmul(pb[bank][:, 0:TC], wt3[:, k, hh * 128:(hh + 1) * 128],
                                             rhs3[:, k, :], start=(k == 0), stop=(k == KH - 1))
                        return inst
                    ev_mm = TE(mmip, reads=wt_b + rhs_b, writes=[bpb[bank]],
                               after=[ev_wt, ev_rhs])
                    stg = (ev0_sb if ct % 2 == 0 else ev1_sb)[:, 0:TC]
                    stg_bu = bev0 if ct % 2 == 0 else bev1
                    ev_ev = VE(lambda v, bank=bank, stg=stg: v.tensor_copy(stg, pb[bank][:, 0:TC]),
                               reads=[bpb[bank]], writes=stg_bu, after=[ev_mm])
                    dst = x_sp if ct < NT_D else z_sp
                    dtile = ct if ct < NT_D else ct - NT_D
                    GDMA(lambda s, dst=dst, b=b, dtile=dtile, t0=t0, stg=stg: s.dma_start(
                        out=dst.ap()[b, dtile * 128:(dtile + 1) * 128, t0:t0 + TC], in_=stg),
                        reads=stg_bu, writes=[bB((dst.name, b, dtile, tc))], after=[ev_ev])

        for dtile in range(NT_D):
            for th in range(2):
                t0 = th * THL
                xp_ap, xp_b = phG.f(0, THL + KCV - 1)
                deps = [bB(("x_sp", b, dtile, tci))
                        for tci in range(max(0, (t0 - KCV + 1)) // TC, (t0 + THL) // TC)]
                if th == 0:
                    ev_z = VE(lambda v: v.memset(xp_ap[:, 0:KCV - 1], 0.0), writes=xp_b)
                    DMA(lambda s, b=b, dtile=dtile: s.dma_start(
                        out=xp_ap[:, KCV - 1:], in_=x_sp.ap()[b, dtile * 128:(dtile + 1) * 128, 0:THL]),
                        reads=deps, writes=xp_b)
                else:
                    DMA(lambda s, b=b, dtile=dtile, t0=t0: s.dma_start(
                        out=xp_ap, in_=x_sp.ap()[b, dtile * 128:(dtile + 1) * 128,
                                                 t0 - (KCV - 1):t0 + THL]),
                        reads=deps, writes=xp_b)
                cv_ap, cv_b = phG.f(1536, 1536 + THL)
                VE(lambda v, dtile=dtile: v.tensor_scalar(
                    cv_ap, xp_ap[:, 0:THL], cvw_sb[:, dtile, 0:1], cvb_sb[:, dtile:dtile + 1],
                    ALU.mult, ALU.add),
                    reads=xp_b + bcvw + bcvb, writes=cv_b, after=[ev_cvw, ev_cvb])
                for k in range(1, KCV):
                    VE(lambda v, dtile=dtile, k=k: v.scalar_tensor_tensor(
                        cv_ap, xp_ap[:, k:k + THL], cvw_sb[:, dtile, k:k + 1], cv_ap,
                        ALU.mult, ALU.add),
                        reads=xp_b + bcvw + cv_b, writes=cv_b)
                xcf_ap, xcf_b = phG.f(2560, 2560 + THL)
                SC(lambda s: s.activation(xcf_ap, cv_ap, AF.Silu), reads=cv_b, writes=xcf_b)
                xcb_ap, xcb_b = phG.bf(3584, THL)
                VE(lambda v: v.tensor_copy(xcb_ap, xcf_ap), reads=xcf_b, writes=xcb_b)
                GDMA(lambda s, b=b, dtile=dtile, t0=t0: s.dma_start(
                    out=xc_sp.ap()[b, dtile * 128:(dtile + 1) * 128, t0:t0 + THL], in_=xcf_ap),
                    reads=xcf_b, writes=[bB(("xc_sp", b, dtile, th))])
                GDMA(lambda s, b=b, dtile=dtile, t0=t0: s.dma_start(
                    out=xcbf_sp.ap()[b, dtile * 128:(dtile + 1) * 128, t0:t0 + THL], in_=xcb_ap),
                    reads=xcb_b, writes=[bB(("xcbf_sp", b, dtile, th))])

        for tp in range(T // 256):
            ev_last = []
            for dtile in range(NT_D):
                xt = (xcb_t0 if dtile % 2 == 0 else xcb_t1).bitcast(BF16)[:, 0:256]
                xt_bu = bxcb0 if dtile % 2 == 0 else bxcb1
                ev_lx = DMA(lambda s, b=b, dtile=dtile, tp=tp, xt=xt: s.dma_start(
                    out=xt, in_=xcbf_sp.ap()[b, dtile * 128:(dtile + 1) * 128,
                                             tp * 256:(tp + 1) * 256]),
                    reads=[bB(("xcbf_sp", b, dtile, (tp * 256) // THL))], writes=xt_bu)

                def mmxp(te, xt=xt, dtile=dtile):
                    for tt in range(2):
                        inst = te.matmul(pb[4 + tt][:, 0:E], xt[:, tt * 128:(tt + 1) * 128],
                                         xpwT_sb[:, dtile, :],
                                         start=(dtile == 0), stop=(dtile == NT_D - 1))
                    return inst
                ev_mm = TE(mmxp, reads=xt_bu + bxpw, writes=[bpb[4], bpb[5]],
                           after=[ev_lx, ev_xpws])
                ev_last = [ev_mm]
            for tt in range(2):
                stg = (ev0_sb if tt == 0 else ev1_sb)[:, 0:E]
                stg_bu = bev0 if tt == 0 else bev1
                VE(lambda v, tt=tt, stg=stg: v.tensor_copy(stg, pb[4 + tt][:, 0:E]),
                   reads=[bpb[4 + tt]], writes=stg_bu, after=ev_last)
                GDMA(lambda s, b=b, tp=tp, tt=tt, stg=stg: s.dma_start(
                    out=ssm_in[b].ap()[tp * 256 + tt * 128:tp * 256 + (tt + 1) * 128, :], in_=stg),
                    reads=stg_bu, writes=[bB(("ssm_in", b))])
        ev_ar = CC(lambda g, b=b: g.collective_compute(
            "AllReduce", ALU.add, replica_groups=[list(range(NC))],
            ins=[ssm_in[b].ap().opt()], outs=[ssm_out[b].ap().opt()]),
            reads=[bB(("ssm_in", b))], writes=[bB(("ssm_out", b))])

        dtrT_ap, dtrT_b = phJ.bf(0, RT * T)
        dtrT = dtrT_ap.rearrange("p (r t) -> p r t", t=T)
        for tt in range(T // 128):
            st_ap, st_b = phH.f((tt % 2) * 512, (tt % 2) * 512 + E)
            DMA(lambda s, b=b, tt=tt, st=st_ap: s.dma_start(
                out=st, in_=ssm_out[b].ap()[tt * 128:(tt + 1) * 128, :]),
                reads=[bB(("ssm_out", b))], writes=st_b, after=[ev_ar])
            for (off, w, ri) in ((0, R, 0), (R, NST, 1), (R + NST, NST, 2)):
                SC(lambda s, st=st_ap, off=off, w=w: s.activation(
                    junk_sb[:, 0:w], st[:, off:off + w], AF.Square, accum_out=s2_t),
                    reads=st_b, writes=bs2 + bjunk)
                VE(lambda v, w=w, ri=ri: v.tensor_scalar(rho_t[:, ri:ri + 1], s2_t,
                                                          1.0 / w, c["EPS_RMS_A"],
                                                          ALU.mult, ALU.add),
                   reads=bs2, writes=brho)
                SC(lambda s, ri=ri: s.activation(rho_t[:, ri:ri + 1], rho_t[:, ri:ri + 1], AF.Ln),
                   reads=brho, writes=brho)
                SC(lambda s, ri=ri: s.activation(rho_t[:, ri:ri + 1], rho_t[:, ri:ri + 1],
                                                 AF.Exp, scale=-0.5),
                   reads=brho, writes=brho)
            dtr_ap, dtr_b = phH.f(1024, 1024 + R)
            VE(lambda v, st=st_ap: v.tensor_scalar(dtr_ap, st[:, 0:R], rho_t[:, 0:1], None, ALU.mult),
               reads=st_b + brho, writes=dtr_b)
            VE(lambda v: v.tensor_tensor(dtr_ap, dtr_ap, dtln_bc, ALU.mult),
               reads=dtr_b + bdtln, writes=dtr_b, after=[ev_dtln])
            bc_ap, bc_b = phH.f(1536, 1536 + 2 * NST)
            brow, crow = bc_ap[:, 0:NST], bc_ap[:, NST:2 * NST]
            VE(lambda v, st=st_ap: v.tensor_scalar(brow, st[:, R:R + NST], rho_t[:, 1:2],
                                                   None, ALU.mult),
               reads=st_b + brho, writes=bc_b)
            VE(lambda v: v.tensor_tensor(brow, brow, Bln_bc, ALU.mult),
               reads=bc_b + bBln, writes=bc_b, after=[ev_Bln])
            VE(lambda v, st=st_ap: v.tensor_scalar(crow, st[:, R + NST:R + 2 * NST], rho_t[:, 2:3],
                                                   None, ALU.mult),
               reads=st_b + brho, writes=bc_b)
            VE(lambda v: v.tensor_tensor(crow, crow, Cln_bc, ALU.mult),
               reads=bc_b + bCln, writes=bc_b, after=[ev_Cln])
            for rc in range(RT):
                w = min(128, R - rc * 128)
                ev_tp = TE(lambda te, rc=rc, w=w: te.transpose(
                    pb[7][0:w, 0:128], dtr_ap[:, rc * 128:rc * 128 + w], eyef_t[:, :]),
                    reads=dtr_b + [beyef], writes=[bpb[7]], after=[ev_eyef])
                SC(lambda s, rc=rc, tt=tt, w=w: s.copy(dtrT[0:w, rc, tt * 128:(tt + 1) * 128],
                                                       pb[7][0:w, 0:128]),
                   reads=[bpb[7]], writes=dtrT_b, after=[ev_tp])
            for (row, dst) in ((brow, Bt_i), (crow, Ct_i)):
                ev_tp = TE(lambda te, row=row: te.transpose(pb[7][0:NST, 0:128], row, eyef_t[:, :]),
                           reads=bc_b + [beyef], writes=[bpb[7]], after=[ev_eyef])
                VE(lambda v: v.tensor_copy(btct[0:NST, 0:128], pb[7][0:NST, 0:128]),
                   reads=[bpb[7]], writes=bbtct, after=[ev_tp])
                GDMA(lambda s, dst=dst, b=b, tt=tt: s.dma_start(
                    out=dst.ap()[b, :, tt * 128:(tt + 1) * 128], in_=btct[0:NST, 0:128]),
                    reads=bbtct, writes=[bB((dst.name, b))])

        for dtile in range(NT_D):
            for th in range(2):
                dtp_ap, dtp_b = phB.f(0, THL)
                for sub in range(THL // TC):
                    tcg = th * (THL // TC) + sub

                    def mmdt(te, dtile=dtile, tcg=tcg):
                        for rc in range(RT):
                            w = min(128, R - rc * 128)
                            inst = te.matmul(
                                pb[6][:, 0:TC],
                                dtpwT_t[0:w, rc * DSL + dtile * 128:rc * DSL + (dtile + 1) * 128],
                                dtrT[0:w, rc, tcg * TC:(tcg + 1) * TC],
                                start=(rc == 0), stop=(rc == RT - 1))
                        return inst
                    ev_mm = TE(mmdt, reads=dtrT_b + [bdtpw], writes=[bpb[6]], after=[ev_dtpws])
                    VE(lambda v, sub=sub, dtile=dtile: v.tensor_scalar(
                        dtp_ap[:, sub * TC:(sub + 1) * TC], pb[6][:, 0:TC],
                        dtb_sb[:, dtile:dtile + 1], None, ALU.add),
                        reads=[bpb[6]] + bdtb, writes=dtp_b, after=[ev_mm, ev_dtb])
                sp0_ap, sp0_b = phB.f(1024, 1024 + THL)
                sp1_ap, sp1_b = phB.f(2048, 2048 + THL)
                VE(lambda v: v.tensor_scalar_max(sp0_ap, dtp_ap, 0.0), reads=dtp_b, writes=sp0_b)
                SC(lambda s: s.activation(sp1_ap, dtp_ap, AF.Abs), reads=dtp_b, writes=sp1_b)
                SC(lambda s: s.activation(sp1_ap, sp1_ap, AF.Exp, scale=-1.0),
                   reads=sp1_b, writes=sp1_b)
                VE(lambda v: v.tensor_scalar_add(sp1_ap, sp1_ap, 1.0),
                   reads=sp1_b, writes=sp1_b)
                SC(lambda s: s.activation(sp1_ap, sp1_ap, AF.Ln),
                   reads=sp1_b, writes=sp1_b)
                VE(lambda v: v.tensor_add(dtp_ap, sp0_ap, sp1_ap),
                   reads=sp0_b + sp1_b, writes=dtp_b)
                GDMA(lambda s, b=b, dtile=dtile, th=th: s.dma_start(
                    out=dt_sp.ap()[b, dtile * 128:(dtile + 1) * 128, th * THL:(th + 1) * THL],
                    in_=dtp_ap),
                    reads=dtp_b, writes=[bB(("dt_sp", b, dtile, th))])

    def emit_P3(b):
        for th in range(2):
            for blk in range(NBLK):
                dts = [blk * DTBLK + j for j in range(DTBLK)]
                for j, dtile in enumerate(dts):
                    dtt_ap, dtt_b = phC.f(j * THL, (j + 1) * THL)
                    SDMA(lambda s, b=b, dtile=dtile, th=th, d=dtt_ap: s.dma_start(
                        out=d, in_=dt_sp.ap()[b, dtile * 128:(dtile + 1) * 128,
                                              th * THL:(th + 1) * THL]),
                        reads=[bB(("dt_sp", b, dtile, th))], writes=dtt_b)
                    xct_ap, xct_b = phF.f(0, THL)
                    SDMA(lambda s, b=b, dtile=dtile, th=th, d=xct_ap: s.dma_start(
                        out=d, in_=xc_sp.ap()[b, dtile * 128:(dtile + 1) * 128,
                                              th * THL:(th + 1) * THL]),
                        reads=[bB(("xc_sp", b, dtile, th))], writes=xct_b)
                    w_ap, w_b = phD.f(j * THL, (j + 1) * THL)
                    VE(lambda v, d=dtt_ap, x=xct_ap, w=w_ap: v.tensor_tensor(w, d, x, ALU.mult),
                       reads=dtt_b + xct_b, writes=w_b)
                pend_acc = []

                def flush_acc():
                    while pend_acc:
                        fnp, rd, wr = pend_acc.pop(0)
                        VE(fnp, reads=rd, writes=wr)
                pipes = (phF, phK, phM)
                itc = [0]
                for n in range(NST):
                    bct = phI if n % 2 == 0 else phL
                    bcB_ap, bcB_b = bct.f(0, THL)
                    bcC_ap, bcC_b = bct.f(1024, 1024 + THL)
                    SDMA(lambda s, b=b, n=n, th=th, d=bcB_ap: s.dma_start(
                        out=d, in_=Bt_i.ap()[b, n, th * THL:(th + 1) * THL]
                        .unsqueeze(0).broadcast_to([128, THL])),
                        reads=[bB(("Bt_i", b))], writes=bcB_b)
                    SDMA(lambda s, b=b, n=n, th=th, d=bcC_ap: s.dma_start(
                        out=d, in_=Ct_i.ap()[b, n, th * THL:(th + 1) * THL]
                        .unsqueeze(0).broadcast_to([128, THL])),
                        reads=[bB(("Ct_i", b))], writes=bcC_b)
                    for j, dtile in enumerate(dts):
                        dtt_ap, dtt_b = phC.f(j * THL, (j + 1) * THL)
                        w_ap, w_b = phD.f(j * THL, (j + 1) * THL)
                        pipet = pipes[itc[0] % 3]
                        itc[0] += 1
                        pA_ap, pA_b = pipet.f(0, THL)
                        pB_ap, pB_b = pipet.f(1024, 1024 + THL)
                        pP_ap, pP_b = pipet.f(2048, 2048 + THL)
                        SC(lambda s, d=dtt_ap, dtile=dtile, n=n, o=pA_ap: s.activation(
                            o, d, AF.Exp, scale=A_sb[:, dtile * NST + n:dtile * NST + n + 1]),
                            reads=dtt_b + bA_sb, writes=pA_b, after=[ev_A])
                        VE(lambda v, w=w_ap, o=pB_ap, bc=bcB_ap: v.tensor_tensor(o, w, bc, ALU.mult),
                           reads=w_b + bcB_b, writes=pB_b)
                        if th == 0:
                            ini = 0.0
                            rd_ini = []
                        else:
                            ini = sstate[:, dtile * NST + n:dtile * NST + n + 1]
                            rd_ini = bsst
                        VE(lambda v, a=pA_ap, x=pB_ap, ini=ini: v.tensor_tensor_scan(
                            x, a, x, ini, ALU.mult, ALU.add),
                            reads=pA_b + pB_b + rd_ini, writes=pB_b)
                        if th == 0:
                            VE(lambda v, dtile=dtile, n=n, x=pB_ap: v.tensor_copy(
                                sstate[:, dtile * NST + n:dtile * NST + n + 1],
                                x[:, THL - 1:THL]),
                                reads=pB_b, writes=bsst)
                        GP(lambda g, x=pB_ap, o=pP_ap, bc=bcC_ap: g.tensor_tensor(o, x, bc, ALU.mult),
                           reads=pB_b + bcC_b, writes=pP_b)
                        ya_ap, ya_b = phE.f(j * THL, (j + 1) * THL)
                        if n == 0:
                            fnp = (lambda v, ya=ya_ap, o=pP_ap: v.tensor_copy(ya, o))
                        else:
                            fnp = (lambda v, ya=ya_ap, o=pP_ap: v.tensor_add(ya, ya, o))
                        pend_acc.append((fnp, pP_b + (ya_b if n > 0 else []), ya_b))
                        if len(pend_acc) > 1:
                            fnp, rd, wr = pend_acc.pop(0)
                            VE(fnp, reads=rd, writes=wr)
                flush_acc()
                for j, dtile in enumerate(dts):
                    ya_ap, ya_b = phE.f(j * THL, (j + 1) * THL)
                    xct_ap, xct_b = phF.f(0, THL)
                    zt_ap, zt_b = phF.f(2048, 2048 + THL)
                    SDMA(lambda s, b=b, dtile=dtile, th=th, d=xct_ap: s.dma_start(
                        out=d, in_=xc_sp.ap()[b, dtile * 128:(dtile + 1) * 128,
                                              th * THL:(th + 1) * THL]),
                        reads=[bB(("xc_sp", b, dtile, th))], writes=xct_b)
                    SDMA(lambda s, b=b, dtile=dtile, th=th, d=zt_ap: s.dma_start(
                        out=d, in_=z_sp.ap()[b, dtile * 128:(dtile + 1) * 128,
                                             th * THL:(th + 1) * THL]),
                        reads=[bB(("z_sp", b, dtile, tci))
                               for tci in range((th * THL) // TC, ((th + 1) * THL) // TC)],
                        writes=zt_b)
                    tf_ap, tf_b = phF.f(1024, 1024 + THL)
                    VE(lambda v, x=xct_ap, dtile=dtile, ya=ya_ap, o=tf_ap: v.scalar_tensor_tensor(
                        o, x, D_sb[:, dtile:dtile + 1], ya, ALU.mult, ALU.add),
                        reads=xct_b + ya_b + bD_sb, writes=tf_b, after=[ev_D])
                    SC(lambda s, z=zt_ap: s.activation(z, z, AF.Silu), reads=zt_b, writes=zt_b)
                    yb_ap, yb_b = phF.bf(0, THL)
                    VE(lambda v, t=tf_ap, z=zt_ap, o=yb_ap: v.tensor_tensor(o, t, z, ALU.mult),
                       reads=tf_b + zt_b, writes=yb_b)
                    pset = 0 if b < 2 else 1
                    for qq in range(NQB):
                        qglob = th * NQB + qq
                        shard = 4 * (b % 2) + qglob
                        DMA(lambda s, pset=pset, shard=shard, dtile=dtile, qq=qq, y=yb_ap:
                            s.dma_start(out=y_in[pset].ap()[shard, dtile * 128:(dtile + 1) * 128, :],
                                        in_=y[:, qq * QT:(qq + 1) * QT]),
                            reads=yb_b, writes=[bB(("y_in", pset))])
    def emit_A2A(pset):
        CC(lambda g, pset=pset: g.collective_compute(
            "AllToAll", ALU.bypass, replica_groups=[list(range(NC))],
            ins=[y_in[pset].ap().opt()], outs=[y_my[pset].ap().opt()]),
            reads=[bB(("y_in", pset))], writes=[bB(("y_my", pset))])


    emit_P2(0)
    emit_P2(1)
    emit_P3(0)
    ev_ag_wout = emit_ag_wout()
    emit_P2(2)
    emit_P3(1)
    emit_A2A(0)
    emit_P2(3)
    emit_P3(2)
    emit_P3(3)
    emit_A2A(1)

    WOW = min(2048, H)
    NHH = H // WOW
    NHC = WOW // 512

    def clamp_ga(tile, bufs, ev_after):
        evs = ev_after
        CW = min(1024, H)
        for hf in range(H // CW):
            o0 = hf * CW
            m_ap, m_b = phI.f(0, CW)
            hi_ap, hi_b = phI.f(1024, 1024 + CW)
            lo_ap, lo_b = phJ.f(0, CW)
            g_ap = tile.t[:, o0:o0 + CW]
            VE(lambda v, g=g_ap, m=m_ap: v.tensor_single_scalar(m, g, 0.0, ALU.is_ge),
               reads=bufs, writes=m_b, after=[evs])
            VE(lambda v, g=g_ap, hi=hi_ap: v.tensor_scalar_max(hi, g, 0.25),
               reads=bufs, writes=hi_b)
            VE(lambda v, g=g_ap, lo=lo_ap: v.tensor_scalar_min(lo, g, -0.25),
               reads=bufs, writes=lo_b)
            VE(lambda v, hi=hi_ap, lo=lo_ap: v.tensor_sub(hi, hi, lo),
               reads=hi_b + lo_b, writes=hi_b)
            VE(lambda v, hi=hi_ap, m=m_ap: v.tensor_mul(hi, hi, m),
               reads=hi_b + m_b, writes=hi_b)
            evs = VE(lambda v, g=g_ap, lo=lo_ap, hi=hi_ap: v.tensor_add(g, lo, hi),
                     reads=lo_b + hi_b, writes=bufs)
        return evs

    for ph in range(2):
        ev_ga0 = bcast_mod(phG, 0, ph, 2, 0, H)
        ga_ap, ga_b = phG.f(0, H)
        ev_ga = clamp_ga(phG, ga_b, ev_ga0)
        for tpair in range(NT_Q // 2):
            xin_t = [phC, phD]
            for tt in range(2):
                ttg = tpair * 2 + tt
                xi_ap, xi_b = xin_t[tt].f(0, H)
                SDMA(lambda s, ph=ph, ttg=ttg, d=xi_ap: s.dma_start(
                    out=d, in_=x_my.ap()[ph, ttg * 128:(ttg + 1) * 128, :]), writes=xi_b)
            for hh in range(NHH):
                ev_mmo = None
                yts = [xcb_t0, xcb_t1, xcb_t2, xcb_t3]
                ytb = [bxcb0, bxcb1, bxcb2, bxcb3]
                for k in range(KD):
                    wtile = phB if k % 2 == 0 else phB2
                    wlo = 0 if (k // 2) % 2 == 0 else 2048
                    wo_ap, wo_b = wtile.bf(wlo, WOW)
                    ev_wo = (SDMA if k % 2 == 0 else DMA)(lambda s, k=k, hh=hh, w=wo_ap: s.dma_start(
                        out=w, in_=wout_all.ap()[k // NT_D, (k % NT_D) * 128:(k % NT_D) * 128 + 128,
                                                 hh * WOW:(hh + 1) * WOW]),
                        reads=[bB("wout_all")], writes=wo_b, after=[ev_ag_wout])
                    yt = yts[k % 4].bitcast(BF16)[:, 0:256]
                    yt_bu = ytb[k % 4]
                    ev_yt = (DMA if k % 2 == 0 else SDMA)(lambda s, ph=ph, k=k, tpair=tpair, y=yt: s.dma_start(
                        out=y, in_=y_my[ph].ap()[k // NT_D, (k % NT_D) * 128:(k % NT_D) * 128 + 128,
                                                 tpair * 256:(tpair + 1) * 256]),
                        reads=[bB(("y_my", ph))], writes=yt_bu)

                    def mmop(te, yt=yt, wo=wo_ap, k=k):
                        for tt in range(2):
                            for hc in range(NHC):
                                inst = te.matmul(pb[tt * NHC + hc][:, 0:512],
                                                 yt[:, tt * 128:(tt + 1) * 128],
                                                 wo[:, hc * 512:(hc + 1) * 512],
                                                 start=(k == 0), stop=(k == KD - 1))
                        return inst
                    ev_mmo = TE(mmop, reads=wo_b + yt_bu,
                                writes=[bpb[tt * NHC + hc] for tt in range(2) for hc in range(NHC)],
                                after=[ev_wo, ev_yt])
                for tt in range(2):
                    x2t = phA if tt == 0 else phE
                    x2_ap, x2_b = x2t.f(0, H)
                    xi_ap, xi_b = xin_t[tt].f(0, H)
                    for hc in range(NHC):
                        co = hh * WOW + hc * 512
                        bank = tt * NHC + hc
                        VE(lambda v, bank=bank, co=co, o=x2_ap: v.tensor_tensor(
                            o[:, co:co + 512], pb[bank][:, 0:512], ga_ap[:, co:co + 512], ALU.mult),
                            reads=[bpb[bank]] + ga_b, writes=x2_b, after=[ev_ga, ev_mmo])
                        VE(lambda v, co=co, o=x2_ap, xi=xi_ap: v.tensor_add(
                            o[:, co:co + 512], o[:, co:co + 512], xi[:, co:co + 512]),
                            reads=xi_b + x2_b, writes=x2_b)
            for tt in range(2):
                ttg = tpair * 2 + tt
                x2t = phA if tt == 0 else phE
                x2_ap, x2_b = x2t.f(0, H)
                GDMA(lambda s, ph=ph, ttg=ttg, i=x2_ap: s.dma_start(
                    out=x2_sp.ap()[ph, ttg * 128:(ttg + 1) * 128, :], in_=i),
                    reads=x2_b, writes=[bB(("x2_sp", ph, ttg))])
                VE(lambda v, i=x2_ap: v.reduce_sum(s1_t, i, AX.X), reads=x2_b, writes=bs1)
                dum_ap, dum_b = phD.f(0, H)
                SC(lambda s, i=x2_ap, d=dum_ap: s.activation(d, i, AF.Square, accum_out=s2_t),
                   reads=x2_b, writes=bs2 + dum_b)
                VE(lambda v: v.tensor_scalar_mul(mu_t, s1_t, 1.0 / H), reads=bs1, writes=bmu)
                VE(lambda v: v.tensor_tensor(var_t, mu_t, mu_t, ALU.mult), reads=bmu, writes=bvar)
                VE(lambda v: v.tensor_scalar(junk_sb[:, 0:1], s2_t, 1.0 / H, None, ALU.mult),
                   reads=bs2, writes=bjunk)
                VE(lambda v: v.tensor_sub(var_t, junk_sb[:, 0:1], var_t),
                   reads=bjunk + bvar, writes=bvar)
                VE(lambda v: v.tensor_scalar_add(var_t, var_t, c["EPS_LN"]),
                   reads=bvar, writes=bvar)
                SC(lambda s: s.activation(rstd_t, var_t, AF.Ln),
                   reads=bvar, writes=brstd)
                SC(lambda s: s.activation(rstd_t, rstd_t, AF.Exp, scale=-0.5),
                   reads=brstd, writes=brstd)
                hn_ap, hn_b = phD.f(0, H)
                VE(lambda v, i=x2_ap, o=hn_ap: v.tensor_scalar(o, i, mu_t, rstd_t,
                                                               ALU.subtract, ALU.mult),
                   reads=x2_b + bmu + brstd, writes=hn_b)
                h2_ap, h2_b = phC.f(0, H)
                for hf in range(2):
                    o0 = hf * (H // 2)
                    cb_ap, cb_b = phI.f(0, H // 2)
                    ev_cb = bcast_mod(phI, 0, ph, 4, o0, o0 + H // 2)
                    VE(lambda v, a=cb_ap: v.tensor_scalar_add(a, a, 1.0),
                       reads=cb_b, writes=cb_b, after=[ev_cb])
                    sb_ap, sb_b = phJ.f(0, H // 2)
                    ev_sb = bcast_mod(phJ, 0, ph, 3, o0, o0 + H // 2)
                    VE(lambda v, o0=o0, h2=h2_ap, hn=hn_ap, a=cb_ap: v.tensor_mul(
                        h2[:, o0:o0 + H // 2], hn[:, o0:o0 + H // 2], a),
                        reads=hn_b + cb_b, writes=h2_b)
                    VE(lambda v, o0=o0, h2=h2_ap, a=sb_ap: v.tensor_add(
                        h2[:, o0:o0 + H // 2], h2[:, o0:o0 + H // 2], a),
                        reads=sb_b + h2_b, writes=h2_b, after=[ev_sb])
                GDMA(lambda s, ph=ph, ttg=ttg, i=h2_ap: s.dma_start(
                    out=h2_sp.ap()[ph, ttg * 128:(ttg + 1) * 128, :], in_=i),
                    reads=h2_b, writes=[bB(("h2_sp", ph, ttg))])
                SC(lambda s, i=h2_ap, d=dum_ap: s.activation(d, i, AF.Square, accum_out=s2_t),
                   reads=h2_b, writes=bs2 + dum_b)
                VE(lambda v: v.tensor_scalar(rstd_t, s2_t, 1.0 / H, c["EPS_RMS_F"],
                                             ALU.mult, ALU.add),
                   reads=bs2, writes=brstd)
                SC(lambda s: s.activation(rstd_t, rstd_t, AF.Ln),
                   reads=brstd, writes=brstd)
                SC(lambda s: s.activation(rstd_t, rstd_t, AF.Exp, scale=-0.5),
                   reads=brstd, writes=brstd)
                xn_ap, xn_b = phD.f(0, H)
                VE(lambda v, i=h2_ap, o=xn_ap: v.tensor_scalar(o, i, rstd_t, None, ALU.mult),
                   reads=h2_b + brstd, writes=xn_b)
                xnT_ap, xnT_b = phB.bf(0, KH * 128)
                xnT = xnT_ap.rearrange("p (cc q) -> p cc q", q=128)
                for kc in range(KH):
                    ev_tp = TE(lambda te, kc=kc, i=xn_ap: te.transpose(
                        pb[7][:, 0:128], i[:, kc * 128:(kc + 1) * 128], eyef_t[:, :]),
                        reads=xn_b + [beyef], writes=[bpb[7]], after=[ev_eyef])
                    SC(lambda s, kc=kc: s.copy(xnT[:, kc, :], pb[7][:, 0:128]),
                       reads=[bpb[7]], writes=xnT_b, after=[ev_tp])

                def mmlg(te, xnT=xnT):
                    for kc in range(KH):
                        inst = te.matmul(pb[6][:, 0:24], xnT[:, kc, :], wres_tiles[:, kc, :],
                                         start=(kc == 0), stop=(kc == KH - 1))
                    return inst
                ev_lg = TE(mmlg, reads=xnT_b + bwrs, writes=[bpb[6]], after=[ev_wres_ld])
                gidx = ph * NT_Q + ttg
                VE(lambda v, gidx=gidx: v.tensor_tensor(
                    logits_sb.rearrange("p (g e) -> p g e", e=24)[:, gidx, :],
                    pb[6][:, 0:24], base_bc, ALU.add),
                    reads=[bpb[6]] + bbase, writes=blg, after=[ev_base, ev_lg])

    lg3 = logits_sb.rearrange("p (g e) -> p g e", e=24)
    Z4 = Z_sb.rearrange("p (g i j) -> p g i j", i=4, j=4)
    ZT4 = ZT_sb.rearrange("p (g j i) -> p g j i", j=4, i=4)
    u4 = u_sb.rearrange("p (g i) -> p g i", i=4)
    v4 = v_sb.rearrange("p (g j) -> p g j", j=4)
    m4 = mx_sb.rearrange("p (g i) -> p g i", i=4)
    s4 = snk_sb.rearrange("p (g i j) -> p g i j", i=4, j=4)
    VE(lambda v: v.tensor_scalar_mul(Z_sb.rearrange("p (g e) -> p g e", e=16),
                                     lg3[:, :, 0:16], 1.0 / TAU),
       reads=blg, writes=bZ)
    VE(lambda v: v.tensor_copy(ZT4, Z4.transpose([0, 1, 3, 2])),
       reads=bZ, writes=bZT)
    VE(lambda v: v.memset(u_sb, 0.0), writes=bu)
    VE(lambda v: v.memset(v_sb, 0.0), writes=bv)
    for it in range(ITERS):
        for (src_b, src_sb, dst_b, dst_sb, oth4) in (
                (bZ, Z_sb, bu, u_sb, v4), (bZT, ZT_sb, bv, v_sb, u4)):
            oth_b = bv if dst_sb is u_sb else bu
            VE(lambda v, s_=src_sb, o4=oth4: v.tensor_tensor(
                s4, s_.rearrange("p (g a b) -> p g a b", a=4, b=4),
                o4.unsqueeze(2).broadcast_to([128, G, 4, 4]), ALU.add),
                reads=src_b + oth_b, writes=bsnk)
            VE(lambda v: v.reduce_max(mx_sb, s4, AX.X), reads=bsnk, writes=bmx)
            VE(lambda v: v.tensor_tensor(s4, s4,
                                         m4.unsqueeze(3).broadcast_to([128, G, 4, 4]),
                                         ALU.subtract),
               reads=bsnk + bmx, writes=bsnk)
            SC(lambda s: s.activation(snk_sb, snk_sb, AF.Exp), reads=bsnk, writes=bsnk)
            VE(lambda v: v.reduce_sum(junk_sb[:, 0:G * 4], s4, AX.X),
               reads=bsnk, writes=bjunk)
            SC(lambda s: s.activation(junk_sb[:, 0:G * 4], junk_sb[:, 0:G * 4], AF.Ln),
               reads=bjunk, writes=bjunk)
            VE(lambda v: v.tensor_add(junk_sb[:, 0:G * 4], junk_sb[:, 0:G * 4], mx_sb),
               reads=bjunk + bmx, writes=bjunk)
            VE(lambda v, d=dst_sb: v.tensor_scalar_mul(d, junk_sb[:, 0:G * 4], -1.0),
               reads=bjunk, writes=dst_b)
    VE(lambda v: v.tensor_tensor(Hres_sb.rearrange("p (g a b) -> p g a b", a=4, b=4),
                                 Z4, u4.unsqueeze(3).broadcast_to([128, G, 4, 4]), ALU.add),
       reads=bZ + bu, writes=bHr)
    VE(lambda v: v.tensor_tensor(Hres_sb.rearrange("p (g a b) -> p g a b", a=4, b=4),
                                 Hres_sb.rearrange("p (g a b) -> p g a b", a=4, b=4),
                                 v4.unsqueeze(2).broadcast_to([128, G, 4, 4]), ALU.add),
       reads=bHr + bv, writes=bHr)
    SC(lambda s: s.activation(Hres_sb, Hres_sb, AF.Exp), reads=bHr, writes=bHr)
    VE(lambda v: v.tensor_scalar(Hres_sb, Hres_sb, g_bc[:, 0:1], None, ALU.mult),
       reads=bHr + bg, writes=bHr, after=[ev_g5])
    ev_hres = VE(lambda v: v.scalar_tensor_tensor(
        Hres_sb.rearrange("p (g e) -> p g e", e=NST),
        eye4_bc.unsqueeze(1).broadcast_to([128, G, NST]),
        gm1_bc[:, 0:1], Hres_sb.rearrange("p (g e) -> p g e", e=NST), ALU.mult, ALU.add),
        reads=bHr + beye4 + bgm1, writes=bHr, after=[ev_eye4, ev_g6])
    for (lo, dst_sb, dst_b) in ((16, Hpre_sb, bHp), (20, Hpost_sb, bHq)):
        s3 = snk_sb[:, 0:G * 4].rearrange("p (g e) -> p g e", e=4)
        VE(lambda v, lo=lo: v.tensor_copy(
            snk_sb[:, 0:G * 4].rearrange("p (g e) -> p g e", e=4), lg3[:, :, lo:lo + 4]),
           reads=blg, writes=bsnk)
        VE(lambda v, s3=s3: v.reduce_max(mx_sb[:, 0:G], s3, AX.X), reads=bsnk, writes=bmx)
        VE(lambda v: v.tensor_tensor(
            snk_sb[:, 0:G * 4].rearrange("p (g e) -> p g e", e=4),
            snk_sb[:, 0:G * 4].rearrange("p (g e) -> p g e", e=4),
            mx_sb[:, 0:G].rearrange("p (g o) -> p g o", o=1).broadcast_to([128, G, 4]),
            ALU.subtract),
           reads=bsnk + bmx, writes=bsnk)
        SC(lambda s: s.activation(snk_sb[:, 0:G * 4], snk_sb[:, 0:G * 4], AF.Exp),
           reads=bsnk, writes=bsnk)
        VE(lambda v, s3=s3: v.reduce_sum(t4_sb[:, 0:G], s3, AX.X), reads=bsnk, writes=bt4)
        VE(lambda v: v.reciprocal(t4_sb[:, G:2 * G], t4_sb[:, 0:G]), reads=bt4, writes=bt4)
        VE(lambda v, d=dst_sb: v.tensor_tensor(
            d.rearrange("p (g e) -> p g e", e=4),
            snk_sb[:, 0:G * 4].rearrange("p (g e) -> p g e", e=4),
            t4_sb[:, G:2 * G].rearrange("p (g o) -> p g o", o=1).broadcast_to([128, G, 4]),
            ALU.mult),
           reads=bsnk + bt4, writes=dst_b)

    Hr3 = Hres_sb.rearrange("p (g e) -> p g e", e=NST)
    Hp3 = Hpre_sb.rearrange("p (g e) -> p g e", e=4)
    Hq3 = Hpost_sb.rearrange("p (g e) -> p g e", e=4)
    for ph in range(2):
        ev_gb0 = bcast_mod(phG, 0, ph, 5, 0, H)
        gb_ap, gb_b = phG.f(0, H)
        ev_gb = clamp_ga(phG, gb_b, ev_gb0)
        for ttg in range(NT_Q):
            gidx = ph * NT_Q + ttg
            if ttg % 2 == 0:
                h2_ap, h2_b = phB.f(0, H)
                h2c = [h2_ap[:, n * CD:(n + 1) * CD] for n in range(NSTR)]
                DMA(lambda s, ph=ph, ttg=ttg, d=h2_ap: s.dma_start(
                    out=d, in_=h2_sp.ap()[ph, ttg * 128:(ttg + 1) * 128, :]),
                    reads=[bB(("h2_sp", ph, ttg))], writes=h2_b)
            else:
                hl_ap, hl_b = phL.f(0, 2 * CD)
                hj_ap, hj_b = phJ.f(0, 2 * CD)
                h2_b = hl_b + hj_b
                h2c = [hl_ap[:, 0:CD], hl_ap[:, CD:2 * CD],
                       hj_ap[:, 0:CD], hj_ap[:, CD:2 * CD]]
                DMA(lambda s, ph=ph, ttg=ttg, d=hl_ap: s.dma_start(
                    out=d, in_=h2_sp.ap()[ph, ttg * 128:(ttg + 1) * 128, 0:2 * CD]),
                    reads=[bB(("h2_sp", ph, ttg))], writes=hl_b)
                DMA(lambda s, ph=ph, ttg=ttg, d=hj_ap: s.dma_start(
                    out=d, in_=h2_sp.ap()[ph, ttg * 128:(ttg + 1) * 128, 2 * CD:4 * CD]),
                    reads=[bB(("h2_sp", ph, ttg))], writes=hj_b)
            if ttg % 2 == 0:
                x2f_ap, x2f_b = phB2.f(0, H)
                x2c = [(x2f_ap[:, 0:H // 2], x2f_b), (x2f_ap[:, H // 2:H], x2f_b)]
                DMA(lambda s, ph=ph, ttg=ttg, d=x2f_ap: s.dma_start(
                    out=d, in_=x2_sp.ap()[ph, ttg * 128:(ttg + 1) * 128, :]),
                    reads=[bB(("x2_sp", ph, ttg))], writes=x2f_b)
            else:
                xh_ap, xh_b = phH.f(0, H // 2)
                xi2_ap, xi2_b = phI.f(0, H // 2)
                x2c = [(xh_ap, xh_b), (xi2_ap, xi2_b)]
                DMA(lambda s, ph=ph, ttg=ttg, d=xh_ap: s.dma_start(
                    out=d, in_=x2_sp.ap()[ph, ttg * 128:(ttg + 1) * 128, 0:H // 2]),
                    reads=[bB(("x2_sp", ph, ttg))], writes=xh_b)
                DMA(lambda s, ph=ph, ttg=ttg, d=xi2_ap: s.dma_start(
                    out=d, in_=x2_sp.ap()[ph, ttg * 128:(ttg + 1) * 128, H // 2:H]),
                    reads=[bB(("x2_sp", ph, ttg))], writes=xi2_b)
            xp0_ap, xp0_b = phD.f(2048, 2048 + CD)
            xp1_ap, xp1_b = phD.f(2048 + CD, 2048 + 2 * CD) if CD <= 1024 else phD.f(2048, 2048 + CD)
            VE(lambda v, gidx=gidx, h0=h2c[0]: v.tensor_scalar(xp0_ap, h0,
                                                    Hp3[:, gidx, 0:1], None, ALU.mult),
               reads=h2_b + bHp, writes=xp0_b)
            cur_ap, cur_b, oth_ap, oth_b = xp0_ap, xp0_b, xp1_ap, xp1_b
            for n in range(1, NSTR):
                VE(lambda v, n=n, cu=cur_ap, ot=oth_ap, gidx=gidx, hn=h2c[n]: v.scalar_tensor_tensor(
                    ot, hn, Hp3[:, gidx, n:n + 1], cu,
                    ALU.mult, ALU.add),
                    reads=h2_b + bHp + cur_b, writes=oth_b)
                cur_ap, cur_b, oth_ap, oth_b = oth_ap, oth_b, cur_ap, cur_b
            xm_ap, xm_b = phC.f(0, H)
            for i in range(NSTR):
                VE(lambda v, i=i, gidx=gidx, h0=h2c[0]: v.tensor_scalar(
                    xm_ap[:, i * CD:(i + 1) * CD], h0,
                    Hr3[:, gidx, i * 4:i * 4 + 1], None, ALU.mult),
                    reads=h2_b + bHr, writes=xm_b, after=[ev_hres])
                for jj in range(1, NSTR):
                    VE(lambda v, i=i, jj=jj, gidx=gidx, hj=h2c[jj]: v.scalar_tensor_tensor(
                        xm_ap[:, i * CD:(i + 1) * CD], hj,
                        Hr3[:, gidx, i * 4 + jj:i * 4 + jj + 1],
                        xm_ap[:, i * CD:(i + 1) * CD], ALU.mult, ALU.add),
                        reads=h2_b + bHr + xm_b, writes=xm_b)
            xpT_ap, xpT_b = phD.bf(0, KCD * 128)
            xpT = xpT_ap.rearrange("p (k q) -> p k q", q=128)
            for kk in range(KCD):
                ev_tp = TE(lambda te, kk=kk, cu=cur_ap: te.transpose(
                    pb[7][:, 0:128], cu[:, kk * 128:(kk + 1) * 128], eyef_t[:, :]),
                    reads=cur_b + [beyef], writes=[bpb[7]], after=[ev_eyef])
                SC(lambda s, kk=kk: s.copy(xpT[:, kk, :], pb[7][:, 0:128]),
                   reads=[bpb[7]], writes=xpT_b, after=[ev_tp])
            ybr_ap, ybr_b = phE.f(0, CD)
            nwc = (CD + WBN - 1) // WBN
            for cc2 in range(nwc):
                w0 = cc2 * WBN
                wid = min(WBN, CD - w0)
                bank = 4 + cc2 % 2
                wb_ap, wb_b = (phF if (ttg * nwc + cc2) % 2 == 0 else phK).bf(0, KCD * wid)
                wb_tiles = wb_ap.rearrange("p (k w) -> p k w", w=wid)
                ev_wbl = DMA(lambda s, w0=w0, wid=wid, wt=wb_tiles: s.dma_start(
                    out=wt, in_=wb_bf.ap()[:, w0:w0 + wid].rearrange("(k p) w -> p k w", p=128)),
                    reads=[bB("wb_bf")], writes=wb_b, after=[ev_wb])

                def mmwb(te, w0=w0, wid=wid, bank=bank, xpT=xpT, wt=wb_tiles):
                    for kk in range(KCD):
                        inst = te.matmul(pb[bank][:, 0:wid], xpT[:, kk, :],
                                         wt[:, kk, :],
                                         start=(kk == 0), stop=(kk == KCD - 1))
                    return inst
                ev_mw = TE(mmwb, reads=xpT_b + wb_b, writes=[bpb[bank]], after=[ev_wbl])
                VE(lambda v, w0=w0, wid=wid, bank=bank: v.tensor_copy(
                    ybr_ap[:, w0:w0 + wid], pb[bank][:, 0:wid]),
                    reads=[bpb[bank]], writes=ybr_b, after=[ev_mw])
            for n in range(NSTR):
                VE(lambda v, n=n, gidx=gidx: v.scalar_tensor_tensor(
                    xm_ap[:, n * CD:(n + 1) * CD], ybr_ap, Hq3[:, gidx, n:n + 1],
                    xm_ap[:, n * CD:(n + 1) * CD], ALU.mult, ALU.add),
                    reads=ybr_b + bHq + xm_b, writes=xm_b)
            o_ap, o_b = phA.f(0, H)
            VE(lambda v: v.tensor_tensor(o_ap, xm_ap, gb_ap, ALU.mult),
               reads=xm_b + gb_b, writes=o_b, after=[ev_gb])
            for hf in range(2):
                xa, xb = x2c[hf]
                ev_o = VE(lambda v, hf=hf, xa=xa: v.tensor_add(
                    o_ap[:, hf * (H // 2):(hf + 1) * (H // 2)],
                    o_ap[:, hf * (H // 2):(hf + 1) * (H // 2)], xa),
                    reads=xb + o_b, writes=o_b)
            DMA(lambda s, ph=ph, ttg=ttg, i=o_ap: s.dma_start(
                out=out_my.ap()[ph, ttg * 128:(ttg + 1) * 128, :], in_=i),
                reads=o_b, writes=[bB("out_dma")], after=[ev_o])

    if cfg.get("TAPS2"):
        def tapcp(name, t, shape, deps):
            o = nc.dram_tensor(name, list(shape), t.dtype if hasattr(t, "dtype") else F32,
                               kind="ExternalOutput")
            n = 1
            for s_ in shape:
                n *= s_
            DMA(lambda s, o=o, t=t: s.dma_start(out=o.ap().rearrange(
                " ".join("abcde"[:len(shape)]) + " -> (" + " ".join("abcde"[:len(shape)]) + ")"),
                in_=t.ap().rearrange(
                " ".join("abcde"[:len(shape)]) + " -> (" + " ".join("abcde"[:len(shape)]) + ")")),
                reads=[bB(k) for k in deps], writes=[bB(name)])
        tapcp("o_x2", x2_sp, (2, QT, H), [("x2_sp", ph_, t_) for ph_ in range(2) for t_ in range(NT_Q)])
        tapcp("o_h2", h2_sp, (2, QT, H), [("h2_sp", ph_, t_) for ph_ in range(2) for t_ in range(NT_Q)])
        tapcp("o_ssm0", ssm_out[0], (T, E), [("ssm_out", 0)])
        tapcp("o_dt", dt_sp, (B, DSL, T), [("dt_sp", b_, d_, th_) for b_ in range(B)
                                           for d_ in range(NT_D) for th_ in range(2)])
        tapcp("o_xc", xc_sp, (B, DSL, T), [("xc_sp", b_, d_, th_) for b_ in range(B)
                                           for d_ in range(NT_D) for th_ in range(2)])
        tapcp("o_xsp", x_sp, (B, DSL, T), [("x_sp", b_, d_, tc_) for b_ in range(B)
                                           for d_ in range(NT_D) for tc_ in range(NTC)])
        tapcp("o_zsp", z_sp, (B, DSL, T), [("z_sp", b_, d_, tc_) for b_ in range(B)
                                           for d_ in range(NT_D) for tc_ in range(NTC)])
        tapcp("o_Bt", Bt_i, (B, NST, T), [("Bt_i", b_) for b_ in range(B)])
        tapcp("o_Ct", Ct_i, (B, NST, T), [("Ct_i", b_) for b_ in range(B)])
        tapcp("o_ymyA", y_my[0], (NC, DSL, QT), [("y_my", 0)])
        tapcp("o_mod", mod_my, (NC, 2, MODC), ["mod_my"])
        tapcp("o_hT", hT_all, (NC, H, 2 * QT), ["hT_all"])

    if cfg.get("DEBUG_TAPS"):
        dbg = nc.dram_tensor("dbg", [128, G * 24 + 3 * G * NST + 3 * CD + 2 * H], F32,
                             kind="ExternalOutput")
        o0 = 0
        xpf_ap, xpf_b = phD.f(2048, 2048 + CD)
        xpf2_ap, xpf2_b = phD.f(2048 + CD, 2048 + 2 * CD)
        ybr2_ap, ybr2_b = phE.f(0, CD)
        xm2_ap, xm2_b = phC.f(0, H)
        for (ap_, bufs_, w_) in ((logits_sb, blg, G * 24), (Hres_sb, bHr, G * NST),
                                 (Hpre_sb, bHp, G * 4), (Hpost_sb, bHq, G * 4),
                                 (xpf_ap, xpf_b, CD), (xpf2_ap, xpf2_b, CD),
                                 (ybr2_ap, ybr2_b, CD), (xm2_ap, xm2_b, H),
                                 (phG.f(0, H)[0], phG.f(0, H)[1], H)):
            DMA(lambda s, a=ap_, o=o0, w=w_: s.dma_start(out=dbg.ap()[:, o:o + w], in_=a),
                reads=bufs_, writes=[bB("dbg")])
            o0 += w_

    fin = bB("out_dma").ready
    P.op("gp", lambda g: g.engine_nop(), after=[fin])

    P.check_deadlock()
    global _LASTPROG
    _LASTPROG = P
    semnames = sorted(P.cnt.keys())
    with contextlib.ExitStack() as sctx:
        sems = {s: sctx.enter_context(nc.semaphore(f"sem_{s}")) for s in semnames}
        with nc.Block() as block:
            with allow_ici_all_to_all():
                P.realize(sems, block)
    ctx.close()
    return nc


def _stage_inputs(inputs, c, k):
    d = derived(c)
    B, T, H, DIN, NST, KCV, R = c["B"], c["T"], c["H"], c["DIN"], c["NST"], c["KCV"], c["R"]
    DSL, MODC, QT, RT, RP, NCP = d["DSL"], d["MODC"], d["QT"], d["RT"], d["RP"], d["NCP"]
    f32 = np.float32
    x = np.asarray(inputs["L0_attn_hidden_states"], f32)
    q = k % 4
    bA, bBv = k // 4, 2 + k // 4
    x_myv = np.stack([x[bA, q * QT:(q + 1) * QT, :], x[bBv, q * QT:(q + 1) * QT, :]])
    win = np.asarray(inputs["L0_attn_in_proj_w"], f32)
    win_cat = np.concatenate([win[:, k * DSL:(k + 1) * DSL],
                              win[:, DIN + k * DSL:DIN + (k + 1) * DSL]], axis=1)
    win_tv = np.ascontiguousarray(
        win_cat.reshape(H // 128, 128, NCP, 256).transpose(2, 0, 1, 3))
    wmod = np.asarray(inputs["L0_wmod"], f32)
    return {
        "x_my": np.ascontiguousarray(x_myv),
        "cT_in": np.ascontiguousarray(np.asarray(inputs["L0_c"], f32).T),
        "wmod_sl": np.ascontiguousarray(wmod[:, k * MODC:(k + 1) * MODC]),
        "win_t": win_tv,
        "xpwT_sl": np.ascontiguousarray(
            np.asarray(inputs["L0_attn_x_proj_w"], f32)[:, k * DSL:(k + 1) * DSL].T),
        "dtpwT_sl": np.ascontiguousarray(
            np.asarray(inputs["L0_attn_dt_proj_w"], f32)[k * DSL:(k + 1) * DSL, :].T
            .reshape(RT, RP, DSL)),
        "dtb_sl": np.ascontiguousarray(
            np.asarray(inputs["L0_attn_dt_bias"], f32)[k * DSL:(k + 1) * DSL]),
        "A_sl": np.ascontiguousarray(
            np.asarray(inputs["L0_attn_A_log"], f32)[k * DSL:(k + 1) * DSL]),
        "D_sl": np.ascontiguousarray(
            np.asarray(inputs["L0_attn_D_param"], f32)[k * DSL:(k + 1) * DSL]),
        "convw_sl": np.ascontiguousarray(
            np.asarray(inputs["L0_attn_conv_w"], f32)[k * DSL:(k + 1) * DSL]),
        "convb_sl": np.ascontiguousarray(
            np.asarray(inputs["L0_attn_conv_b"], f32)[k * DSL:(k + 1) * DSL]),
        "wout_sl": np.ascontiguousarray(
            np.asarray(inputs["L0_attn_out_proj_w"], f32).T[k * DSL:(k + 1) * DSL, :]),
        "dt_ln": np.ascontiguousarray(np.asarray(inputs["L0_attn_dt_ln_w"], f32)),
        "B_ln": np.ascontiguousarray(np.asarray(inputs["L0_attn_B_ln_w"], f32)),
        "C_ln": np.ascontiguousarray(np.asarray(inputs["L0_attn_C_ln_w"], f32)),
        "wres_t": np.ascontiguousarray(np.concatenate(
            [np.asarray(inputs["L0_ffn_W_res"], f32),
             np.asarray(inputs["L0_ffn_W_pre"], f32),
             np.asarray(inputs["L0_ffn_W_post"], f32)], axis=0).T),
        "base_cat": np.ascontiguousarray(np.concatenate(
            [np.asarray(inputs["L0_ffn_H_res_base"], f32).ravel(),
             np.asarray(inputs["L0_ffn_H_pre_base"], f32).ravel(),
             np.asarray(inputs["L0_ffn_H_post_base"], f32).ravel()])),
        "wb_in": np.ascontiguousarray(np.asarray(inputs["L0_ffn_W_branch"], f32)),
        "gate_in": np.ascontiguousarray(np.asarray(inputs["L0_ffn_gate"], f32).reshape(1, 1)),
        "eye_f": np.eye(128, dtype=f32),
        "eye4_in": np.ascontiguousarray(np.eye(4, dtype=f32).ravel()),
    }


def kernel(**inputs):
    c = real_cfg()
    d = derived(c)
    nc = build_nc(c)
    in_maps = [_stage_inputs(inputs, c, k) for k in range(c["NC"])]
    with allow_ici_all_to_all():
        res = run_bass_kernel_spmd(nc, in_maps, core_ids=list(range(c["NC"])))
    B, T, H, QT = c["B"], c["T"], c["H"], d["QT"]
    out = np.zeros((B, T, H), np.float32)
    for k in range(c["NC"]):
        o = np.asarray(res.results[k]["out_my"])
        q = k % 4
        out[k // 4, q * QT:(q + 1) * QT, :] = o[0]
        out[2 + k // 4, q * QT:(q + 1) * QT, :] = o[1]
    return out


# version 22
# speedup vs baseline: 1.0086x; 1.0086x over previous
import contextlib
import numpy as np

import concourse.bass as bass
import concourse.mybir as mybir
from concourse.bass_utils import run_bass_kernel_spmd
from concourse.aot_env import allow_ici_all_to_all

F32 = mybir.dt.float32
BF16 = mybir.dt.bfloat16
AF = mybir.ActivationFunctionType
ALU = mybir.AluOpType
AX = mybir.AxisListType

ENGS = ("te", "ve", "sc", "gp", "sy")
_LASTPROG = None


def real_cfg():
    return dict(B=4, T=2048, H=4096, DIN=8192, NST=16, KCV=4, R=256, NC=8,
                TAU=0.1, ITERS=50, NSTR=4,
                EPS_LN=1e-5, EPS_RMS_A=1e-6, EPS_RMS_F=1.1920929e-07)


def derived(c):
    d = dict(c)
    d["DSL"] = c["DIN"] // c["NC"]
    d["NT_D"] = d["DSL"] // 128
    d["KH"] = c["H"] // 128
    d["KD"] = c["DIN"] // 128
    d["MODC"] = 6 * c["H"] // c["NC"]
    d["QT"] = c["T"] // 4
    d["NT_Q"] = d["QT"] // 128
    d["TC"] = 256
    d["NTC"] = c["T"] // d["TC"]
    d["THL"] = c["T"] // 2
    d["DTBLK"] = min(4, d["NT_D"])
    d["NBLK"] = max(1, d["NT_D"] // d["DTBLK"])
    d["E"] = c["R"] + 2 * c["NST"]
    d["RT"] = (c["R"] + 127) // 128
    d["RP"] = min(c["R"], 128)
    d["CD"] = c["H"] // c["NSTR"]
    d["KCD"] = d["CD"] // 128
    d["NCP"] = 2 * d["DSL"] // 256
    d["G"] = 2 * d["NT_Q"]
    d["WBN"] = min(512, d["CD"])
    return d


class Buf:
    __slots__ = ("ready", "users")

    def __init__(self):
        self.ready = None
        self.users = []


class Prog:
    NROT = {"dma": 16, "cc": 8}

    def __init__(self):
        self.stream = {e: [] for e in ENGS}
        self.cnt = {}
        self.bufs = {}
        self.qcnt = {}
        self.evwm = {}

    def B(self, key):
        b = self.bufs.get(key)
        if b is None:
            b = self.bufs[key] = Buf()
        return b

    def op(self, eng, fn, reads=(), writes=(), after=(), kind="compute"):
        deps = [a for a in after if a is not None]
        if kind == "compute":
            sem, inc = eng, 1
            self.cnt[sem] = self.cnt.get(sem, 0) + inc
        else:
            if kind == "dma":
                qbase, inc = {"sy": "syd", "gp": "gpd", "sc": "scd"}[eng], 16
            else:
                qbase, inc = "cc", 1
            nrot = self.NROT[kind]
            idx = self.qcnt.get(qbase, 0)
            self.qcnt[qbase] = idx + 1
            sem = f"{qbase}{idx % nrot}"
            prev = self.cnt.get(sem, 0)
            if prev > 0:
                deps.append((sem, prev))
            self.cnt[sem] = prev + inc
        ev = (sem, self.cnt[sem])
        for b in reads:
            if b.ready is not None:
                deps.append(b.ready)
        for b in writes:
            deps.extend(b.users)
            if b.ready is not None:
                deps.append(b.ready)
        wm = {}
        for s, v in deps:
            if wm.get(s, 0) < v:
                wm[s] = v
        for b in reads:
            b.users.append(ev)
        for b in writes:
            b.ready = ev
            b.users = []
        self.stream[eng].append((tuple(sorted(wm.items())), fn, sem, inc))
        return ev

    def check_deadlock(self):
        ptr = {e: 0 for e in ENGS}
        val = {}
        progressed = True
        while progressed:
            progressed = False
            for e in ENGS:
                while ptr[e] < len(self.stream[e]):
                    waits, _, sem, inc = self.stream[e][ptr[e]]
                    if any(val.get(s, 0) < v for s, v in waits):
                        break
                    val[sem] = val.get(sem, 0) + inc
                    ptr[e] += 1
                    progressed = True
        stuck = {e: (ptr[e], len(self.stream[e])) for e in ENGS if ptr[e] < len(self.stream[e])}
        if stuck:
            msgs = []
            for e, (p, n) in stuck.items():
                waits = self.stream[e][p][0]
                bad = [(s, v, val.get(s, 0)) for s, v in waits if val.get(s, 0) < v]
                msgs.append(f"{e}@{p}/{n} unmet={bad[:4]}")
            raise RuntimeError("deadlock: " + "; ".join(msgs))

    def realize(self, sems, block):
        def mk(e):
            def body(eng):
                maxw = {}
                for waits, fn, sem, inc in self.stream[e]:
                    for s, v in waits:
                        if maxw.get(s, 0) < v:
                            eng.wait_ge(sems[s], v)
                            maxw[s] = v
                    fn(eng).then_inc(sems[sem], inc)
            return body
        block.sync(mk("sy"))
        block.tensor(mk("te"))
        block.vector(mk("ve"))
        block.scalar(mk("sc"))
        block.gpsimd(mk("gp"))


class Tile:

    def __init__(self, P, t, name, cols):
        self.P = P
        self.t = t
        self.name = name
        self.cols = cols
        self.ngrid = (cols + 511) // 512

    def bufs(self, lo, hi):
        return [self.P.B((self.name, i)) for i in range(lo // 512, (hi + 511) // 512)]

    def f(self, lo, hi):
        return self.t[:, lo:hi], self.bufs(lo, hi)

    def bf(self, lo, hi_bf):
        ap = self.t[:, lo:self.cols].bitcast(BF16)[:, 0:hi_bf]
        return ap, self.bufs(lo, lo + (hi_bf + 1) // 2)


def build_nc(cfg):
    c = derived(cfg)
    B, T, H, DIN, NST, KCV, R, NC = (c["B"], c["T"], c["H"], c["DIN"], c["NST"],
                                     c["KCV"], c["R"], c["NC"])
    DSL, NT_D, KH, KD, MODC = c["DSL"], c["NT_D"], c["KH"], c["KD"], c["MODC"]
    QT, NT_Q, TC, NTC, THL = c["QT"], c["NT_Q"], c["TC"], c["NTC"], c["THL"]
    DTBLK, NBLK, E, RT, RP = c["DTBLK"], c["NBLK"], c["E"], c["RT"], c["RP"]
    CD, KCD, NCP, G, WBN = c["CD"], c["KCD"], c["NCP"], c["G"], c["WBN"]
    NSTR, TAU, ITERS = c["NSTR"], c["TAU"], c["ITERS"]
    NQB = THL // QT

    nc = bass.Bass("TRN2", target_bir_lowering=False, detect_race_conditions=False)
    P = Prog()
    bB = P.B

    def din(name, shape, dtype=F32):
        return nc.dram_tensor(name, list(shape), dtype, kind="ExternalInput")

    def dint(name, shape, dtype=F32):
        return nc.dram_tensor(name, list(shape), dtype, kind="Internal")

    x_my = din("x_my", (2, QT, H))
    cT_in = din("cT_in", (H, B))
    wmod_sl = din("wmod_sl", (H, MODC))
    win_t = din("win_t", (NCP, KH, 128, 256))
    xpwT_sl = din("xpwT_sl", (DSL, E))
    dtpwT_sl = din("dtpwT_sl", (RT, RP, DSL))
    dtb_sl = din("dtb_sl", (DSL,))
    A_sl = din("A_sl", (DSL, NST))
    D_sl = din("D_sl", (DSL,))
    convw_sl = din("convw_sl", (DSL, KCV))
    convb_sl = din("convb_sl", (DSL,))
    wout_sl = din("wout_sl", (DSL, H))
    dt_ln = din("dt_ln", (R,))
    B_ln = din("B_ln", (NST,))
    C_ln = din("C_ln", (NST,))
    wres_t = din("wres_t", (H, 24))
    base_cat = din("base_cat", (24,))
    wb_in = din("wb_in", (CD, CD))
    gate_in = din("gate_in", (1, 1))
    eye_f = din("eye_f", (128, 128))
    eye4_in = din("eye4_in", (NST,))

    out_my = nc.dram_tensor("out_my", [2, QT, H], F32, kind="ExternalOutput")

    win_bf = dint("win_bf", (NCP, KH, 128, 256), BF16)
    wout_bf = dint("wout_bf", (DSL, H), BF16)
    wout_all = dint("wout_all", (NC, DSL, H), BF16)
    wmod_bf = dint("wmod_bf", (H, MODC), BF16)
    wb_bf = dint("wb_bf", (CD, CD), BF16)
    xpwT_bf = dint("xpwT_bf", (DSL, E), BF16)
    dtpwT_bf = dint("dtpwT_bf", (RT, RP, DSL), BF16)
    wres_bf = dint("wres_bf", (H, 24), BF16)
    mod_part = dint("mod_part", (NC, 2, MODC))
    mod_my = dint("mod_my", (NC, 2, MODC))
    hT_my = [dint(f"hT_my{p}", (H, QT), BF16) for p in range(2)]
    hT_all = [dint(f"hT_all{p}", (NC, H, QT), BF16) for p in range(2)]
    ssm_in = [dint(f"ssm_in{b}", (T, E)) for b in range(B)]
    ssm_out = [dint(f"ssm_out{b}", (T, E)) for b in range(B)]
    x_sp = dint("x_sp", (B, DSL, T))
    z_sp = dint("z_sp", (B, DSL, T))
    xc_sp = dint("xc_sp", (B, DSL, T))
    xcbf_sp = dint("xcbf_sp", (B, DSL, T), BF16)
    dt_sp = dint("dt_sp", (B, DSL, T))
    Bt_i = dint("Bt_i", (B, NST, T))
    Ct_i = dint("Ct_i", (B, NST, T))
    g_dr = dint("g_dr", (1, 1))
    y_in = [dint(f"y_in{p}", (NC, DSL, QT), BF16) for p in range(2)]
    y_my = [dint(f"y_my{p}", (NC, DSL, QT), BF16) for p in range(2)]
    x2_sp = dint("x2_sp", (2, QT, H))
    h2_sp = dint("h2_sp", (2, QT, H))

    ctx = contextlib.ExitStack()

    def TT(name, cols):
        t = ctx.enter_context(nc.sbuf_tensor(name, [128, cols], F32))
        return Tile(P, t, name, cols)

    phA = TT("phA", 4096)
    phB = TT("phB", 4096)
    phB2 = TT("phB2", 4096)
    phC = TT("phC", 4096)
    phD = TT("phD", 4096)
    phE = TT("phE", 4096)
    phF = TT("phF", 3072)
    phG = TT("phG", 4096)
    phH = TT("phH", 2048)
    phI = TT("phI", 2048)
    phJ = TT("phJ", 2048)
    phK = TT("phK", 3072)
    phL = TT("phL", 2048)
    phM = TT("phM", 3072)
    smT = TT("sm", 5120)
    dtpwT_t = ctx.enter_context(nc.sbuf_tensor("dtpwT_sb", [RP, RT * DSL], BF16))
    eyef_t = ctx.enter_context(nc.sbuf_tensor("eyef_sb", [128, 128], F32))

    _smoff = [0]

    def sml(name, cols):
        off = _smoff[0]
        _smoff[0] += cols
        assert _smoff[0] <= 5120, f"sm overflow at {name}"
        return smT.t[:, off:off + cols], [bB(("sm", name))]

    A_sb, bA_sb = sml("A", NT_D * NST)
    D_sb, bD_sb = sml("D", NT_D)
    dtb_sb, bdtb = sml("dtb", NT_D)
    cvb_sb, bcvb = sml("cvb", NT_D)
    cvw_sb0, bcvw = sml("cvw", NT_D * KCV)
    cvw_sb = cvw_sb0.rearrange("p (d k) -> p d k", k=KCV)
    dtln_bc, bdtln = sml("dtln", R)
    Bln_bc, bBln = sml("Bln", NST)
    Cln_bc, bCln = sml("Cln", NST)
    base_bc, bbase = sml("base", 24)
    eye4_bc, beye4 = sml("eye4", NST)
    g_bc, bg = sml("g", 1)
    gm1_bc, bgm1 = sml("gm1", 1)
    s1_t, bs1 = sml("s1", 1)
    s2_t, bs2 = sml("s2", 1)
    mu_t, bmu = sml("mu", 1)
    var_t, bvar = sml("var", 1)
    rstd_t, brstd = sml("rstd", 1)
    rho_t, brho = sml("rho", 3)
    sstate, bsst = sml("sstate", NT_D * NST)
    ct_f, bctf = sml("ct_f", KH * B)
    ct_b0, bctb = sml("ct_b", (KH * B + 1) // 2)
    ct_bf = ct_b0.bitcast(BF16)[:, 0:KH * B]
    btct, bbtct = sml("btct", 128)
    xcb_t0, bxcb0 = sml("xcb0", 128)
    xcb_t1, bxcb1 = sml("xcb1", 128)
    xcb_t2, bxcb2 = sml("xcb2", 128)
    xcb_t3, bxcb3 = sml("xcb3", 128)
    logits_sb, blg = sml("logits", G * 24)
    Z_sb, bZ = sml("Z", G * NST)
    ZT_sb, bZT = sml("ZT", G * NST)
    u_sb, bu = sml("u", G * 4)
    v_sb, bv = sml("v", G * 4)
    mx_sb, bmx = sml("mx", G * 4)
    t4_sb, bt4 = sml("t4", 2 * G)
    Hres_sb, bHr = sml("Hres", G * NST)
    Hpre_sb, bHp = sml("Hpre", G * 4)
    Hpost_sb, bHq = sml("Hpost", G * 4)
    snk_sb, bsnk = sml("snk", G * NST)
    ev0_sb, bev0 = sml("ev0", 288)
    ev1_sb, bev1 = sml("ev1", 288)
    wrs0, bwrs = sml("wrs", (KH * 24 + 1) // 2)
    wres_tiles = wrs0.bitcast(BF16)[:, 0:KH * 24].rearrange("p (c e) -> p c e", e=24)
    xpw0, bxpw = sml("xpw", (NT_D * E + 1) // 2)
    xpwT_sb = xpw0.bitcast(BF16)[:, 0:NT_D * E].rearrange("p (c e) -> p c e", e=E)
    gt_sb, bgt = sml("gt", 4)
    junk_sb, bjunk = sml("junk", 260)

    pb = [ctx.enter_context(nc.psum_tensor(f"pb{i}", [128, 512], F32)) for i in range(8)]
    bpb = [bB(("pb", i)) for i in range(8)]
    beyef = bB("eyef")
    bdtpw = bB("dtpw_sb")

    def TE(fn, **kw):
        return P.op("te", fn, **kw)

    def VE(fn, **kw):
        return P.op("ve", fn, **kw)

    def SC(fn, **kw):
        return P.op("sc", fn, **kw)

    def GP(fn, **kw):
        return P.op("gp", fn, **kw)

    def DMA(fn, **kw):
        return P.op("sy", fn, kind="dma", **kw)

    def SDMA(fn, **kw):
        return P.op("sc", fn, kind="dma", **kw)

    def GDMA(fn, **kw):
        return P.op("gp", fn, kind="dma", **kw)

    def CC(fn, **kw):
        return P.op("gp", fn, kind="cc", **kw)

    ev_win = []
    for i in range(NCP):
        ev_win.append(GDMA(lambda g, i=i: g.dma_start(out=win_bf.ap()[i], in_=win_t.ap()[i]),
                           writes=[bB(("win_bf", i))]))
    nwo = 4
    ev_wout_c = []
    for i in range(nwo):
        r0, r1 = i * DSL // nwo, (i + 1) * DSL // nwo
        ev_wout_c.append(GDMA(lambda g, r0=r0, r1=r1: g.dma_start(
            out=wout_bf.ap()[r0:r1, :], in_=wout_sl.ap()[r0:r1, :]), writes=[bB("wout_bf")]))
    nwm = 8
    ev_wmod_c = []
    for i in range(nwm):
        r0, r1 = i * H // nwm, (i + 1) * H // nwm
        ev_wmod_c.append(GDMA(lambda g, r0=r0, r1=r1: g.dma_start(
            out=wmod_bf.ap()[r0:r1, :], in_=wmod_sl.ap()[r0:r1, :]), writes=[bB("wmod_bf")]))
    ev_wb = GDMA(lambda g: g.dma_start(out=wb_bf.ap(), in_=wb_in.ap()), writes=[bB("wb_bf")])
    ev_xpwc = GDMA(lambda g: g.dma_start(out=xpwT_bf.ap(), in_=xpwT_sl.ap()), writes=[bB("xpw_bf")])
    ev_dtpwc = GDMA(lambda g: g.dma_start(out=dtpwT_bf.ap(), in_=dtpwT_sl.ap()),
                    writes=[bB("dtpw_bf")])
    ev_wresc = GDMA(lambda g: g.dma_start(out=wres_bf.ap(), in_=wres_t.ap()), writes=[bB("wres_bf")])

    def ld(dst, src, writes, after=(), slow=False):
        return DMA(lambda s, d=dst, i=src, slow=slow: s.dma_start(
            out=d, in_=i, allow_slow_non_contiguous=slow), writes=writes, after=after)

    ev_A0 = ld(A_sb.rearrange("p (c n) -> p c n", n=NST),
               A_sl.ap().rearrange("(c p) n -> p c n", p=128), bA_sb)
    ev_A1 = SC(lambda s: s.activation(A_sb, A_sb, AF.Exp), reads=bA_sb, writes=bA_sb)
    ev_A = VE(lambda v: v.tensor_scalar_mul(A_sb, A_sb, -1.0), reads=bA_sb, writes=bA_sb)
    ev_D = ld(D_sb, D_sl.ap().rearrange("(c p) -> p c", p=128), bD_sb, slow=True)
    ev_dtb = ld(dtb_sb, dtb_sl.ap().rearrange("(c p) -> p c", p=128), bdtb, slow=True)
    ev_cvb = ld(cvb_sb, convb_sl.ap().rearrange("(c p) -> p c", p=128), bcvb, slow=True)
    ev_cvw = ld(cvw_sb0.rearrange("p (c k) -> p c k", k=KCV),
                convw_sl.ap().rearrange("(c p) k -> p c k", p=128), bcvw)
    ev_dtln = ld(dtln_bc, dt_ln.ap().unsqueeze(0).broadcast_to([128, R]), bdtln)
    ev_Bln = ld(Bln_bc, B_ln.ap().unsqueeze(0).broadcast_to([128, NST]), bBln)
    ev_Cln = ld(Cln_bc, C_ln.ap().unsqueeze(0).broadcast_to([128, NST]), bCln)
    ev_base = ld(base_bc, base_cat.ap().unsqueeze(0).broadcast_to([128, 24]), bbase)
    ev_eye4 = ld(eye4_bc, eye4_in.ap().unsqueeze(0).broadcast_to([128, NST]), beye4)
    ev_eyef = ld(eyef_t[:, :], eye_f.ap(), [beyef])
    ev_dtpws = ld(dtpwT_t[:, :].rearrange("p (r d) -> p r d", d=DSL),
                  dtpwT_bf.ap().rearrange("r p d -> p r d"), [bdtpw],
                  after=[ev_dtpwc])
    ev_xpws = ld(xpw0.bitcast(BF16)[:, 0:NT_D * E].rearrange("p (cc e) -> p cc e", e=E),
                 xpwT_bf.ap().rearrange("(cc p) e -> p cc e", p=128), bxpw, after=[ev_xpwc])
    ev_wres_ld = ld(wrs0.bitcast(BF16)[:, 0:KH * 24].rearrange("p (cc e) -> p cc e", e=24),
                    wres_bf.ap().rearrange("(cc p) e -> p cc e", p=128), bwrs,
                    after=[ev_wresc])

    ev_g0 = ld(gt_sb[:, 0:1][0:1, :], gate_in.ap(), bgt)
    ev_g1 = SC(lambda s: s.activation(gt_sb[0:1, 1:2], gt_sb[0:1, 0:1], AF.Exp, scale=-1.0),
               reads=bgt, writes=bgt)
    ev_g2 = VE(lambda v: v.tensor_scalar_add(gt_sb[0:1, 2:3], gt_sb[0:1, 1:2], 1.0),
               reads=bgt, writes=bgt)
    ev_g3 = VE(lambda v: v.reciprocal(gt_sb[0:1, 3:4], gt_sb[0:1, 2:3]), reads=bgt, writes=bgt)
    ev_g4 = DMA(lambda s: s.dma_start(out=g_dr.ap(), in_=gt_sb[0:1, 3:4]),
                reads=bgt, writes=[bB("g_dr")])
    ev_g5 = DMA(lambda s: s.dma_start(out=g_bc, in_=g_dr.ap().broadcast_to([128, 1])),
                reads=[bB("g_dr")], writes=bg)
    ev_g6 = VE(lambda v: v.tensor_scalar(gm1_bc, g_bc, -1.0, 1.0, ALU.mult, ALU.add),
               reads=bg, writes=bgm1)

    ev_ct0 = ld(ct_f.rearrange("p (cc b) -> p cc b", b=B),
                cT_in.ap().rearrange("(cc p) b -> p cc b", p=128), bctf)
    ev_ct1 = SC(lambda s: s.activation(ct_f, ct_f, AF.Silu), reads=bctf, writes=bctf)
    ev_ct2 = VE(lambda v: v.tensor_copy(ct_bf, ct_f), reads=bctf, writes=bctb)
    ct_tiles = ct_bf.rearrange("p (cc b) -> p cc b", b=B)

    stg_ap, stg_b = phA.f(0, MODC)
    mod_stage = stg_ap[0:B, :]
    n_mch = (MODC + 255) // 256
    for i in range(n_mch):
        w0 = i * 256
        wid = min(256, MODC - w0)
        rhs_ap, rhs_b = phB.bf(0, KH * wid)
        rhs3 = rhs_ap.rearrange("p (cc w) -> p cc w", w=wid)
        ev_l = DMA(lambda s, w0=w0, wid=wid, r=rhs3: s.dma_start(
            out=r, in_=wmod_bf.ap()[:, w0:w0 + wid].rearrange("(cc p) w -> p cc w", p=128)),
            reads=[bB("wmod_bf")], writes=rhs_b, after=ev_wmod_c)

        def mmod(te, rhs3=rhs3, wid=wid):
            for k in range(KH):
                inst = te.matmul(pb[0][0:B, 0:wid], ct_tiles[:, k, :], rhs3[:, k, :],
                                 start=(k == 0), stop=(k == KH - 1))
            return inst
        ev_m = TE(mmod, reads=rhs_b + bctb, writes=[bpb[0]], after=[ev_ct2, ev_l])
        VE(lambda v, w0=w0, wid=wid: v.tensor_copy(mod_stage[:, w0:w0 + wid], pb[0][0:B, 0:wid]),
           reads=[bpb[0]], writes=stg_b, after=[ev_m])
    for half in range(2):
        for bb in range(2):
            b_src = half * 2 + bb
            for rr in range(NC // 2):
                DMA(lambda s, half=half, bb=bb, b_src=b_src, rr=rr: s.dma_start(
                    out=mod_part.ap()[bb * (NC // 2) + rr, half, :].unsqueeze(0),
                    in_=mod_stage[b_src:b_src + 1, :]),
                    reads=stg_b, writes=[bB("mod_part")])
    ev_a2a_mod = CC(lambda g: g.collective_compute(
        "AllToAll", ALU.bypass, replica_groups=[list(range(NC))],
        ins=[mod_part.ap().opt()], outs=[mod_my.ap().opt()]),
        reads=[bB("mod_part")], writes=[bB("mod_my")])
    def emit_ag_wout():
        return CC(lambda g: g.collective_compute(
            "AllGather", ALU.bypass, replica_groups=[list(range(NC))],
            ins=[wout_bf.ap().opt()], outs=[wout_all.ap().opt()]),
            reads=[bB("wout_bf")], writes=[bB("wout_all")], after=ev_wout_c)

    mm_flat = mod_my.ap().rearrange("j r m -> (j r m)")

    def bcast_mod(tile, lo_col, ph, vec, lo, hi):
        dst_ap, dst_b = tile.f(lo_col, lo_col + (hi - lo))
        evs = None
        coff = 0
        pos = vec * H + lo
        end = vec * H + hi
        while pos < end:
            j = pos // MODC
            ib = pos % MODC
            take = min(MODC - ib, end - pos)
            src = mm_flat[j * (2 * MODC) + ph * MODC + ib:
                          j * (2 * MODC) + ph * MODC + ib + take]
            evs = DMA(lambda s, d=dst_ap[:, coff:coff + take],
                      i=src.unsqueeze(0).broadcast_to([128, take]): s.dma_start(out=d, in_=i),
                      reads=[bB("mod_my")], writes=dst_b)
            coff += take
            pos += take
        return evs

    for ph in range(2):
        ev_ca = bcast_mod(phC, 0, ph, 1, 0, H)
        ca_ap, ca_b = phC.f(0, H)
        VE(lambda v: v.tensor_scalar_add(ca_ap, ca_ap, 1.0), reads=ca_b, writes=ca_b)
        ev_sa = bcast_mod(phD, 0, ph, 0, 0, H)
        sa_ap, sa_b = phD.f(0, H)
        for tt in range(NT_Q):
            xt_ap, xt_b = phA.f(0, H)
            DMA(lambda s, ph=ph, tt=tt, d=xt_ap: s.dma_start(
                out=d, in_=x_my.ap()[ph, tt * 128:(tt + 1) * 128, :]), writes=xt_b)
            VE(lambda v, i=xt_ap: v.reduce_sum(s1_t, i, AX.X), reads=xt_b, writes=bs1)
            dum_ap, dum_b = phE.f(0, H)
            SC(lambda s, i=xt_ap, d=dum_ap: s.activation(d, i, AF.Square, accum_out=s2_t),
               reads=xt_b, writes=bs2 + dum_b)
            VE(lambda v: v.tensor_scalar_mul(mu_t, s1_t, 1.0 / H), reads=bs1, writes=bmu)
            VE(lambda v: v.tensor_tensor(var_t, mu_t, mu_t, ALU.mult), reads=bmu, writes=bvar)
            VE(lambda v: v.tensor_scalar(junk_sb[:, 0:1], s2_t, 1.0 / H, None, ALU.mult),
               reads=bs2, writes=bjunk)
            VE(lambda v: v.tensor_sub(var_t, junk_sb[:, 0:1], var_t),
               reads=bjunk + bvar, writes=bvar)
            VE(lambda v: v.tensor_scalar_add(var_t, var_t, c["EPS_LN"]),
               reads=bvar, writes=bvar)
            SC(lambda s: s.activation(rstd_t, var_t, AF.Ln),
               reads=bvar, writes=brstd)
            SC(lambda s: s.activation(rstd_t, rstd_t, AF.Exp, scale=-0.5),
               reads=brstd, writes=brstd)
            hn1_ap, hn1_b = phB.f(0, H)
            VE(lambda v, i=xt_ap, o=hn1_ap: v.tensor_scalar(o, i, mu_t, rstd_t,
                                                            ALU.subtract, ALU.mult),
               reads=xt_b + bmu + brstd, writes=hn1_b)
            t1_ap, t1_b = phE.f(0, H)
            VE(lambda v, i=hn1_ap, o=t1_ap, cav=ca_ap: v.tensor_tensor(o, i, cav, ALU.mult),
               reads=hn1_b + ca_b, writes=t1_b)
            h_ap, h_b = phB2.f(0, H)
            VE(lambda v, i=t1_ap, o=h_ap, sav=sa_ap: v.tensor_tensor(o, i, sav, ALU.add),
               reads=t1_b + sa_b, writes=h_b, after=[ev_sa])
            hstg_ap, hstg_b = phG.bf(0, H)
            hstg3 = hstg_ap.rearrange("p (cc q) -> p cc q", q=128)
            for kc in range(KH):
                ev_tp = TE(lambda te, kc=kc, i=h_ap: te.transpose(
                    pb[7][:, 0:128], i[:, kc * 128:(kc + 1) * 128], eyef_t[:, :]),
                    reads=h_b + [beyef], writes=[bpb[7]], after=[ev_eyef])
                SC(lambda s, kc=kc, o=hstg3: s.copy(o[:, kc, :], pb[7][:, 0:128]),
                   reads=[bpb[7]], writes=hstg_b, after=[ev_tp])
            DMA(lambda s, ph=ph, tt=tt: s.dma_start(
                out=hT_my[ph].ap()[:, tt * 128:(tt + 1) * 128]
                .rearrange("(cc p) q -> p cc q", p=128), in_=hstg3),
                reads=hstg_b, writes=[bB(("hT_my", ph))])
    ev_ag_hT = [CC(lambda g, p=p: g.collective_compute(
        "AllGather", ALU.bypass, replica_groups=[list(range(NC))],
        ins=[hT_my[p].ap().opt()], outs=[hT_all[p].ap().opt()]),
        reads=[bB(("hT_my", p))], writes=[bB(("hT_all", p))]) for p in range(2)]

    def h_block(b, t0):
        return 4 * (b % 2) + (t0 // QT), t0 % QT

    def emit_P2(b):
        for tc in range(NTC):
            t0 = tc * TC
            rr, cc0 = h_block(b, t0)
            rhs_ap, rhs_b = phA.bf(0, KH * TC)
            rhs3 = rhs_ap.rearrange("p (cc w) -> p cc w", w=TC)
            ev_rhs = DMA(lambda s, rr=rr, cc0=cc0, ph=b // 2, r=rhs3: s.dma_start(
                out=r, in_=hT_all[ph].ap()[rr, :, cc0:cc0 + TC]
                .rearrange("(cc p) w -> p cc w", p=128)),
                reads=[bB(("hT_all", b // 2))], writes=rhs_b, after=[ev_ag_hT[b // 2]])
            for ctp in range(NCP):
                wtile = phB if ctp % 2 == 0 else phB2
                wt_ap, wt_b = wtile.bf(0, KH * 256)
                wt3 = wt_ap.rearrange("p (cc w) -> p cc w", w=256)
                ev_wt = DMA(lambda s, ctp=ctp, w=wt3: s.dma_start(
                    out=w, in_=win_bf.ap()[ctp].rearrange("cc p w -> p cc w")),
                    reads=[bB(("win_bf", ctp))], writes=wt_b, after=[ev_win[ctp]])
                for hh in range(2):
                    ct = ctp * 2 + hh
                    bank = ct % 6

                    def mmip(te, wt3=wt3, hh=hh, bank=bank, rhs3=rhs3):
                        for k in range(KH):
                            inst = te.matmul(pb[bank][:, 0:TC], wt3[:, k, hh * 128:(hh + 1) * 128],
                                             rhs3[:, k, :], start=(k == 0), stop=(k == KH - 1))
                        return inst
                    ev_mm = TE(mmip, reads=wt_b + rhs_b, writes=[bpb[bank]],
                               after=[ev_wt, ev_rhs])
                    stg = (ev0_sb if ct % 2 == 0 else ev1_sb)[:, 0:TC]
                    stg_bu = bev0 if ct % 2 == 0 else bev1
                    ev_ev = SC(lambda s, bank=bank, stg=stg: s.copy(stg, pb[bank][:, 0:TC]),
                               reads=[bpb[bank]], writes=stg_bu, after=[ev_mm])
                    dst = x_sp if ct < NT_D else z_sp
                    dtile = ct if ct < NT_D else ct - NT_D
                    GDMA(lambda s, dst=dst, b=b, dtile=dtile, t0=t0, stg=stg: s.dma_start(
                        out=dst.ap()[b, dtile * 128:(dtile + 1) * 128, t0:t0 + TC], in_=stg),
                        reads=stg_bu, writes=[bB((dst.name, b, dtile, tc))], after=[ev_ev])

        for dtile in range(NT_D):
            for th in range(2):
                t0 = th * THL
                xp_ap, xp_b = phG.f(0, THL + KCV - 1)
                deps = [bB(("x_sp", b, dtile, tci))
                        for tci in range(max(0, (t0 - KCV + 1)) // TC, (t0 + THL) // TC)]
                if th == 0:
                    ev_z = VE(lambda v: v.memset(xp_ap[:, 0:KCV - 1], 0.0), writes=xp_b)
                    DMA(lambda s, b=b, dtile=dtile: s.dma_start(
                        out=xp_ap[:, KCV - 1:], in_=x_sp.ap()[b, dtile * 128:(dtile + 1) * 128, 0:THL]),
                        reads=deps, writes=xp_b)
                else:
                    DMA(lambda s, b=b, dtile=dtile, t0=t0: s.dma_start(
                        out=xp_ap, in_=x_sp.ap()[b, dtile * 128:(dtile + 1) * 128,
                                                 t0 - (KCV - 1):t0 + THL]),
                        reads=deps, writes=xp_b)
                cv_ap, cv_b = phG.f(1536, 1536 + THL)
                VE(lambda v, dtile=dtile: v.tensor_scalar(
                    cv_ap, xp_ap[:, 0:THL], cvw_sb[:, dtile, 0:1], cvb_sb[:, dtile:dtile + 1],
                    ALU.mult, ALU.add),
                    reads=xp_b + bcvw + bcvb, writes=cv_b, after=[ev_cvw, ev_cvb])
                for k in range(1, KCV):
                    VE(lambda v, dtile=dtile, k=k: v.scalar_tensor_tensor(
                        cv_ap, xp_ap[:, k:k + THL], cvw_sb[:, dtile, k:k + 1], cv_ap,
                        ALU.mult, ALU.add),
                        reads=xp_b + bcvw + cv_b, writes=cv_b)
                xcf_ap, xcf_b = phG.f(2560, 2560 + THL)
                SC(lambda s: s.activation(xcf_ap, cv_ap, AF.Silu), reads=cv_b, writes=xcf_b)
                xcb_ap, xcb_b = phG.bf(3584, THL)
                VE(lambda v: v.tensor_copy(xcb_ap, xcf_ap), reads=xcf_b, writes=xcb_b)
                GDMA(lambda s, b=b, dtile=dtile, t0=t0: s.dma_start(
                    out=xc_sp.ap()[b, dtile * 128:(dtile + 1) * 128, t0:t0 + THL], in_=xcf_ap),
                    reads=xcf_b, writes=[bB(("xc_sp", b, dtile, th))])
                GDMA(lambda s, b=b, dtile=dtile, t0=t0: s.dma_start(
                    out=xcbf_sp.ap()[b, dtile * 128:(dtile + 1) * 128, t0:t0 + THL], in_=xcb_ap),
                    reads=xcb_b, writes=[bB(("xcbf_sp", b, dtile, th))])

        for tp in range(T // 256):
            ev_last = []
            for dtile in range(NT_D):
                xt = (xcb_t0 if dtile % 2 == 0 else xcb_t1).bitcast(BF16)[:, 0:256]
                xt_bu = bxcb0 if dtile % 2 == 0 else bxcb1
                ev_lx = DMA(lambda s, b=b, dtile=dtile, tp=tp, xt=xt: s.dma_start(
                    out=xt, in_=xcbf_sp.ap()[b, dtile * 128:(dtile + 1) * 128,
                                             tp * 256:(tp + 1) * 256]),
                    reads=[bB(("xcbf_sp", b, dtile, (tp * 256) // THL))], writes=xt_bu)

                def mmxp(te, xt=xt, dtile=dtile):
                    for tt in range(2):
                        inst = te.matmul(pb[4 + tt][:, 0:E], xt[:, tt * 128:(tt + 1) * 128],
                                         xpwT_sb[:, dtile, :],
                                         start=(dtile == 0), stop=(dtile == NT_D - 1))
                    return inst
                ev_mm = TE(mmxp, reads=xt_bu + bxpw, writes=[bpb[4], bpb[5]],
                           after=[ev_lx, ev_xpws])
                ev_last = [ev_mm]
            for tt in range(2):
                stg = (ev0_sb if tt == 0 else ev1_sb)[:, 0:E]
                stg_bu = bev0 if tt == 0 else bev1
                VE(lambda v, tt=tt, stg=stg: v.tensor_copy(stg, pb[4 + tt][:, 0:E]),
                   reads=[bpb[4 + tt]], writes=stg_bu, after=ev_last)
                GDMA(lambda s, b=b, tp=tp, tt=tt, stg=stg: s.dma_start(
                    out=ssm_in[b].ap()[tp * 256 + tt * 128:tp * 256 + (tt + 1) * 128, :], in_=stg),
                    reads=stg_bu, writes=[bB(("ssm_in", b))])
        ev_ar = CC(lambda g, b=b: g.collective_compute(
            "AllReduce", ALU.add, replica_groups=[list(range(NC))],
            ins=[ssm_in[b].ap().opt()], outs=[ssm_out[b].ap().opt()]),
            reads=[bB(("ssm_in", b))], writes=[bB(("ssm_out", b))])

        dtrT_ap, dtrT_b = phJ.bf(0, RT * T)
        dtrT = dtrT_ap.rearrange("p (r t) -> p r t", t=T)
        for tt in range(T // 128):
            st_ap, st_b = phH.f((tt % 2) * 512, (tt % 2) * 512 + E)
            DMA(lambda s, b=b, tt=tt, st=st_ap: s.dma_start(
                out=st, in_=ssm_out[b].ap()[tt * 128:(tt + 1) * 128, :]),
                reads=[bB(("ssm_out", b))], writes=st_b, after=[ev_ar])
            for (off, w, ri) in ((0, R, 0), (R, NST, 1), (R + NST, NST, 2)):
                SC(lambda s, st=st_ap, off=off, w=w: s.activation(
                    junk_sb[:, 0:w], st[:, off:off + w], AF.Square, accum_out=s2_t),
                    reads=st_b, writes=bs2 + bjunk)
                VE(lambda v, w=w, ri=ri: v.tensor_scalar(rho_t[:, ri:ri + 1], s2_t,
                                                          1.0 / w, c["EPS_RMS_A"],
                                                          ALU.mult, ALU.add),
                   reads=bs2, writes=brho)
                SC(lambda s, ri=ri: s.activation(rho_t[:, ri:ri + 1], rho_t[:, ri:ri + 1], AF.Ln),
                   reads=brho, writes=brho)
                SC(lambda s, ri=ri: s.activation(rho_t[:, ri:ri + 1], rho_t[:, ri:ri + 1],
                                                 AF.Exp, scale=-0.5),
                   reads=brho, writes=brho)
            dtr_ap, dtr_b = phH.f(1024, 1024 + R)
            VE(lambda v, st=st_ap: v.tensor_scalar(dtr_ap, st[:, 0:R], rho_t[:, 0:1], None, ALU.mult),
               reads=st_b + brho, writes=dtr_b)
            VE(lambda v: v.tensor_tensor(dtr_ap, dtr_ap, dtln_bc, ALU.mult),
               reads=dtr_b + bdtln, writes=dtr_b, after=[ev_dtln])
            bc_ap, bc_b = phH.f(1536, 1536 + 2 * NST)
            brow, crow = bc_ap[:, 0:NST], bc_ap[:, NST:2 * NST]
            VE(lambda v, st=st_ap: v.tensor_scalar(brow, st[:, R:R + NST], rho_t[:, 1:2],
                                                   None, ALU.mult),
               reads=st_b + brho, writes=bc_b)
            VE(lambda v: v.tensor_tensor(brow, brow, Bln_bc, ALU.mult),
               reads=bc_b + bBln, writes=bc_b, after=[ev_Bln])
            VE(lambda v, st=st_ap: v.tensor_scalar(crow, st[:, R + NST:R + 2 * NST], rho_t[:, 2:3],
                                                   None, ALU.mult),
               reads=st_b + brho, writes=bc_b)
            VE(lambda v: v.tensor_tensor(crow, crow, Cln_bc, ALU.mult),
               reads=bc_b + bCln, writes=bc_b, after=[ev_Cln])
            for rc in range(RT):
                w = min(128, R - rc * 128)
                ev_tp = TE(lambda te, rc=rc, w=w: te.transpose(
                    pb[7][0:w, 0:128], dtr_ap[:, rc * 128:rc * 128 + w], eyef_t[:, :]),
                    reads=dtr_b + [beyef], writes=[bpb[7]], after=[ev_eyef])
                SC(lambda s, rc=rc, tt=tt, w=w: s.copy(dtrT[0:w, rc, tt * 128:(tt + 1) * 128],
                                                       pb[7][0:w, 0:128]),
                   reads=[bpb[7]], writes=dtrT_b, after=[ev_tp])
            for (row, dst) in ((brow, Bt_i), (crow, Ct_i)):
                ev_tp = TE(lambda te, row=row: te.transpose(pb[7][0:NST, 0:128], row, eyef_t[:, :]),
                           reads=bc_b + [beyef], writes=[bpb[7]], after=[ev_eyef])
                VE(lambda v: v.tensor_copy(btct[0:NST, 0:128], pb[7][0:NST, 0:128]),
                   reads=[bpb[7]], writes=bbtct, after=[ev_tp])
                GDMA(lambda s, dst=dst, b=b, tt=tt: s.dma_start(
                    out=dst.ap()[b, :, tt * 128:(tt + 1) * 128], in_=btct[0:NST, 0:128]),
                    reads=bbtct, writes=[bB((dst.name, b))])

        for dtile in range(NT_D):
            for th in range(2):
                dtp_ap, dtp_b = phB.f(0, THL)
                for sub in range(THL // TC):
                    tcg = th * (THL // TC) + sub

                    def mmdt(te, dtile=dtile, tcg=tcg):
                        for rc in range(RT):
                            w = min(128, R - rc * 128)
                            inst = te.matmul(
                                pb[6][:, 0:TC],
                                dtpwT_t[0:w, rc * DSL + dtile * 128:rc * DSL + (dtile + 1) * 128],
                                dtrT[0:w, rc, tcg * TC:(tcg + 1) * TC],
                                start=(rc == 0), stop=(rc == RT - 1))
                        return inst
                    ev_mm = TE(mmdt, reads=dtrT_b + [bdtpw], writes=[bpb[6]], after=[ev_dtpws])
                    VE(lambda v, sub=sub, dtile=dtile: v.tensor_scalar(
                        dtp_ap[:, sub * TC:(sub + 1) * TC], pb[6][:, 0:TC],
                        dtb_sb[:, dtile:dtile + 1], None, ALU.add),
                        reads=[bpb[6]] + bdtb, writes=dtp_b, after=[ev_mm, ev_dtb])
                sp0_ap, sp0_b = phB.f(1024, 1024 + THL)
                sp1_ap, sp1_b = phB.f(2048, 2048 + THL)
                VE(lambda v: v.tensor_scalar_max(sp0_ap, dtp_ap, 0.0), reads=dtp_b, writes=sp0_b)
                SC(lambda s: s.activation(sp1_ap, dtp_ap, AF.Abs), reads=dtp_b, writes=sp1_b)
                SC(lambda s: s.activation(sp1_ap, sp1_ap, AF.Exp, scale=-1.0),
                   reads=sp1_b, writes=sp1_b)
                VE(lambda v: v.tensor_scalar_add(sp1_ap, sp1_ap, 1.0),
                   reads=sp1_b, writes=sp1_b)
                SC(lambda s: s.activation(sp1_ap, sp1_ap, AF.Ln),
                   reads=sp1_b, writes=sp1_b)
                VE(lambda v: v.tensor_add(dtp_ap, sp0_ap, sp1_ap),
                   reads=sp0_b + sp1_b, writes=dtp_b)
                GDMA(lambda s, b=b, dtile=dtile, th=th: s.dma_start(
                    out=dt_sp.ap()[b, dtile * 128:(dtile + 1) * 128, th * THL:(th + 1) * THL],
                    in_=dtp_ap),
                    reads=dtp_b, writes=[bB(("dt_sp", b, dtile, th))])

    def emit_P3(b):
        for th in range(2):
            for blk in range(NBLK):
                dts = [blk * DTBLK + j for j in range(DTBLK)]
                for j, dtile in enumerate(dts):
                    dtt_ap, dtt_b = phC.f(j * THL, (j + 1) * THL)
                    SDMA(lambda s, b=b, dtile=dtile, th=th, d=dtt_ap: s.dma_start(
                        out=d, in_=dt_sp.ap()[b, dtile * 128:(dtile + 1) * 128,
                                              th * THL:(th + 1) * THL]),
                        reads=[bB(("dt_sp", b, dtile, th))], writes=dtt_b)
                    xct_ap, xct_b = phF.f(0, THL)
                    SDMA(lambda s, b=b, dtile=dtile, th=th, d=xct_ap: s.dma_start(
                        out=d, in_=xc_sp.ap()[b, dtile * 128:(dtile + 1) * 128,
                                              th * THL:(th + 1) * THL]),
                        reads=[bB(("xc_sp", b, dtile, th))], writes=xct_b)
                    w_ap, w_b = phD.f(j * THL, (j + 1) * THL)
                    VE(lambda v, d=dtt_ap, x=xct_ap, w=w_ap: v.tensor_tensor(w, d, x, ALU.mult),
                       reads=dtt_b + xct_b, writes=w_b)
                pend_acc = []

                def flush_acc():
                    while pend_acc:
                        fnp, rd, wr = pend_acc.pop(0)
                        VE(fnp, reads=rd, writes=wr)
                pipes = (phF, phK, phM)
                itc = [0]
                for n in range(NST):
                    bct = phI if n % 2 == 0 else phL
                    bcB_ap, bcB_b = bct.f(0, THL)
                    bcC_ap, bcC_b = bct.f(1024, 1024 + THL)
                    SDMA(lambda s, b=b, n=n, th=th, d=bcB_ap: s.dma_start(
                        out=d, in_=Bt_i.ap()[b, n, th * THL:(th + 1) * THL]
                        .unsqueeze(0).broadcast_to([128, THL])),
                        reads=[bB(("Bt_i", b))], writes=bcB_b)
                    SDMA(lambda s, b=b, n=n, th=th, d=bcC_ap: s.dma_start(
                        out=d, in_=Ct_i.ap()[b, n, th * THL:(th + 1) * THL]
                        .unsqueeze(0).broadcast_to([128, THL])),
                        reads=[bB(("Ct_i", b))], writes=bcC_b)
                    for j, dtile in enumerate(dts):
                        dtt_ap, dtt_b = phC.f(j * THL, (j + 1) * THL)
                        w_ap, w_b = phD.f(j * THL, (j + 1) * THL)
                        pipet = pipes[itc[0] % 3]
                        itc[0] += 1
                        pA_ap, pA_b = pipet.f(0, THL)
                        pB_ap, pB_b = pipet.f(1024, 1024 + THL)
                        pP_ap, pP_b = pipet.f(2048, 2048 + THL)
                        SC(lambda s, d=dtt_ap, dtile=dtile, n=n, o=pA_ap: s.activation(
                            o, d, AF.Exp, scale=A_sb[:, dtile * NST + n:dtile * NST + n + 1]),
                            reads=dtt_b + bA_sb, writes=pA_b, after=[ev_A])
                        VE(lambda v, w=w_ap, o=pB_ap, bc=bcB_ap: v.tensor_tensor(o, w, bc, ALU.mult),
                           reads=w_b + bcB_b, writes=pB_b)
                        if th == 0:
                            ini = 0.0
                            rd_ini = []
                        else:
                            ini = sstate[:, dtile * NST + n:dtile * NST + n + 1]
                            rd_ini = bsst
                        VE(lambda v, a=pA_ap, x=pB_ap, ini=ini: v.tensor_tensor_scan(
                            x, a, x, ini, ALU.mult, ALU.add),
                            reads=pA_b + pB_b + rd_ini, writes=pB_b)
                        if th == 0:
                            VE(lambda v, dtile=dtile, n=n, x=pB_ap: v.tensor_copy(
                                sstate[:, dtile * NST + n:dtile * NST + n + 1],
                                x[:, THL - 1:THL]),
                                reads=pB_b, writes=bsst)
                        GP(lambda g, x=pB_ap, o=pP_ap, bc=bcC_ap: g.tensor_tensor(o, x, bc, ALU.mult),
                           reads=pB_b + bcC_b, writes=pP_b)
                        ya_ap, ya_b = phE.f(j * THL, (j + 1) * THL)
                        if n == 0:
                            fnp = (lambda v, ya=ya_ap, o=pP_ap: v.tensor_copy(ya, o))
                        else:
                            fnp = (lambda v, ya=ya_ap, o=pP_ap: v.tensor_add(ya, ya, o))
                        pend_acc.append((fnp, pP_b + (ya_b if n > 0 else []), ya_b))
                        if len(pend_acc) > 1:
                            fnp, rd, wr = pend_acc.pop(0)
                            VE(fnp, reads=rd, writes=wr)
                flush_acc()
                for j, dtile in enumerate(dts):
                    ya_ap, ya_b = phE.f(j * THL, (j + 1) * THL)
                    xct_ap, xct_b = phF.f(0, THL)
                    zt_ap, zt_b = phF.f(2048, 2048 + THL)
                    SDMA(lambda s, b=b, dtile=dtile, th=th, d=xct_ap: s.dma_start(
                        out=d, in_=xc_sp.ap()[b, dtile * 128:(dtile + 1) * 128,
                                              th * THL:(th + 1) * THL]),
                        reads=[bB(("xc_sp", b, dtile, th))], writes=xct_b)
                    SDMA(lambda s, b=b, dtile=dtile, th=th, d=zt_ap: s.dma_start(
                        out=d, in_=z_sp.ap()[b, dtile * 128:(dtile + 1) * 128,
                                             th * THL:(th + 1) * THL]),
                        reads=[bB(("z_sp", b, dtile, tci))
                               for tci in range((th * THL) // TC, ((th + 1) * THL) // TC)],
                        writes=zt_b)
                    tf_ap, tf_b = phF.f(1024, 1024 + THL)
                    VE(lambda v, x=xct_ap, dtile=dtile, ya=ya_ap, o=tf_ap: v.scalar_tensor_tensor(
                        o, x, D_sb[:, dtile:dtile + 1], ya, ALU.mult, ALU.add),
                        reads=xct_b + ya_b + bD_sb, writes=tf_b, after=[ev_D])
                    SC(lambda s, z=zt_ap: s.activation(z, z, AF.Silu), reads=zt_b, writes=zt_b)
                    yb_ap, yb_b = phF.bf(0, THL)
                    VE(lambda v, t=tf_ap, z=zt_ap, o=yb_ap: v.tensor_tensor(o, t, z, ALU.mult),
                       reads=tf_b + zt_b, writes=yb_b)
                    pset = 0 if b < 2 else 1
                    for qq in range(NQB):
                        qglob = th * NQB + qq
                        shard = 4 * (b % 2) + qglob
                        DMA(lambda s, pset=pset, shard=shard, dtile=dtile, qq=qq, y=yb_ap:
                            s.dma_start(out=y_in[pset].ap()[shard, dtile * 128:(dtile + 1) * 128, :],
                                        in_=y[:, qq * QT:(qq + 1) * QT]),
                            reads=yb_b, writes=[bB(("y_in", pset))])
    def emit_A2A(pset):
        CC(lambda g, pset=pset: g.collective_compute(
            "AllToAll", ALU.bypass, replica_groups=[list(range(NC))],
            ins=[y_in[pset].ap().opt()], outs=[y_my[pset].ap().opt()]),
            reads=[bB(("y_in", pset))], writes=[bB(("y_my", pset))])


    emit_P2(0)
    emit_P2(1)
    emit_P3(0)
    ev_ag_wout = emit_ag_wout()
    emit_P2(2)
    emit_P3(1)
    emit_A2A(0)
    emit_P2(3)
    emit_P3(2)
    emit_P3(3)
    emit_A2A(1)

    WOW = min(2048, H)
    NHH = H // WOW
    NHC = WOW // 512

    def clamp_ga(tile, bufs, ev_after):
        evs = ev_after
        CW = min(1024, H)
        for hf in range(H // CW):
            o0 = hf * CW
            m_ap, m_b = phI.f(0, CW)
            hi_ap, hi_b = phI.f(1024, 1024 + CW)
            lo_ap, lo_b = phJ.f(0, CW)
            g_ap = tile.t[:, o0:o0 + CW]
            VE(lambda v, g=g_ap, m=m_ap: v.tensor_single_scalar(m, g, 0.0, ALU.is_ge),
               reads=bufs, writes=m_b, after=[evs])
            VE(lambda v, g=g_ap, hi=hi_ap: v.tensor_scalar_max(hi, g, 0.25),
               reads=bufs, writes=hi_b)
            VE(lambda v, g=g_ap, lo=lo_ap: v.tensor_scalar_min(lo, g, -0.25),
               reads=bufs, writes=lo_b)
            VE(lambda v, hi=hi_ap, lo=lo_ap: v.tensor_sub(hi, hi, lo),
               reads=hi_b + lo_b, writes=hi_b)
            VE(lambda v, hi=hi_ap, m=m_ap: v.tensor_mul(hi, hi, m),
               reads=hi_b + m_b, writes=hi_b)
            evs = VE(lambda v, g=g_ap, lo=lo_ap, hi=hi_ap: v.tensor_add(g, lo, hi),
                     reads=lo_b + hi_b, writes=bufs)
        return evs

    for ph in range(2):
        ev_ga0 = bcast_mod(phG, 0, ph, 2, 0, H)
        ga_ap, ga_b = phG.f(0, H)
        ev_ga = clamp_ga(phG, ga_b, ev_ga0)
        for tpair in range(NT_Q // 2):
            xin_t = [phC, phD]
            for tt in range(2):
                ttg = tpair * 2 + tt
                xi_ap, xi_b = xin_t[tt].f(0, H)
                SDMA(lambda s, ph=ph, ttg=ttg, d=xi_ap: s.dma_start(
                    out=d, in_=x_my.ap()[ph, ttg * 128:(ttg + 1) * 128, :]), writes=xi_b)
            for hh in range(NHH):
                ev_mmo = None
                yts = [xcb_t0, xcb_t1, xcb_t2, xcb_t3]
                ytb = [bxcb0, bxcb1, bxcb2, bxcb3]
                for k in range(KD):
                    wtile = phB if k % 2 == 0 else phB2
                    wlo = 0 if (k // 2) % 2 == 0 else 2048
                    wo_ap, wo_b = wtile.bf(wlo, WOW)
                    ev_wo = (SDMA if k % 2 == 0 else DMA)(lambda s, k=k, hh=hh, w=wo_ap: s.dma_start(
                        out=w, in_=wout_all.ap()[k // NT_D, (k % NT_D) * 128:(k % NT_D) * 128 + 128,
                                                 hh * WOW:(hh + 1) * WOW]),
                        reads=[bB("wout_all")], writes=wo_b, after=[ev_ag_wout])
                    yt = yts[k % 4].bitcast(BF16)[:, 0:256]
                    yt_bu = ytb[k % 4]
                    ev_yt = (DMA if k % 2 == 0 else SDMA)(lambda s, ph=ph, k=k, tpair=tpair, y=yt: s.dma_start(
                        out=y, in_=y_my[ph].ap()[k // NT_D, (k % NT_D) * 128:(k % NT_D) * 128 + 128,
                                                 tpair * 256:(tpair + 1) * 256]),
                        reads=[bB(("y_my", ph))], writes=yt_bu)

                    def mmop(te, yt=yt, wo=wo_ap, k=k):
                        for tt in range(2):
                            for hc in range(NHC):
                                inst = te.matmul(pb[tt * NHC + hc][:, 0:512],
                                                 yt[:, tt * 128:(tt + 1) * 128],
                                                 wo[:, hc * 512:(hc + 1) * 512],
                                                 start=(k == 0), stop=(k == KD - 1))
                        return inst
                    ev_mmo = TE(mmop, reads=wo_b + yt_bu,
                                writes=[bpb[tt * NHC + hc] for tt in range(2) for hc in range(NHC)],
                                after=[ev_wo, ev_yt])
                for tt in range(2):
                    x2t = phA if tt == 0 else phE
                    x2_ap, x2_b = x2t.f(0, H)
                    xi_ap, xi_b = xin_t[tt].f(0, H)
                    for hc in range(NHC):
                        co = hh * WOW + hc * 512
                        bank = tt * NHC + hc
                        VE(lambda v, bank=bank, co=co, o=x2_ap: v.tensor_tensor(
                            o[:, co:co + 512], pb[bank][:, 0:512], ga_ap[:, co:co + 512], ALU.mult),
                            reads=[bpb[bank]] + ga_b, writes=x2_b, after=[ev_ga, ev_mmo])
                        VE(lambda v, co=co, o=x2_ap, xi=xi_ap: v.tensor_add(
                            o[:, co:co + 512], o[:, co:co + 512], xi[:, co:co + 512]),
                            reads=xi_b + x2_b, writes=x2_b)
            for tt in range(2):
                ttg = tpair * 2 + tt
                x2t = phA if tt == 0 else phE
                x2_ap, x2_b = x2t.f(0, H)
                GDMA(lambda s, ph=ph, ttg=ttg, i=x2_ap: s.dma_start(
                    out=x2_sp.ap()[ph, ttg * 128:(ttg + 1) * 128, :], in_=i),
                    reads=x2_b, writes=[bB(("x2_sp", ph, ttg))])
                VE(lambda v, i=x2_ap: v.reduce_sum(s1_t, i, AX.X), reads=x2_b, writes=bs1)
                dum_ap, dum_b = phD.f(0, H)
                SC(lambda s, i=x2_ap, d=dum_ap: s.activation(d, i, AF.Square, accum_out=s2_t),
                   reads=x2_b, writes=bs2 + dum_b)
                VE(lambda v: v.tensor_scalar_mul(mu_t, s1_t, 1.0 / H), reads=bs1, writes=bmu)
                VE(lambda v: v.tensor_tensor(var_t, mu_t, mu_t, ALU.mult), reads=bmu, writes=bvar)
                VE(lambda v: v.tensor_scalar(junk_sb[:, 0:1], s2_t, 1.0 / H, None, ALU.mult),
                   reads=bs2, writes=bjunk)
                VE(lambda v: v.tensor_sub(var_t, junk_sb[:, 0:1], var_t),
                   reads=bjunk + bvar, writes=bvar)
                VE(lambda v: v.tensor_scalar_add(var_t, var_t, c["EPS_LN"]),
                   reads=bvar, writes=bvar)
                SC(lambda s: s.activation(rstd_t, var_t, AF.Ln),
                   reads=bvar, writes=brstd)
                SC(lambda s: s.activation(rstd_t, rstd_t, AF.Exp, scale=-0.5),
                   reads=brstd, writes=brstd)
                hn_ap, hn_b = phD.f(0, H)
                VE(lambda v, i=x2_ap, o=hn_ap: v.tensor_scalar(o, i, mu_t, rstd_t,
                                                               ALU.subtract, ALU.mult),
                   reads=x2_b + bmu + brstd, writes=hn_b)
                h2_ap, h2_b = phC.f(0, H)
                for hf in range(2):
                    o0 = hf * (H // 2)
                    cb_ap, cb_b = phI.f(0, H // 2)
                    ev_cb = bcast_mod(phI, 0, ph, 4, o0, o0 + H // 2)
                    VE(lambda v, a=cb_ap: v.tensor_scalar_add(a, a, 1.0),
                       reads=cb_b, writes=cb_b, after=[ev_cb])
                    sb_ap, sb_b = phJ.f(0, H // 2)
                    ev_sb = bcast_mod(phJ, 0, ph, 3, o0, o0 + H // 2)
                    VE(lambda v, o0=o0, h2=h2_ap, hn=hn_ap, a=cb_ap: v.tensor_mul(
                        h2[:, o0:o0 + H // 2], hn[:, o0:o0 + H // 2], a),
                        reads=hn_b + cb_b, writes=h2_b)
                    VE(lambda v, o0=o0, h2=h2_ap, a=sb_ap: v.tensor_add(
                        h2[:, o0:o0 + H // 2], h2[:, o0:o0 + H // 2], a),
                        reads=sb_b + h2_b, writes=h2_b, after=[ev_sb])
                GDMA(lambda s, ph=ph, ttg=ttg, i=h2_ap: s.dma_start(
                    out=h2_sp.ap()[ph, ttg * 128:(ttg + 1) * 128, :], in_=i),
                    reads=h2_b, writes=[bB(("h2_sp", ph, ttg))])
                SC(lambda s, i=h2_ap, d=dum_ap: s.activation(d, i, AF.Square, accum_out=s2_t),
                   reads=h2_b, writes=bs2 + dum_b)
                VE(lambda v: v.tensor_scalar(rstd_t, s2_t, 1.0 / H, c["EPS_RMS_F"],
                                             ALU.mult, ALU.add),
                   reads=bs2, writes=brstd)
                SC(lambda s: s.activation(rstd_t, rstd_t, AF.Ln),
                   reads=brstd, writes=brstd)
                SC(lambda s: s.activation(rstd_t, rstd_t, AF.Exp, scale=-0.5),
                   reads=brstd, writes=brstd)
                xn_ap, xn_b = phD.f(0, H)
                VE(lambda v, i=h2_ap, o=xn_ap: v.tensor_scalar(o, i, rstd_t, None, ALU.mult),
                   reads=h2_b + brstd, writes=xn_b)
                xnT_ap, xnT_b = phB.bf(0, KH * 128)
                xnT = xnT_ap.rearrange("p (cc q) -> p cc q", q=128)
                for kc in range(KH):
                    ev_tp = TE(lambda te, kc=kc, i=xn_ap: te.transpose(
                        pb[7][:, 0:128], i[:, kc * 128:(kc + 1) * 128], eyef_t[:, :]),
                        reads=xn_b + [beyef], writes=[bpb[7]], after=[ev_eyef])
                    SC(lambda s, kc=kc: s.copy(xnT[:, kc, :], pb[7][:, 0:128]),
                       reads=[bpb[7]], writes=xnT_b, after=[ev_tp])

                def mmlg(te, xnT=xnT):
                    for kc in range(KH):
                        inst = te.matmul(pb[6][:, 0:24], xnT[:, kc, :], wres_tiles[:, kc, :],
                                         start=(kc == 0), stop=(kc == KH - 1))
                    return inst
                ev_lg = TE(mmlg, reads=xnT_b + bwrs, writes=[bpb[6]], after=[ev_wres_ld])
                gidx = ph * NT_Q + ttg
                VE(lambda v, gidx=gidx: v.tensor_tensor(
                    logits_sb.rearrange("p (g e) -> p g e", e=24)[:, gidx, :],
                    pb[6][:, 0:24], base_bc, ALU.add),
                    reads=[bpb[6]] + bbase, writes=blg, after=[ev_base, ev_lg])

    lg3 = logits_sb.rearrange("p (g e) -> p g e", e=24)
    Z4 = Z_sb.rearrange("p (g i j) -> p g i j", i=4, j=4)
    ZT4 = ZT_sb.rearrange("p (g j i) -> p g j i", j=4, i=4)
    u4 = u_sb.rearrange("p (g i) -> p g i", i=4)
    v4 = v_sb.rearrange("p (g j) -> p g j", j=4)
    m4 = mx_sb.rearrange("p (g i) -> p g i", i=4)
    s4 = snk_sb.rearrange("p (g i j) -> p g i j", i=4, j=4)
    VE(lambda v: v.tensor_scalar_mul(Z_sb.rearrange("p (g e) -> p g e", e=16),
                                     lg3[:, :, 0:16], 1.0 / TAU),
       reads=blg, writes=bZ)
    VE(lambda v: v.tensor_copy(ZT4, Z4.transpose([0, 1, 3, 2])),
       reads=bZ, writes=bZT)
    VE(lambda v: v.memset(u_sb, 0.0), writes=bu)
    VE(lambda v: v.memset(v_sb, 0.0), writes=bv)
    for it in range(ITERS):
        for (src_b, src_sb, dst_b, dst_sb, oth4) in (
                (bZ, Z_sb, bu, u_sb, v4), (bZT, ZT_sb, bv, v_sb, u4)):
            oth_b = bv if dst_sb is u_sb else bu
            VE(lambda v, s_=src_sb, o4=oth4: v.tensor_tensor(
                s4, s_.rearrange("p (g a b) -> p g a b", a=4, b=4),
                o4.unsqueeze(2).broadcast_to([128, G, 4, 4]), ALU.add),
                reads=src_b + oth_b, writes=bsnk)
            VE(lambda v: v.reduce_max(mx_sb, s4, AX.X), reads=bsnk, writes=bmx)
            VE(lambda v: v.tensor_tensor(s4, s4,
                                         m4.unsqueeze(3).broadcast_to([128, G, 4, 4]),
                                         ALU.subtract),
               reads=bsnk + bmx, writes=bsnk)
            SC(lambda s: s.activation(snk_sb, snk_sb, AF.Exp), reads=bsnk, writes=bsnk)
            VE(lambda v: v.reduce_sum(junk_sb[:, 0:G * 4], s4, AX.X),
               reads=bsnk, writes=bjunk)
            SC(lambda s: s.activation(junk_sb[:, 0:G * 4], junk_sb[:, 0:G * 4], AF.Ln),
               reads=bjunk, writes=bjunk)
            VE(lambda v: v.tensor_add(junk_sb[:, 0:G * 4], junk_sb[:, 0:G * 4], mx_sb),
               reads=bjunk + bmx, writes=bjunk)
            VE(lambda v, d=dst_sb: v.tensor_scalar_mul(d, junk_sb[:, 0:G * 4], -1.0),
               reads=bjunk, writes=dst_b)
    VE(lambda v: v.tensor_tensor(Hres_sb.rearrange("p (g a b) -> p g a b", a=4, b=4),
                                 Z4, u4.unsqueeze(3).broadcast_to([128, G, 4, 4]), ALU.add),
       reads=bZ + bu, writes=bHr)
    VE(lambda v: v.tensor_tensor(Hres_sb.rearrange("p (g a b) -> p g a b", a=4, b=4),
                                 Hres_sb.rearrange("p (g a b) -> p g a b", a=4, b=4),
                                 v4.unsqueeze(2).broadcast_to([128, G, 4, 4]), ALU.add),
       reads=bHr + bv, writes=bHr)
    SC(lambda s: s.activation(Hres_sb, Hres_sb, AF.Exp), reads=bHr, writes=bHr)
    VE(lambda v: v.tensor_scalar(Hres_sb, Hres_sb, g_bc[:, 0:1], None, ALU.mult),
       reads=bHr + bg, writes=bHr, after=[ev_g5])
    ev_hres = VE(lambda v: v.scalar_tensor_tensor(
        Hres_sb.rearrange("p (g e) -> p g e", e=NST),
        eye4_bc.unsqueeze(1).broadcast_to([128, G, NST]),
        gm1_bc[:, 0:1], Hres_sb.rearrange("p (g e) -> p g e", e=NST), ALU.mult, ALU.add),
        reads=bHr + beye4 + bgm1, writes=bHr, after=[ev_eye4, ev_g6])
    for (lo, dst_sb, dst_b) in ((16, Hpre_sb, bHp), (20, Hpost_sb, bHq)):
        s3 = snk_sb[:, 0:G * 4].rearrange("p (g e) -> p g e", e=4)
        VE(lambda v, lo=lo: v.tensor_copy(
            snk_sb[:, 0:G * 4].rearrange("p (g e) -> p g e", e=4), lg3[:, :, lo:lo + 4]),
           reads=blg, writes=bsnk)
        VE(lambda v, s3=s3: v.reduce_max(mx_sb[:, 0:G], s3, AX.X), reads=bsnk, writes=bmx)
        VE(lambda v: v.tensor_tensor(
            snk_sb[:, 0:G * 4].rearrange("p (g e) -> p g e", e=4),
            snk_sb[:, 0:G * 4].rearrange("p (g e) -> p g e", e=4),
            mx_sb[:, 0:G].rearrange("p (g o) -> p g o", o=1).broadcast_to([128, G, 4]),
            ALU.subtract),
           reads=bsnk + bmx, writes=bsnk)
        SC(lambda s: s.activation(snk_sb[:, 0:G * 4], snk_sb[:, 0:G * 4], AF.Exp),
           reads=bsnk, writes=bsnk)
        VE(lambda v, s3=s3: v.reduce_sum(t4_sb[:, 0:G], s3, AX.X), reads=bsnk, writes=bt4)
        VE(lambda v: v.reciprocal(t4_sb[:, G:2 * G], t4_sb[:, 0:G]), reads=bt4, writes=bt4)
        VE(lambda v, d=dst_sb: v.tensor_tensor(
            d.rearrange("p (g e) -> p g e", e=4),
            snk_sb[:, 0:G * 4].rearrange("p (g e) -> p g e", e=4),
            t4_sb[:, G:2 * G].rearrange("p (g o) -> p g o", o=1).broadcast_to([128, G, 4]),
            ALU.mult),
           reads=bsnk + bt4, writes=dst_b)

    Hr3 = Hres_sb.rearrange("p (g e) -> p g e", e=NST)
    Hp3 = Hpre_sb.rearrange("p (g e) -> p g e", e=4)
    Hq3 = Hpost_sb.rearrange("p (g e) -> p g e", e=4)
    for ph in range(2):
        ev_gb0 = bcast_mod(phG, 0, ph, 5, 0, H)
        gb_ap, gb_b = phG.f(0, H)
        ev_gb = clamp_ga(phG, gb_b, ev_gb0)
        for ttg in range(NT_Q):
            gidx = ph * NT_Q + ttg
            h2_ap, h2_b = phB.f(0, H)
            DMA(lambda s, ph=ph, ttg=ttg, d=h2_ap: s.dma_start(
                out=d, in_=h2_sp.ap()[ph, ttg * 128:(ttg + 1) * 128, :]),
                reads=[bB(("h2_sp", ph, ttg))], writes=h2_b)
            x2_ap, x2_b = phB2.f(0, H)
            DMA(lambda s, ph=ph, ttg=ttg, d=x2_ap: s.dma_start(
                out=d, in_=x2_sp.ap()[ph, ttg * 128:(ttg + 1) * 128, :]),
                reads=[bB(("x2_sp", ph, ttg))], writes=x2_b)
            xp0_ap, xp0_b = phD.f(2048, 2048 + CD)
            xp1_ap, xp1_b = phD.f(2048 + CD, 2048 + 2 * CD) if CD <= 1024 else phD.f(2048, 2048 + CD)
            VE(lambda v, gidx=gidx: v.tensor_scalar(xp0_ap, h2_ap[:, 0:CD],
                                                    Hp3[:, gidx, 0:1], None, ALU.mult),
               reads=h2_b + bHp, writes=xp0_b)
            cur_ap, cur_b, oth_ap, oth_b = xp0_ap, xp0_b, xp1_ap, xp1_b
            for n in range(1, NSTR):
                VE(lambda v, n=n, cu=cur_ap, ot=oth_ap, gidx=gidx: v.scalar_tensor_tensor(
                    ot, h2_ap[:, n * CD:(n + 1) * CD], Hp3[:, gidx, n:n + 1], cu,
                    ALU.mult, ALU.add),
                    reads=h2_b + bHp + cur_b, writes=oth_b)
                cur_ap, cur_b, oth_ap, oth_b = oth_ap, oth_b, cur_ap, cur_b
            xm_ap, xm_b = phC.f(0, H)
            for i in range(NSTR):
                VE(lambda v, i=i, gidx=gidx: v.tensor_scalar(
                    xm_ap[:, i * CD:(i + 1) * CD], h2_ap[:, 0:CD],
                    Hr3[:, gidx, i * 4:i * 4 + 1], None, ALU.mult),
                    reads=h2_b + bHr, writes=xm_b, after=[ev_hres])
                for jj in range(1, NSTR):
                    VE(lambda v, i=i, jj=jj, gidx=gidx: v.scalar_tensor_tensor(
                        xm_ap[:, i * CD:(i + 1) * CD], h2_ap[:, jj * CD:(jj + 1) * CD],
                        Hr3[:, gidx, i * 4 + jj:i * 4 + jj + 1],
                        xm_ap[:, i * CD:(i + 1) * CD], ALU.mult, ALU.add),
                        reads=h2_b + bHr + xm_b, writes=xm_b)
            xpT_ap, xpT_b = phD.bf(0, KCD * 128)
            xpT = xpT_ap.rearrange("p (k q) -> p k q", q=128)
            for kk in range(KCD):
                ev_tp = TE(lambda te, kk=kk, cu=cur_ap: te.transpose(
                    pb[7][:, 0:128], cu[:, kk * 128:(kk + 1) * 128], eyef_t[:, :]),
                    reads=cur_b + [beyef], writes=[bpb[7]], after=[ev_eyef])
                SC(lambda s, kk=kk: s.copy(xpT[:, kk, :], pb[7][:, 0:128]),
                   reads=[bpb[7]], writes=xpT_b, after=[ev_tp])
            ybr_ap, ybr_b = phE.f(0, CD)
            nwc = (CD + WBN - 1) // WBN
            for cc2 in range(nwc):
                w0 = cc2 * WBN
                wid = min(WBN, CD - w0)
                bank = 4 + cc2 % 2
                wb_ap, wb_b = phF.bf(0, KCD * wid)
                wb_tiles = wb_ap.rearrange("p (k w) -> p k w", w=wid)
                ev_wbl = DMA(lambda s, w0=w0, wid=wid, wt=wb_tiles: s.dma_start(
                    out=wt, in_=wb_bf.ap()[:, w0:w0 + wid].rearrange("(k p) w -> p k w", p=128)),
                    reads=[bB("wb_bf")], writes=wb_b, after=[ev_wb])

                def mmwb(te, w0=w0, wid=wid, bank=bank, xpT=xpT, wt=wb_tiles):
                    for kk in range(KCD):
                        inst = te.matmul(pb[bank][:, 0:wid], xpT[:, kk, :],
                                         wt[:, kk, :],
                                         start=(kk == 0), stop=(kk == KCD - 1))
                    return inst
                ev_mw = TE(mmwb, reads=xpT_b + wb_b, writes=[bpb[bank]], after=[ev_wbl])
                SC(lambda s, w0=w0, wid=wid, bank=bank: s.copy(
                    ybr_ap[:, w0:w0 + wid], pb[bank][:, 0:wid]),
                    reads=[bpb[bank]], writes=ybr_b, after=[ev_mw])
            for n in range(NSTR):
                VE(lambda v, n=n, gidx=gidx: v.scalar_tensor_tensor(
                    xm_ap[:, n * CD:(n + 1) * CD], ybr_ap, Hq3[:, gidx, n:n + 1],
                    xm_ap[:, n * CD:(n + 1) * CD], ALU.mult, ALU.add),
                    reads=ybr_b + bHq + xm_b, writes=xm_b)
            o_ap, o_b = phA.f(0, H)
            VE(lambda v: v.tensor_tensor(o_ap, xm_ap, gb_ap, ALU.mult),
               reads=xm_b + gb_b, writes=o_b, after=[ev_gb])
            ev_o = VE(lambda v: v.tensor_add(o_ap, o_ap, x2_ap), reads=x2_b + o_b, writes=o_b)
            DMA(lambda s, ph=ph, ttg=ttg, i=o_ap: s.dma_start(
                out=out_my.ap()[ph, ttg * 128:(ttg + 1) * 128, :], in_=i),
                reads=o_b, writes=[bB("out_dma")], after=[ev_o])

    if cfg.get("TAPS2"):
        def tapcp(name, t, shape, deps):
            o = nc.dram_tensor(name, list(shape), t.dtype if hasattr(t, "dtype") else F32,
                               kind="ExternalOutput")
            n = 1
            for s_ in shape:
                n *= s_
            DMA(lambda s, o=o, t=t: s.dma_start(out=o.ap().rearrange(
                " ".join("abcde"[:len(shape)]) + " -> (" + " ".join("abcde"[:len(shape)]) + ")"),
                in_=t.ap().rearrange(
                " ".join("abcde"[:len(shape)]) + " -> (" + " ".join("abcde"[:len(shape)]) + ")")),
                reads=[bB(k) for k in deps], writes=[bB(name)])
        tapcp("o_x2", x2_sp, (2, QT, H), [("x2_sp", ph_, t_) for ph_ in range(2) for t_ in range(NT_Q)])
        tapcp("o_h2", h2_sp, (2, QT, H), [("h2_sp", ph_, t_) for ph_ in range(2) for t_ in range(NT_Q)])
        tapcp("o_ssm0", ssm_out[0], (T, E), [("ssm_out", 0)])
        tapcp("o_dt", dt_sp, (B, DSL, T), [("dt_sp", b_, d_, th_) for b_ in range(B)
                                           for d_ in range(NT_D) for th_ in range(2)])
        tapcp("o_xc", xc_sp, (B, DSL, T), [("xc_sp", b_, d_, th_) for b_ in range(B)
                                           for d_ in range(NT_D) for th_ in range(2)])
        tapcp("o_xsp", x_sp, (B, DSL, T), [("x_sp", b_, d_, tc_) for b_ in range(B)
                                           for d_ in range(NT_D) for tc_ in range(NTC)])
        tapcp("o_zsp", z_sp, (B, DSL, T), [("z_sp", b_, d_, tc_) for b_ in range(B)
                                           for d_ in range(NT_D) for tc_ in range(NTC)])
        tapcp("o_Bt", Bt_i, (B, NST, T), [("Bt_i", b_) for b_ in range(B)])
        tapcp("o_Ct", Ct_i, (B, NST, T), [("Ct_i", b_) for b_ in range(B)])
        tapcp("o_ymyA", y_my[0], (NC, DSL, QT), [("y_my", 0)])
        tapcp("o_mod", mod_my, (NC, 2, MODC), ["mod_my"])
        tapcp("o_hT", hT_all, (NC, H, 2 * QT), ["hT_all"])

    if cfg.get("DEBUG_TAPS"):
        dbg = nc.dram_tensor("dbg", [128, G * 24 + 3 * G * NST + 3 * CD + 2 * H], F32,
                             kind="ExternalOutput")
        o0 = 0
        xpf_ap, xpf_b = phD.f(2048, 2048 + CD)
        xpf2_ap, xpf2_b = phD.f(2048 + CD, 2048 + 2 * CD)
        ybr2_ap, ybr2_b = phE.f(0, CD)
        xm2_ap, xm2_b = phC.f(0, H)
        for (ap_, bufs_, w_) in ((logits_sb, blg, G * 24), (Hres_sb, bHr, G * NST),
                                 (Hpre_sb, bHp, G * 4), (Hpost_sb, bHq, G * 4),
                                 (xpf_ap, xpf_b, CD), (xpf2_ap, xpf2_b, CD),
                                 (ybr2_ap, ybr2_b, CD), (xm2_ap, xm2_b, H),
                                 (phG.f(0, H)[0], phG.f(0, H)[1], H)):
            DMA(lambda s, a=ap_, o=o0, w=w_: s.dma_start(out=dbg.ap()[:, o:o + w], in_=a),
                reads=bufs_, writes=[bB("dbg")])
            o0 += w_

    fin = bB("out_dma").ready
    P.op("gp", lambda g: g.engine_nop(), after=[fin])

    P.check_deadlock()
    global _LASTPROG
    _LASTPROG = P
    semnames = sorted(P.cnt.keys())
    with contextlib.ExitStack() as sctx:
        sems = {s: sctx.enter_context(nc.semaphore(f"sem_{s}")) for s in semnames}
        with nc.Block() as block:
            with allow_ici_all_to_all():
                P.realize(sems, block)
    ctx.close()
    return nc


def _stage_inputs(inputs, c, k):
    d = derived(c)
    B, T, H, DIN, NST, KCV, R = c["B"], c["T"], c["H"], c["DIN"], c["NST"], c["KCV"], c["R"]
    DSL, MODC, QT, RT, RP, NCP = d["DSL"], d["MODC"], d["QT"], d["RT"], d["RP"], d["NCP"]
    f32 = np.float32
    x = np.asarray(inputs["L0_attn_hidden_states"], f32)
    q = k % 4
    bA, bBv = k // 4, 2 + k // 4
    x_myv = np.stack([x[bA, q * QT:(q + 1) * QT, :], x[bBv, q * QT:(q + 1) * QT, :]])
    win = np.asarray(inputs["L0_attn_in_proj_w"], f32)
    win_cat = np.concatenate([win[:, k * DSL:(k + 1) * DSL],
                              win[:, DIN + k * DSL:DIN + (k + 1) * DSL]], axis=1)
    win_tv = np.ascontiguousarray(
        win_cat.reshape(H // 128, 128, NCP, 256).transpose(2, 0, 1, 3))
    wmod = np.asarray(inputs["L0_wmod"], f32)
    return {
        "x_my": np.ascontiguousarray(x_myv),
        "cT_in": np.ascontiguousarray(np.asarray(inputs["L0_c"], f32).T),
        "wmod_sl": np.ascontiguousarray(wmod[:, k * MODC:(k + 1) * MODC]),
        "win_t": win_tv,
        "xpwT_sl": np.ascontiguousarray(
            np.asarray(inputs["L0_attn_x_proj_w"], f32)[:, k * DSL:(k + 1) * DSL].T),
        "dtpwT_sl": np.ascontiguousarray(
            np.asarray(inputs["L0_attn_dt_proj_w"], f32)[k * DSL:(k + 1) * DSL, :].T
            .reshape(RT, RP, DSL)),
        "dtb_sl": np.ascontiguousarray(
            np.asarray(inputs["L0_attn_dt_bias"], f32)[k * DSL:(k + 1) * DSL]),
        "A_sl": np.ascontiguousarray(
            np.asarray(inputs["L0_attn_A_log"], f32)[k * DSL:(k + 1) * DSL]),
        "D_sl": np.ascontiguousarray(
            np.asarray(inputs["L0_attn_D_param"], f32)[k * DSL:(k + 1) * DSL]),
        "convw_sl": np.ascontiguousarray(
            np.asarray(inputs["L0_attn_conv_w"], f32)[k * DSL:(k + 1) * DSL]),
        "convb_sl": np.ascontiguousarray(
            np.asarray(inputs["L0_attn_conv_b"], f32)[k * DSL:(k + 1) * DSL]),
        "wout_sl": np.ascontiguousarray(
            np.asarray(inputs["L0_attn_out_proj_w"], f32).T[k * DSL:(k + 1) * DSL, :]),
        "dt_ln": np.ascontiguousarray(np.asarray(inputs["L0_attn_dt_ln_w"], f32)),
        "B_ln": np.ascontiguousarray(np.asarray(inputs["L0_attn_B_ln_w"], f32)),
        "C_ln": np.ascontiguousarray(np.asarray(inputs["L0_attn_C_ln_w"], f32)),
        "wres_t": np.ascontiguousarray(np.concatenate(
            [np.asarray(inputs["L0_ffn_W_res"], f32),
             np.asarray(inputs["L0_ffn_W_pre"], f32),
             np.asarray(inputs["L0_ffn_W_post"], f32)], axis=0).T),
        "base_cat": np.ascontiguousarray(np.concatenate(
            [np.asarray(inputs["L0_ffn_H_res_base"], f32).ravel(),
             np.asarray(inputs["L0_ffn_H_pre_base"], f32).ravel(),
             np.asarray(inputs["L0_ffn_H_post_base"], f32).ravel()])),
        "wb_in": np.ascontiguousarray(np.asarray(inputs["L0_ffn_W_branch"], f32)),
        "gate_in": np.ascontiguousarray(np.asarray(inputs["L0_ffn_gate"], f32).reshape(1, 1)),
        "eye_f": np.eye(128, dtype=f32),
        "eye4_in": np.ascontiguousarray(np.eye(4, dtype=f32).ravel()),
    }


def kernel(**inputs):
    c = real_cfg()
    d = derived(c)
    nc = build_nc(c)
    in_maps = [_stage_inputs(inputs, c, k) for k in range(c["NC"])]
    with allow_ici_all_to_all():
        res = run_bass_kernel_spmd(nc, in_maps, core_ids=list(range(c["NC"])))
    B, T, H, QT = c["B"], c["T"], c["H"], d["QT"]
    out = np.zeros((B, T, H), np.float32)
    for k in range(c["NC"]):
        o = np.asarray(res.results[k]["out_my"])
        q = k % 4
        out[k // 4, q * QT:(q + 1) * QT, :] = o[0]
        out[2 + k // 4, q * QT:(q + 1) * QT, :] = o[1]
    return out


# version 23 (final)
# speedup vs baseline: 1.0099x; 1.0013x over previous
import contextlib
import numpy as np

import concourse.bass as bass
import concourse.mybir as mybir
from concourse.bass_utils import run_bass_kernel_spmd
from concourse.aot_env import allow_ici_all_to_all

F32 = mybir.dt.float32
BF16 = mybir.dt.bfloat16
AF = mybir.ActivationFunctionType
ALU = mybir.AluOpType
AX = mybir.AxisListType

ENGS = ("te", "ve", "sc", "gp", "sy")
_LASTPROG = None


def real_cfg():
    return dict(B=4, T=2048, H=4096, DIN=8192, NST=16, KCV=4, R=256, NC=8,
                TAU=0.1, ITERS=50, NSTR=4,
                EPS_LN=1e-5, EPS_RMS_A=1e-6, EPS_RMS_F=1.1920929e-07)


def derived(c):
    d = dict(c)
    d["DSL"] = c["DIN"] // c["NC"]
    d["NT_D"] = d["DSL"] // 128
    d["KH"] = c["H"] // 128
    d["KD"] = c["DIN"] // 128
    d["MODC"] = 6 * c["H"] // c["NC"]
    d["QT"] = c["T"] // 4
    d["NT_Q"] = d["QT"] // 128
    d["TC"] = 256
    d["NTC"] = c["T"] // d["TC"]
    d["THL"] = c["T"] // 2
    d["DTBLK"] = min(4, d["NT_D"])
    d["NBLK"] = max(1, d["NT_D"] // d["DTBLK"])
    d["E"] = c["R"] + 2 * c["NST"]
    d["RT"] = (c["R"] + 127) // 128
    d["RP"] = min(c["R"], 128)
    d["CD"] = c["H"] // c["NSTR"]
    d["KCD"] = d["CD"] // 128
    d["NCP"] = 2 * d["DSL"] // 256
    d["G"] = 2 * d["NT_Q"]
    d["WBN"] = min(512, d["CD"])
    return d


class Buf:
    __slots__ = ("ready", "users")

    def __init__(self):
        self.ready = None
        self.users = []


class Prog:
    NROT = {"dma": 16, "cc": 8}

    def __init__(self):
        self.stream = {e: [] for e in ENGS}
        self.cnt = {}
        self.bufs = {}
        self.qcnt = {}
        self.evwm = {}

    def B(self, key):
        b = self.bufs.get(key)
        if b is None:
            b = self.bufs[key] = Buf()
        return b

    def op(self, eng, fn, reads=(), writes=(), after=(), kind="compute"):
        deps = [a for a in after if a is not None]
        if kind == "compute":
            sem, inc = eng, 1
            self.cnt[sem] = self.cnt.get(sem, 0) + inc
        else:
            if kind == "dma":
                qbase, inc = {"sy": "syd", "gp": "gpd", "sc": "scd"}[eng], 16
            else:
                qbase, inc = "cc", 1
            nrot = self.NROT[kind]
            idx = self.qcnt.get(qbase, 0)
            self.qcnt[qbase] = idx + 1
            sem = f"{qbase}{idx % nrot}"
            prev = self.cnt.get(sem, 0)
            if prev > 0:
                deps.append((sem, prev))
            self.cnt[sem] = prev + inc
        ev = (sem, self.cnt[sem])
        for b in reads:
            if b.ready is not None:
                deps.append(b.ready)
        for b in writes:
            deps.extend(b.users)
            if b.ready is not None:
                deps.append(b.ready)
        wm = {}
        for s, v in deps:
            if wm.get(s, 0) < v:
                wm[s] = v
        for b in reads:
            b.users.append(ev)
        for b in writes:
            b.ready = ev
            b.users = []
        self.stream[eng].append((tuple(sorted(wm.items())), fn, sem, inc))
        return ev

    def check_deadlock(self):
        ptr = {e: 0 for e in ENGS}
        val = {}
        progressed = True
        while progressed:
            progressed = False
            for e in ENGS:
                while ptr[e] < len(self.stream[e]):
                    waits, _, sem, inc = self.stream[e][ptr[e]]
                    if any(val.get(s, 0) < v for s, v in waits):
                        break
                    val[sem] = val.get(sem, 0) + inc
                    ptr[e] += 1
                    progressed = True
        stuck = {e: (ptr[e], len(self.stream[e])) for e in ENGS if ptr[e] < len(self.stream[e])}
        if stuck:
            msgs = []
            for e, (p, n) in stuck.items():
                waits = self.stream[e][p][0]
                bad = [(s, v, val.get(s, 0)) for s, v in waits if val.get(s, 0) < v]
                msgs.append(f"{e}@{p}/{n} unmet={bad[:4]}")
            raise RuntimeError("deadlock: " + "; ".join(msgs))

    def realize(self, sems, block):
        def mk(e):
            def body(eng):
                maxw = {}
                for waits, fn, sem, inc in self.stream[e]:
                    for s, v in waits:
                        if maxw.get(s, 0) < v:
                            eng.wait_ge(sems[s], v)
                            maxw[s] = v
                    fn(eng).then_inc(sems[sem], inc)
            return body
        block.sync(mk("sy"))
        block.tensor(mk("te"))
        block.vector(mk("ve"))
        block.scalar(mk("sc"))
        block.gpsimd(mk("gp"))


class Tile:

    def __init__(self, P, t, name, cols):
        self.P = P
        self.t = t
        self.name = name
        self.cols = cols
        self.ngrid = (cols + 511) // 512

    def bufs(self, lo, hi):
        return [self.P.B((self.name, i)) for i in range(lo // 512, (hi + 511) // 512)]

    def f(self, lo, hi):
        return self.t[:, lo:hi], self.bufs(lo, hi)

    def bf(self, lo, hi_bf):
        ap = self.t[:, lo:self.cols].bitcast(BF16)[:, 0:hi_bf]
        return ap, self.bufs(lo, lo + (hi_bf + 1) // 2)


def build_nc(cfg):
    c = derived(cfg)
    B, T, H, DIN, NST, KCV, R, NC = (c["B"], c["T"], c["H"], c["DIN"], c["NST"],
                                     c["KCV"], c["R"], c["NC"])
    DSL, NT_D, KH, KD, MODC = c["DSL"], c["NT_D"], c["KH"], c["KD"], c["MODC"]
    QT, NT_Q, TC, NTC, THL = c["QT"], c["NT_Q"], c["TC"], c["NTC"], c["THL"]
    DTBLK, NBLK, E, RT, RP = c["DTBLK"], c["NBLK"], c["E"], c["RT"], c["RP"]
    CD, KCD, NCP, G, WBN = c["CD"], c["KCD"], c["NCP"], c["G"], c["WBN"]
    NSTR, TAU, ITERS = c["NSTR"], c["TAU"], c["ITERS"]
    NQB = THL // QT

    nc = bass.Bass("TRN2", target_bir_lowering=False, detect_race_conditions=False)
    P = Prog()
    bB = P.B

    def din(name, shape, dtype=F32):
        return nc.dram_tensor(name, list(shape), dtype, kind="ExternalInput")

    def dint(name, shape, dtype=F32):
        return nc.dram_tensor(name, list(shape), dtype, kind="Internal")

    x_my = din("x_my", (2, QT, H))
    cT_in = din("cT_in", (H, B))
    wmod_sl = din("wmod_sl", (H, MODC))
    win_t = din("win_t", (NCP, KH, 128, 256))
    xpwT_sl = din("xpwT_sl", (DSL, E))
    dtpwT_sl = din("dtpwT_sl", (RT, RP, DSL))
    dtb_sl = din("dtb_sl", (DSL,))
    A_sl = din("A_sl", (DSL, NST))
    D_sl = din("D_sl", (DSL,))
    convw_sl = din("convw_sl", (DSL, KCV))
    convb_sl = din("convb_sl", (DSL,))
    wout_sl = din("wout_sl", (DSL, H))
    dt_ln = din("dt_ln", (R,))
    B_ln = din("B_ln", (NST,))
    C_ln = din("C_ln", (NST,))
    wres_t = din("wres_t", (H, 24))
    base_cat = din("base_cat", (24,))
    wb_in = din("wb_in", (CD, CD))
    gate_in = din("gate_in", (1, 1))
    eye_f = din("eye_f", (128, 128))
    eye4_in = din("eye4_in", (NST,))

    out_my = nc.dram_tensor("out_my", [2, QT, H], F32, kind="ExternalOutput")

    win_bf = dint("win_bf", (NCP, KH, 128, 256), BF16)
    wout_bf = dint("wout_bf", (DSL, H), BF16)
    wout_all = dint("wout_all", (NC, DSL, H), BF16)
    wmod_bf = dint("wmod_bf", (H, MODC), BF16)
    wb_bf = dint("wb_bf", (CD, CD), BF16)
    xpwT_bf = dint("xpwT_bf", (DSL, E), BF16)
    dtpwT_bf = dint("dtpwT_bf", (RT, RP, DSL), BF16)
    wres_bf = dint("wres_bf", (H, 24), BF16)
    mod_part = dint("mod_part", (NC, 2, MODC))
    mod_my = dint("mod_my", (NC, 2, MODC))
    hT_my = [dint(f"hT_my{p}", (H, QT), BF16) for p in range(2)]
    hT_all = [dint(f"hT_all{p}", (NC, H, QT), BF16) for p in range(2)]
    ssm_in = [dint(f"ssm_in{b}", (T, E)) for b in range(B)]
    ssm_out = [dint(f"ssm_out{b}", (T, E)) for b in range(B)]
    x_sp = dint("x_sp", (B, DSL, T))
    z_sp = dint("z_sp", (B, DSL, T))
    xc_sp = dint("xc_sp", (B, DSL, T))
    xcbf_sp = dint("xcbf_sp", (B, DSL, T), BF16)
    dt_sp = dint("dt_sp", (B, DSL, T))
    Bt_i = dint("Bt_i", (B, NST, T))
    Ct_i = dint("Ct_i", (B, NST, T))
    g_dr = dint("g_dr", (1, 1))
    y_in = [dint(f"y_in{p}", (NC, DSL, QT), BF16) for p in range(2)]
    y_my = [dint(f"y_my{p}", (NC, DSL, QT), BF16) for p in range(2)]
    x2_sp = dint("x2_sp", (2, QT, H))
    h2_sp = dint("h2_sp", (2, QT, H))

    ctx = contextlib.ExitStack()

    def TT(name, cols):
        t = ctx.enter_context(nc.sbuf_tensor(name, [128, cols], F32))
        return Tile(P, t, name, cols)

    phA = TT("phA", 4096)
    phB = TT("phB", 4096)
    phB2 = TT("phB2", 4096)
    phC = TT("phC", 4096)
    phD = TT("phD", 4096)
    phE = TT("phE", 4096)
    phF = TT("phF", 3072)
    phG = TT("phG", 4096)
    phH = TT("phH", 2048)
    phI = TT("phI", 2048)
    phJ = TT("phJ", 2048)
    phK = TT("phK", 3072)
    phL = TT("phL", 2048)
    phM = TT("phM", 3072)
    smT = TT("sm", 5120)
    dtpwT_t = ctx.enter_context(nc.sbuf_tensor("dtpwT_sb", [RP, RT * DSL], BF16))
    eyef_t = ctx.enter_context(nc.sbuf_tensor("eyef_sb", [128, 128], F32))

    _smoff = [0]

    def sml(name, cols):
        off = _smoff[0]
        _smoff[0] += cols
        assert _smoff[0] <= 5120, f"sm overflow at {name}"
        return smT.t[:, off:off + cols], [bB(("sm", name))]

    A_sb, bA_sb = sml("A", NT_D * NST)
    D_sb, bD_sb = sml("D", NT_D)
    dtb_sb, bdtb = sml("dtb", NT_D)
    cvb_sb, bcvb = sml("cvb", NT_D)
    cvw_sb0, bcvw = sml("cvw", NT_D * KCV)
    cvw_sb = cvw_sb0.rearrange("p (d k) -> p d k", k=KCV)
    dtln_bc, bdtln = sml("dtln", R)
    Bln_bc, bBln = sml("Bln", NST)
    Cln_bc, bCln = sml("Cln", NST)
    base_bc, bbase = sml("base", 24)
    eye4_bc, beye4 = sml("eye4", NST)
    g_bc, bg = sml("g", 1)
    gm1_bc, bgm1 = sml("gm1", 1)
    s1_t, bs1 = sml("s1", 1)
    s2_t, bs2 = sml("s2", 1)
    mu_t, bmu = sml("mu", 1)
    var_t, bvar = sml("var", 1)
    rstd_t, brstd = sml("rstd", 1)
    rho_t, brho = sml("rho", 3)
    sstate, bsst = sml("sstate", NT_D * NST)
    ct_f, bctf = sml("ct_f", KH * B)
    ct_b0, bctb = sml("ct_b", (KH * B + 1) // 2)
    ct_bf = ct_b0.bitcast(BF16)[:, 0:KH * B]
    btct, bbtct = sml("btct", 128)
    xcb_t0, bxcb0 = sml("xcb0", 128)
    xcb_t1, bxcb1 = sml("xcb1", 128)
    xcb_t2, bxcb2 = sml("xcb2", 128)
    xcb_t3, bxcb3 = sml("xcb3", 128)
    logits_sb, blg = sml("logits", G * 24)
    Z_sb, bZ = sml("Z", G * NST)
    ZT_sb, bZT = sml("ZT", G * NST)
    u_sb, bu = sml("u", G * 4)
    v_sb, bv = sml("v", G * 4)
    mx_sb, bmx = sml("mx", G * 4)
    t4_sb, bt4 = sml("t4", 2 * G)
    Hres_sb, bHr = sml("Hres", G * NST)
    Hpre_sb, bHp = sml("Hpre", G * 4)
    Hpost_sb, bHq = sml("Hpost", G * 4)
    snk_sb, bsnk = sml("snk", G * NST)
    ev0_sb, bev0 = sml("ev0", 288)
    ev1_sb, bev1 = sml("ev1", 288)
    wrs0, bwrs = sml("wrs", (KH * 24 + 1) // 2)
    wres_tiles = wrs0.bitcast(BF16)[:, 0:KH * 24].rearrange("p (c e) -> p c e", e=24)
    xpw0, bxpw = sml("xpw", (NT_D * E + 1) // 2)
    xpwT_sb = xpw0.bitcast(BF16)[:, 0:NT_D * E].rearrange("p (c e) -> p c e", e=E)
    gt_sb, bgt = sml("gt", 4)
    junk_sb, bjunk = sml("junk", 260)

    pb = [ctx.enter_context(nc.psum_tensor(f"pb{i}", [128, 512], F32)) for i in range(8)]
    bpb = [bB(("pb", i)) for i in range(8)]
    beyef = bB("eyef")
    bdtpw = bB("dtpw_sb")

    def TE(fn, **kw):
        return P.op("te", fn, **kw)

    def VE(fn, **kw):
        return P.op("ve", fn, **kw)

    def SC(fn, **kw):
        return P.op("sc", fn, **kw)

    def GP(fn, **kw):
        return P.op("gp", fn, **kw)

    def DMA(fn, **kw):
        return P.op("sy", fn, kind="dma", **kw)

    def SDMA(fn, **kw):
        return P.op("sc", fn, kind="dma", **kw)

    def GDMA(fn, **kw):
        return P.op("gp", fn, kind="dma", **kw)

    def CC(fn, **kw):
        return P.op("gp", fn, kind="cc", **kw)

    ev_win = []
    for i in range(NCP):
        ev_win.append(GDMA(lambda g, i=i: g.dma_start(out=win_bf.ap()[i], in_=win_t.ap()[i]),
                           writes=[bB(("win_bf", i))]))
    nwo = 4
    ev_wout_c = []
    for i in range(nwo):
        r0, r1 = i * DSL // nwo, (i + 1) * DSL // nwo
        ev_wout_c.append(GDMA(lambda g, r0=r0, r1=r1: g.dma_start(
            out=wout_bf.ap()[r0:r1, :], in_=wout_sl.ap()[r0:r1, :]), writes=[bB("wout_bf")]))
    nwm = 8
    ev_wmod_c = []
    for i in range(nwm):
        r0, r1 = i * H // nwm, (i + 1) * H // nwm
        ev_wmod_c.append(GDMA(lambda g, r0=r0, r1=r1: g.dma_start(
            out=wmod_bf.ap()[r0:r1, :], in_=wmod_sl.ap()[r0:r1, :]), writes=[bB("wmod_bf")]))
    ev_wb = GDMA(lambda g: g.dma_start(out=wb_bf.ap(), in_=wb_in.ap()), writes=[bB("wb_bf")])
    ev_xpwc = GDMA(lambda g: g.dma_start(out=xpwT_bf.ap(), in_=xpwT_sl.ap()), writes=[bB("xpw_bf")])
    ev_dtpwc = GDMA(lambda g: g.dma_start(out=dtpwT_bf.ap(), in_=dtpwT_sl.ap()),
                    writes=[bB("dtpw_bf")])
    ev_wresc = GDMA(lambda g: g.dma_start(out=wres_bf.ap(), in_=wres_t.ap()), writes=[bB("wres_bf")])

    def ld(dst, src, writes, after=(), slow=False):
        return DMA(lambda s, d=dst, i=src, slow=slow: s.dma_start(
            out=d, in_=i, allow_slow_non_contiguous=slow), writes=writes, after=after)

    ev_A0 = ld(A_sb.rearrange("p (c n) -> p c n", n=NST),
               A_sl.ap().rearrange("(c p) n -> p c n", p=128), bA_sb)
    ev_A1 = SC(lambda s: s.activation(A_sb, A_sb, AF.Exp), reads=bA_sb, writes=bA_sb)
    ev_A = VE(lambda v: v.tensor_scalar_mul(A_sb, A_sb, -1.0), reads=bA_sb, writes=bA_sb)
    ev_D = ld(D_sb, D_sl.ap().rearrange("(c p) -> p c", p=128), bD_sb, slow=True)
    ev_dtb = ld(dtb_sb, dtb_sl.ap().rearrange("(c p) -> p c", p=128), bdtb, slow=True)
    ev_cvb = ld(cvb_sb, convb_sl.ap().rearrange("(c p) -> p c", p=128), bcvb, slow=True)
    ev_cvw = ld(cvw_sb0.rearrange("p (c k) -> p c k", k=KCV),
                convw_sl.ap().rearrange("(c p) k -> p c k", p=128), bcvw)
    ev_dtln = ld(dtln_bc, dt_ln.ap().unsqueeze(0).broadcast_to([128, R]), bdtln)
    ev_Bln = ld(Bln_bc, B_ln.ap().unsqueeze(0).broadcast_to([128, NST]), bBln)
    ev_Cln = ld(Cln_bc, C_ln.ap().unsqueeze(0).broadcast_to([128, NST]), bCln)
    ev_base = ld(base_bc, base_cat.ap().unsqueeze(0).broadcast_to([128, 24]), bbase)
    ev_eye4 = ld(eye4_bc, eye4_in.ap().unsqueeze(0).broadcast_to([128, NST]), beye4)
    ev_eyef = ld(eyef_t[:, :], eye_f.ap(), [beyef])
    ev_dtpws = ld(dtpwT_t[:, :].rearrange("p (r d) -> p r d", d=DSL),
                  dtpwT_bf.ap().rearrange("r p d -> p r d"), [bdtpw],
                  after=[ev_dtpwc])
    ev_xpws = ld(xpw0.bitcast(BF16)[:, 0:NT_D * E].rearrange("p (cc e) -> p cc e", e=E),
                 xpwT_bf.ap().rearrange("(cc p) e -> p cc e", p=128), bxpw, after=[ev_xpwc])
    ev_wres_ld = ld(wrs0.bitcast(BF16)[:, 0:KH * 24].rearrange("p (cc e) -> p cc e", e=24),
                    wres_bf.ap().rearrange("(cc p) e -> p cc e", p=128), bwrs,
                    after=[ev_wresc])

    ev_g0 = ld(gt_sb[:, 0:1][0:1, :], gate_in.ap(), bgt)
    ev_g1 = SC(lambda s: s.activation(gt_sb[0:1, 1:2], gt_sb[0:1, 0:1], AF.Exp, scale=-1.0),
               reads=bgt, writes=bgt)
    ev_g2 = VE(lambda v: v.tensor_scalar_add(gt_sb[0:1, 2:3], gt_sb[0:1, 1:2], 1.0),
               reads=bgt, writes=bgt)
    ev_g3 = VE(lambda v: v.reciprocal(gt_sb[0:1, 3:4], gt_sb[0:1, 2:3]), reads=bgt, writes=bgt)
    ev_g4 = DMA(lambda s: s.dma_start(out=g_dr.ap(), in_=gt_sb[0:1, 3:4]),
                reads=bgt, writes=[bB("g_dr")])
    ev_g5 = DMA(lambda s: s.dma_start(out=g_bc, in_=g_dr.ap().broadcast_to([128, 1])),
                reads=[bB("g_dr")], writes=bg)
    ev_g6 = VE(lambda v: v.tensor_scalar(gm1_bc, g_bc, -1.0, 1.0, ALU.mult, ALU.add),
               reads=bg, writes=bgm1)

    ev_ct0 = ld(ct_f.rearrange("p (cc b) -> p cc b", b=B),
                cT_in.ap().rearrange("(cc p) b -> p cc b", p=128), bctf)
    ev_ct1 = SC(lambda s: s.activation(ct_f, ct_f, AF.Silu), reads=bctf, writes=bctf)
    ev_ct2 = VE(lambda v: v.tensor_copy(ct_bf, ct_f), reads=bctf, writes=bctb)
    ct_tiles = ct_bf.rearrange("p (cc b) -> p cc b", b=B)

    stg_ap, stg_b = phA.f(0, MODC)
    mod_stage = stg_ap[0:B, :]
    n_mch = (MODC + 255) // 256
    for i in range(n_mch):
        w0 = i * 256
        wid = min(256, MODC - w0)
        rhs_ap, rhs_b = phB.bf(0, KH * wid)
        rhs3 = rhs_ap.rearrange("p (cc w) -> p cc w", w=wid)
        ev_l = DMA(lambda s, w0=w0, wid=wid, r=rhs3: s.dma_start(
            out=r, in_=wmod_bf.ap()[:, w0:w0 + wid].rearrange("(cc p) w -> p cc w", p=128)),
            reads=[bB("wmod_bf")], writes=rhs_b, after=ev_wmod_c)

        def mmod(te, rhs3=rhs3, wid=wid):
            for k in range(KH):
                inst = te.matmul(pb[0][0:B, 0:wid], ct_tiles[:, k, :], rhs3[:, k, :],
                                 start=(k == 0), stop=(k == KH - 1))
            return inst
        ev_m = TE(mmod, reads=rhs_b + bctb, writes=[bpb[0]], after=[ev_ct2, ev_l])
        VE(lambda v, w0=w0, wid=wid: v.tensor_copy(mod_stage[:, w0:w0 + wid], pb[0][0:B, 0:wid]),
           reads=[bpb[0]], writes=stg_b, after=[ev_m])
    for half in range(2):
        for bb in range(2):
            b_src = half * 2 + bb
            for rr in range(NC // 2):
                DMA(lambda s, half=half, bb=bb, b_src=b_src, rr=rr: s.dma_start(
                    out=mod_part.ap()[bb * (NC // 2) + rr, half, :].unsqueeze(0),
                    in_=mod_stage[b_src:b_src + 1, :]),
                    reads=stg_b, writes=[bB("mod_part")])
    ev_a2a_mod = CC(lambda g: g.collective_compute(
        "AllToAll", ALU.bypass, replica_groups=[list(range(NC))],
        ins=[mod_part.ap().opt()], outs=[mod_my.ap().opt()]),
        reads=[bB("mod_part")], writes=[bB("mod_my")])
    def emit_ag_wout():
        return CC(lambda g: g.collective_compute(
            "AllGather", ALU.bypass, replica_groups=[list(range(NC))],
            ins=[wout_bf.ap().opt()], outs=[wout_all.ap().opt()]),
            reads=[bB("wout_bf")], writes=[bB("wout_all")], after=ev_wout_c)

    mm_flat = mod_my.ap().rearrange("j r m -> (j r m)")

    def bcast_mod(tile, lo_col, ph, vec, lo, hi):
        dst_ap, dst_b = tile.f(lo_col, lo_col + (hi - lo))
        evs = None
        coff = 0
        pos = vec * H + lo
        end = vec * H + hi
        while pos < end:
            j = pos // MODC
            ib = pos % MODC
            take = min(MODC - ib, end - pos)
            src = mm_flat[j * (2 * MODC) + ph * MODC + ib:
                          j * (2 * MODC) + ph * MODC + ib + take]
            evs = DMA(lambda s, d=dst_ap[:, coff:coff + take],
                      i=src.unsqueeze(0).broadcast_to([128, take]): s.dma_start(out=d, in_=i),
                      reads=[bB("mod_my")], writes=dst_b)
            coff += take
            pos += take
        return evs

    for ph in range(2):
        ev_ca = bcast_mod(phC, 0, ph, 1, 0, H)
        ca_ap, ca_b = phC.f(0, H)
        VE(lambda v: v.tensor_scalar_add(ca_ap, ca_ap, 1.0), reads=ca_b, writes=ca_b)
        ev_sa = bcast_mod(phD, 0, ph, 0, 0, H)
        sa_ap, sa_b = phD.f(0, H)
        for tt in range(NT_Q):
            xt_ap, xt_b = phA.f(0, H)
            DMA(lambda s, ph=ph, tt=tt, d=xt_ap: s.dma_start(
                out=d, in_=x_my.ap()[ph, tt * 128:(tt + 1) * 128, :]), writes=xt_b)
            VE(lambda v, i=xt_ap: v.reduce_sum(s1_t, i, AX.X), reads=xt_b, writes=bs1)
            dum_ap, dum_b = phE.f(0, H)
            SC(lambda s, i=xt_ap, d=dum_ap: s.activation(d, i, AF.Square, accum_out=s2_t),
               reads=xt_b, writes=bs2 + dum_b)
            VE(lambda v: v.tensor_scalar_mul(mu_t, s1_t, 1.0 / H), reads=bs1, writes=bmu)
            VE(lambda v: v.tensor_tensor(var_t, mu_t, mu_t, ALU.mult), reads=bmu, writes=bvar)
            VE(lambda v: v.tensor_scalar(junk_sb[:, 0:1], s2_t, 1.0 / H, None, ALU.mult),
               reads=bs2, writes=bjunk)
            VE(lambda v: v.tensor_sub(var_t, junk_sb[:, 0:1], var_t),
               reads=bjunk + bvar, writes=bvar)
            VE(lambda v: v.tensor_scalar_add(var_t, var_t, c["EPS_LN"]),
               reads=bvar, writes=bvar)
            SC(lambda s: s.activation(rstd_t, var_t, AF.Ln),
               reads=bvar, writes=brstd)
            SC(lambda s: s.activation(rstd_t, rstd_t, AF.Exp, scale=-0.5),
               reads=brstd, writes=brstd)
            hn1_ap, hn1_b = phB.f(0, H)
            VE(lambda v, i=xt_ap, o=hn1_ap: v.tensor_scalar(o, i, mu_t, rstd_t,
                                                            ALU.subtract, ALU.mult),
               reads=xt_b + bmu + brstd, writes=hn1_b)
            t1_ap, t1_b = phE.f(0, H)
            VE(lambda v, i=hn1_ap, o=t1_ap, cav=ca_ap: v.tensor_tensor(o, i, cav, ALU.mult),
               reads=hn1_b + ca_b, writes=t1_b)
            h_ap, h_b = phB2.f(0, H)
            VE(lambda v, i=t1_ap, o=h_ap, sav=sa_ap: v.tensor_tensor(o, i, sav, ALU.add),
               reads=t1_b + sa_b, writes=h_b, after=[ev_sa])
            hstg_ap, hstg_b = phG.bf(0, H)
            hstg3 = hstg_ap.rearrange("p (cc q) -> p cc q", q=128)
            for kc in range(KH):
                ev_tp = TE(lambda te, kc=kc, i=h_ap: te.transpose(
                    pb[7][:, 0:128], i[:, kc * 128:(kc + 1) * 128], eyef_t[:, :]),
                    reads=h_b + [beyef], writes=[bpb[7]], after=[ev_eyef])
                SC(lambda s, kc=kc, o=hstg3: s.copy(o[:, kc, :], pb[7][:, 0:128]),
                   reads=[bpb[7]], writes=hstg_b, after=[ev_tp])
            DMA(lambda s, ph=ph, tt=tt: s.dma_start(
                out=hT_my[ph].ap()[:, tt * 128:(tt + 1) * 128]
                .rearrange("(cc p) q -> p cc q", p=128), in_=hstg3),
                reads=hstg_b, writes=[bB(("hT_my", ph))])
    ev_ag_hT = [CC(lambda g, p=p: g.collective_compute(
        "AllGather", ALU.bypass, replica_groups=[list(range(NC))],
        ins=[hT_my[p].ap().opt()], outs=[hT_all[p].ap().opt()]),
        reads=[bB(("hT_my", p))], writes=[bB(("hT_all", p))]) for p in range(2)]

    def h_block(b, t0):
        return 4 * (b % 2) + (t0 // QT), t0 % QT

    def emit_P2(b):
        for tc in range(NTC):
            t0 = tc * TC
            rr, cc0 = h_block(b, t0)
            rhs_ap, rhs_b = phA.bf(0, KH * TC)
            rhs3 = rhs_ap.rearrange("p (cc w) -> p cc w", w=TC)
            ev_rhs = DMA(lambda s, rr=rr, cc0=cc0, ph=b // 2, r=rhs3: s.dma_start(
                out=r, in_=hT_all[ph].ap()[rr, :, cc0:cc0 + TC]
                .rearrange("(cc p) w -> p cc w", p=128)),
                reads=[bB(("hT_all", b // 2))], writes=rhs_b, after=[ev_ag_hT[b // 2]])
            for ctp in range(NCP):
                wtile = phB if ctp % 2 == 0 else phB2
                wt_ap, wt_b = wtile.bf(0, KH * 256)
                wt3 = wt_ap.rearrange("p (cc w) -> p cc w", w=256)
                ev_wt = DMA(lambda s, ctp=ctp, w=wt3: s.dma_start(
                    out=w, in_=win_bf.ap()[ctp].rearrange("cc p w -> p cc w")),
                    reads=[bB(("win_bf", ctp))], writes=wt_b, after=[ev_win[ctp]])
                for hh in range(2):
                    ct = ctp * 2 + hh
                    bank = ct % 6

                    def mmip(te, wt3=wt3, hh=hh, bank=bank, rhs3=rhs3):
                        for k in range(KH):
                            inst = te.matmul(pb[bank][:, 0:TC], wt3[:, k, hh * 128:(hh + 1) * 128],
                                             rhs3[:, k, :], start=(k == 0), stop=(k == KH - 1))
                        return inst
                    ev_mm = TE(mmip, reads=wt_b + rhs_b, writes=[bpb[bank]],
                               after=[ev_wt, ev_rhs])
                    stg = (ev0_sb if ct % 2 == 0 else ev1_sb)[:, 0:TC]
                    stg_bu = bev0 if ct % 2 == 0 else bev1
                    ev_ev = VE(lambda v, bank=bank, stg=stg: v.tensor_copy(stg, pb[bank][:, 0:TC]),
                               reads=[bpb[bank]], writes=stg_bu, after=[ev_mm])
                    dst = x_sp if ct < NT_D else z_sp
                    dtile = ct if ct < NT_D else ct - NT_D
                    GDMA(lambda s, dst=dst, b=b, dtile=dtile, t0=t0, stg=stg: s.dma_start(
                        out=dst.ap()[b, dtile * 128:(dtile + 1) * 128, t0:t0 + TC], in_=stg),
                        reads=stg_bu, writes=[bB((dst.name, b, dtile, tc))], after=[ev_ev])

        for dtile in range(NT_D):
            for th in range(2):
                t0 = th * THL
                xp_ap, xp_b = phG.f(0, THL + KCV - 1)
                deps = [bB(("x_sp", b, dtile, tci))
                        for tci in range(max(0, (t0 - KCV + 1)) // TC, (t0 + THL) // TC)]
                if th == 0:
                    ev_z = VE(lambda v: v.memset(xp_ap[:, 0:KCV - 1], 0.0), writes=xp_b)
                    DMA(lambda s, b=b, dtile=dtile: s.dma_start(
                        out=xp_ap[:, KCV - 1:], in_=x_sp.ap()[b, dtile * 128:(dtile + 1) * 128, 0:THL]),
                        reads=deps, writes=xp_b)
                else:
                    DMA(lambda s, b=b, dtile=dtile, t0=t0: s.dma_start(
                        out=xp_ap, in_=x_sp.ap()[b, dtile * 128:(dtile + 1) * 128,
                                                 t0 - (KCV - 1):t0 + THL]),
                        reads=deps, writes=xp_b)
                cv_ap, cv_b = phG.f(1536, 1536 + THL)
                VE(lambda v, dtile=dtile: v.tensor_scalar(
                    cv_ap, xp_ap[:, 0:THL], cvw_sb[:, dtile, 0:1], cvb_sb[:, dtile:dtile + 1],
                    ALU.mult, ALU.add),
                    reads=xp_b + bcvw + bcvb, writes=cv_b, after=[ev_cvw, ev_cvb])
                for k in range(1, KCV):
                    VE(lambda v, dtile=dtile, k=k: v.scalar_tensor_tensor(
                        cv_ap, xp_ap[:, k:k + THL], cvw_sb[:, dtile, k:k + 1], cv_ap,
                        ALU.mult, ALU.add),
                        reads=xp_b + bcvw + cv_b, writes=cv_b)
                xcf_ap, xcf_b = phG.f(2560, 2560 + THL)
                SC(lambda s: s.activation(xcf_ap, cv_ap, AF.Silu), reads=cv_b, writes=xcf_b)
                xcb_ap, xcb_b = phG.bf(3584, THL)
                SC(lambda s: s.copy(xcb_ap, xcf_ap), reads=xcf_b, writes=xcb_b)
                GDMA(lambda s, b=b, dtile=dtile, t0=t0: s.dma_start(
                    out=xc_sp.ap()[b, dtile * 128:(dtile + 1) * 128, t0:t0 + THL], in_=xcf_ap),
                    reads=xcf_b, writes=[bB(("xc_sp", b, dtile, th))])
                GDMA(lambda s, b=b, dtile=dtile, t0=t0: s.dma_start(
                    out=xcbf_sp.ap()[b, dtile * 128:(dtile + 1) * 128, t0:t0 + THL], in_=xcb_ap),
                    reads=xcb_b, writes=[bB(("xcbf_sp", b, dtile, th))])

        for tp in range(T // 256):
            ev_last = []
            for dtile in range(NT_D):
                xt = (xcb_t0 if dtile % 2 == 0 else xcb_t1).bitcast(BF16)[:, 0:256]
                xt_bu = bxcb0 if dtile % 2 == 0 else bxcb1
                ev_lx = DMA(lambda s, b=b, dtile=dtile, tp=tp, xt=xt: s.dma_start(
                    out=xt, in_=xcbf_sp.ap()[b, dtile * 128:(dtile + 1) * 128,
                                             tp * 256:(tp + 1) * 256]),
                    reads=[bB(("xcbf_sp", b, dtile, (tp * 256) // THL))], writes=xt_bu)

                def mmxp(te, xt=xt, dtile=dtile):
                    for tt in range(2):
                        inst = te.matmul(pb[4 + tt][:, 0:E], xt[:, tt * 128:(tt + 1) * 128],
                                         xpwT_sb[:, dtile, :],
                                         start=(dtile == 0), stop=(dtile == NT_D - 1))
                    return inst
                ev_mm = TE(mmxp, reads=xt_bu + bxpw, writes=[bpb[4], bpb[5]],
                           after=[ev_lx, ev_xpws])
                ev_last = [ev_mm]
            for tt in range(2):
                stg = (ev0_sb if tt == 0 else ev1_sb)[:, 0:E]
                stg_bu = bev0 if tt == 0 else bev1
                VE(lambda v, tt=tt, stg=stg: v.tensor_copy(stg, pb[4 + tt][:, 0:E]),
                   reads=[bpb[4 + tt]], writes=stg_bu, after=ev_last)
                GDMA(lambda s, b=b, tp=tp, tt=tt, stg=stg: s.dma_start(
                    out=ssm_in[b].ap()[tp * 256 + tt * 128:tp * 256 + (tt + 1) * 128, :], in_=stg),
                    reads=stg_bu, writes=[bB(("ssm_in", b))])
        ev_ar = CC(lambda g, b=b: g.collective_compute(
            "AllReduce", ALU.add, replica_groups=[list(range(NC))],
            ins=[ssm_in[b].ap().opt()], outs=[ssm_out[b].ap().opt()]),
            reads=[bB(("ssm_in", b))], writes=[bB(("ssm_out", b))])

        dtrT_ap, dtrT_b = phJ.bf(0, RT * T)
        dtrT = dtrT_ap.rearrange("p (r t) -> p r t", t=T)
        for tt in range(T // 128):
            st_ap, st_b = phH.f((tt % 2) * 512, (tt % 2) * 512 + E)
            DMA(lambda s, b=b, tt=tt, st=st_ap: s.dma_start(
                out=st, in_=ssm_out[b].ap()[tt * 128:(tt + 1) * 128, :]),
                reads=[bB(("ssm_out", b))], writes=st_b, after=[ev_ar])
            for (off, w, ri) in ((0, R, 0), (R, NST, 1), (R + NST, NST, 2)):
                SC(lambda s, st=st_ap, off=off, w=w: s.activation(
                    junk_sb[:, 0:w], st[:, off:off + w], AF.Square, accum_out=s2_t),
                    reads=st_b, writes=bs2 + bjunk)
                VE(lambda v, w=w, ri=ri: v.tensor_scalar(rho_t[:, ri:ri + 1], s2_t,
                                                          1.0 / w, c["EPS_RMS_A"],
                                                          ALU.mult, ALU.add),
                   reads=bs2, writes=brho)
                SC(lambda s, ri=ri: s.activation(rho_t[:, ri:ri + 1], rho_t[:, ri:ri + 1], AF.Ln),
                   reads=brho, writes=brho)
                SC(lambda s, ri=ri: s.activation(rho_t[:, ri:ri + 1], rho_t[:, ri:ri + 1],
                                                 AF.Exp, scale=-0.5),
                   reads=brho, writes=brho)
            dtr_ap, dtr_b = phH.f(1024, 1024 + R)
            VE(lambda v, st=st_ap: v.tensor_scalar(dtr_ap, st[:, 0:R], rho_t[:, 0:1], None, ALU.mult),
               reads=st_b + brho, writes=dtr_b)
            VE(lambda v: v.tensor_tensor(dtr_ap, dtr_ap, dtln_bc, ALU.mult),
               reads=dtr_b + bdtln, writes=dtr_b, after=[ev_dtln])
            bc_ap, bc_b = phH.f(1536, 1536 + 2 * NST)
            brow, crow = bc_ap[:, 0:NST], bc_ap[:, NST:2 * NST]
            VE(lambda v, st=st_ap: v.tensor_scalar(brow, st[:, R:R + NST], rho_t[:, 1:2],
                                                   None, ALU.mult),
               reads=st_b + brho, writes=bc_b)
            VE(lambda v: v.tensor_tensor(brow, brow, Bln_bc, ALU.mult),
               reads=bc_b + bBln, writes=bc_b, after=[ev_Bln])
            VE(lambda v, st=st_ap: v.tensor_scalar(crow, st[:, R + NST:R + 2 * NST], rho_t[:, 2:3],
                                                   None, ALU.mult),
               reads=st_b + brho, writes=bc_b)
            VE(lambda v: v.tensor_tensor(crow, crow, Cln_bc, ALU.mult),
               reads=bc_b + bCln, writes=bc_b, after=[ev_Cln])
            for rc in range(RT):
                w = min(128, R - rc * 128)
                ev_tp = TE(lambda te, rc=rc, w=w: te.transpose(
                    pb[7][0:w, 0:128], dtr_ap[:, rc * 128:rc * 128 + w], eyef_t[:, :]),
                    reads=dtr_b + [beyef], writes=[bpb[7]], after=[ev_eyef])
                SC(lambda s, rc=rc, tt=tt, w=w: s.copy(dtrT[0:w, rc, tt * 128:(tt + 1) * 128],
                                                       pb[7][0:w, 0:128]),
                   reads=[bpb[7]], writes=dtrT_b, after=[ev_tp])
            for (row, dst) in ((brow, Bt_i), (crow, Ct_i)):
                ev_tp = TE(lambda te, row=row: te.transpose(pb[7][0:NST, 0:128], row, eyef_t[:, :]),
                           reads=bc_b + [beyef], writes=[bpb[7]], after=[ev_eyef])
                VE(lambda v: v.tensor_copy(btct[0:NST, 0:128], pb[7][0:NST, 0:128]),
                   reads=[bpb[7]], writes=bbtct, after=[ev_tp])
                GDMA(lambda s, dst=dst, b=b, tt=tt: s.dma_start(
                    out=dst.ap()[b, :, tt * 128:(tt + 1) * 128], in_=btct[0:NST, 0:128]),
                    reads=bbtct, writes=[bB((dst.name, b))])

        for dtile in range(NT_D):
            for th in range(2):
                dtp_ap, dtp_b = phB.f(0, THL)
                for sub in range(THL // TC):
                    tcg = th * (THL // TC) + sub

                    def mmdt(te, dtile=dtile, tcg=tcg):
                        for rc in range(RT):
                            w = min(128, R - rc * 128)
                            inst = te.matmul(
                                pb[6][:, 0:TC],
                                dtpwT_t[0:w, rc * DSL + dtile * 128:rc * DSL + (dtile + 1) * 128],
                                dtrT[0:w, rc, tcg * TC:(tcg + 1) * TC],
                                start=(rc == 0), stop=(rc == RT - 1))
                        return inst
                    ev_mm = TE(mmdt, reads=dtrT_b + [bdtpw], writes=[bpb[6]], after=[ev_dtpws])
                    VE(lambda v, sub=sub, dtile=dtile: v.tensor_scalar(
                        dtp_ap[:, sub * TC:(sub + 1) * TC], pb[6][:, 0:TC],
                        dtb_sb[:, dtile:dtile + 1], None, ALU.add),
                        reads=[bpb[6]] + bdtb, writes=dtp_b, after=[ev_mm, ev_dtb])
                sp0_ap, sp0_b = phB.f(1024, 1024 + THL)
                sp1_ap, sp1_b = phB.f(2048, 2048 + THL)
                VE(lambda v: v.tensor_scalar_max(sp0_ap, dtp_ap, 0.0), reads=dtp_b, writes=sp0_b)
                SC(lambda s: s.activation(sp1_ap, dtp_ap, AF.Abs), reads=dtp_b, writes=sp1_b)
                SC(lambda s: s.activation(sp1_ap, sp1_ap, AF.Exp, scale=-1.0),
                   reads=sp1_b, writes=sp1_b)
                VE(lambda v: v.tensor_scalar_add(sp1_ap, sp1_ap, 1.0),
                   reads=sp1_b, writes=sp1_b)
                SC(lambda s: s.activation(sp1_ap, sp1_ap, AF.Ln),
                   reads=sp1_b, writes=sp1_b)
                VE(lambda v: v.tensor_add(dtp_ap, sp0_ap, sp1_ap),
                   reads=sp0_b + sp1_b, writes=dtp_b)
                GDMA(lambda s, b=b, dtile=dtile, th=th: s.dma_start(
                    out=dt_sp.ap()[b, dtile * 128:(dtile + 1) * 128, th * THL:(th + 1) * THL],
                    in_=dtp_ap),
                    reads=dtp_b, writes=[bB(("dt_sp", b, dtile, th))])

    def emit_P3(b):
        for th in range(2):
            for blk in range(NBLK):
                dts = [blk * DTBLK + j for j in range(DTBLK)]
                for j, dtile in enumerate(dts):
                    dtt_ap, dtt_b = phC.f(j * THL, (j + 1) * THL)
                    SDMA(lambda s, b=b, dtile=dtile, th=th, d=dtt_ap: s.dma_start(
                        out=d, in_=dt_sp.ap()[b, dtile * 128:(dtile + 1) * 128,
                                              th * THL:(th + 1) * THL]),
                        reads=[bB(("dt_sp", b, dtile, th))], writes=dtt_b)
                    xct_ap, xct_b = phF.f(0, THL)
                    SDMA(lambda s, b=b, dtile=dtile, th=th, d=xct_ap: s.dma_start(
                        out=d, in_=xc_sp.ap()[b, dtile * 128:(dtile + 1) * 128,
                                              th * THL:(th + 1) * THL]),
                        reads=[bB(("xc_sp", b, dtile, th))], writes=xct_b)
                    w_ap, w_b = phD.f(j * THL, (j + 1) * THL)
                    VE(lambda v, d=dtt_ap, x=xct_ap, w=w_ap: v.tensor_tensor(w, d, x, ALU.mult),
                       reads=dtt_b + xct_b, writes=w_b)
                pend_acc = []

                def flush_acc():
                    while pend_acc:
                        fnp, rd, wr = pend_acc.pop(0)
                        VE(fnp, reads=rd, writes=wr)
                pipes = (phF, phK, phM)
                itc = [0]
                for n in range(NST):
                    bct = phI if n % 2 == 0 else phL
                    bcB_ap, bcB_b = bct.f(0, THL)
                    bcC_ap, bcC_b = bct.f(1024, 1024 + THL)
                    SDMA(lambda s, b=b, n=n, th=th, d=bcB_ap: s.dma_start(
                        out=d, in_=Bt_i.ap()[b, n, th * THL:(th + 1) * THL]
                        .unsqueeze(0).broadcast_to([128, THL])),
                        reads=[bB(("Bt_i", b))], writes=bcB_b)
                    SDMA(lambda s, b=b, n=n, th=th, d=bcC_ap: s.dma_start(
                        out=d, in_=Ct_i.ap()[b, n, th * THL:(th + 1) * THL]
                        .unsqueeze(0).broadcast_to([128, THL])),
                        reads=[bB(("Ct_i", b))], writes=bcC_b)
                    for j, dtile in enumerate(dts):
                        dtt_ap, dtt_b = phC.f(j * THL, (j + 1) * THL)
                        w_ap, w_b = phD.f(j * THL, (j + 1) * THL)
                        pipet = pipes[itc[0] % 3]
                        itc[0] += 1
                        pA_ap, pA_b = pipet.f(0, THL)
                        pB_ap, pB_b = pipet.f(1024, 1024 + THL)
                        pP_ap, pP_b = pipet.f(2048, 2048 + THL)
                        SC(lambda s, d=dtt_ap, dtile=dtile, n=n, o=pA_ap: s.activation(
                            o, d, AF.Exp, scale=A_sb[:, dtile * NST + n:dtile * NST + n + 1]),
                            reads=dtt_b + bA_sb, writes=pA_b, after=[ev_A])
                        VE(lambda v, w=w_ap, o=pB_ap, bc=bcB_ap: v.tensor_tensor(o, w, bc, ALU.mult),
                           reads=w_b + bcB_b, writes=pB_b)
                        if th == 0:
                            ini = 0.0
                            rd_ini = []
                        else:
                            ini = sstate[:, dtile * NST + n:dtile * NST + n + 1]
                            rd_ini = bsst
                        VE(lambda v, a=pA_ap, x=pB_ap, ini=ini: v.tensor_tensor_scan(
                            x, a, x, ini, ALU.mult, ALU.add),
                            reads=pA_b + pB_b + rd_ini, writes=pB_b)
                        if th == 0:
                            VE(lambda v, dtile=dtile, n=n, x=pB_ap: v.tensor_copy(
                                sstate[:, dtile * NST + n:dtile * NST + n + 1],
                                x[:, THL - 1:THL]),
                                reads=pB_b, writes=bsst)
                        GP(lambda g, x=pB_ap, o=pP_ap, bc=bcC_ap: g.tensor_tensor(o, x, bc, ALU.mult),
                           reads=pB_b + bcC_b, writes=pP_b)
                        ya_ap, ya_b = phE.f(j * THL, (j + 1) * THL)
                        if n == 0:
                            fnp = (lambda v, ya=ya_ap, o=pP_ap: v.tensor_copy(ya, o))
                        else:
                            fnp = (lambda v, ya=ya_ap, o=pP_ap: v.tensor_add(ya, ya, o))
                        pend_acc.append((fnp, pP_b + (ya_b if n > 0 else []), ya_b))
                        if len(pend_acc) > 1:
                            fnp, rd, wr = pend_acc.pop(0)
                            VE(fnp, reads=rd, writes=wr)
                flush_acc()
                for j, dtile in enumerate(dts):
                    ya_ap, ya_b = phE.f(j * THL, (j + 1) * THL)
                    xct_ap, xct_b = phF.f(0, THL)
                    zt_ap, zt_b = phF.f(2048, 2048 + THL)
                    SDMA(lambda s, b=b, dtile=dtile, th=th, d=xct_ap: s.dma_start(
                        out=d, in_=xc_sp.ap()[b, dtile * 128:(dtile + 1) * 128,
                                              th * THL:(th + 1) * THL]),
                        reads=[bB(("xc_sp", b, dtile, th))], writes=xct_b)
                    SDMA(lambda s, b=b, dtile=dtile, th=th, d=zt_ap: s.dma_start(
                        out=d, in_=z_sp.ap()[b, dtile * 128:(dtile + 1) * 128,
                                             th * THL:(th + 1) * THL]),
                        reads=[bB(("z_sp", b, dtile, tci))
                               for tci in range((th * THL) // TC, ((th + 1) * THL) // TC)],
                        writes=zt_b)
                    tf_ap, tf_b = phF.f(1024, 1024 + THL)
                    VE(lambda v, x=xct_ap, dtile=dtile, ya=ya_ap, o=tf_ap: v.scalar_tensor_tensor(
                        o, x, D_sb[:, dtile:dtile + 1], ya, ALU.mult, ALU.add),
                        reads=xct_b + ya_b + bD_sb, writes=tf_b, after=[ev_D])
                    SC(lambda s, z=zt_ap: s.activation(z, z, AF.Silu), reads=zt_b, writes=zt_b)
                    yb_ap, yb_b = phF.bf(0, THL)
                    VE(lambda v, t=tf_ap, z=zt_ap, o=yb_ap: v.tensor_tensor(o, t, z, ALU.mult),
                       reads=tf_b + zt_b, writes=yb_b)
                    pset = 0 if b < 2 else 1
                    for qq in range(NQB):
                        qglob = th * NQB + qq
                        shard = 4 * (b % 2) + qglob
                        DMA(lambda s, pset=pset, shard=shard, dtile=dtile, qq=qq, y=yb_ap:
                            s.dma_start(out=y_in[pset].ap()[shard, dtile * 128:(dtile + 1) * 128, :],
                                        in_=y[:, qq * QT:(qq + 1) * QT]),
                            reads=yb_b, writes=[bB(("y_in", pset))])
    def emit_A2A(pset):
        CC(lambda g, pset=pset: g.collective_compute(
            "AllToAll", ALU.bypass, replica_groups=[list(range(NC))],
            ins=[y_in[pset].ap().opt()], outs=[y_my[pset].ap().opt()]),
            reads=[bB(("y_in", pset))], writes=[bB(("y_my", pset))])


    emit_P2(0)
    emit_P2(1)
    emit_P3(0)
    ev_ag_wout = emit_ag_wout()
    emit_P2(2)
    emit_P3(1)
    emit_A2A(0)
    emit_P2(3)
    emit_P3(2)
    emit_P3(3)
    emit_A2A(1)

    WOW = min(2048, H)
    NHH = H // WOW
    NHC = WOW // 512

    def clamp_ga(tile, bufs, ev_after):
        evs = ev_after
        CW = min(1024, H)
        for hf in range(H // CW):
            o0 = hf * CW
            m_ap, m_b = phI.f(0, CW)
            hi_ap, hi_b = phI.f(1024, 1024 + CW)
            lo_ap, lo_b = phJ.f(0, CW)
            g_ap = tile.t[:, o0:o0 + CW]
            VE(lambda v, g=g_ap, m=m_ap: v.tensor_single_scalar(m, g, 0.0, ALU.is_ge),
               reads=bufs, writes=m_b, after=[evs])
            VE(lambda v, g=g_ap, hi=hi_ap: v.tensor_scalar_max(hi, g, 0.25),
               reads=bufs, writes=hi_b)
            VE(lambda v, g=g_ap, lo=lo_ap: v.tensor_scalar_min(lo, g, -0.25),
               reads=bufs, writes=lo_b)
            VE(lambda v, hi=hi_ap, lo=lo_ap: v.tensor_sub(hi, hi, lo),
               reads=hi_b + lo_b, writes=hi_b)
            VE(lambda v, hi=hi_ap, m=m_ap: v.tensor_mul(hi, hi, m),
               reads=hi_b + m_b, writes=hi_b)
            evs = VE(lambda v, g=g_ap, lo=lo_ap, hi=hi_ap: v.tensor_add(g, lo, hi),
                     reads=lo_b + hi_b, writes=bufs)
        return evs

    for ph in range(2):
        ev_ga0 = bcast_mod(phG, 0, ph, 2, 0, H)
        ga_ap, ga_b = phG.f(0, H)
        ev_ga = clamp_ga(phG, ga_b, ev_ga0)
        for tpair in range(NT_Q // 2):
            xin_t = [phC, phD]
            for tt in range(2):
                ttg = tpair * 2 + tt
                xi_ap, xi_b = xin_t[tt].f(0, H)
                SDMA(lambda s, ph=ph, ttg=ttg, d=xi_ap: s.dma_start(
                    out=d, in_=x_my.ap()[ph, ttg * 128:(ttg + 1) * 128, :]), writes=xi_b)
            for hh in range(NHH):
                ev_mmo = None
                yts = [xcb_t0, xcb_t1, xcb_t2, xcb_t3]
                ytb = [bxcb0, bxcb1, bxcb2, bxcb3]
                for k in range(KD):
                    wtile = phB if k % 2 == 0 else phB2
                    wlo = 0 if (k // 2) % 2 == 0 else 2048
                    wo_ap, wo_b = wtile.bf(wlo, WOW)
                    ev_wo = (SDMA if k % 2 == 0 else DMA)(lambda s, k=k, hh=hh, w=wo_ap: s.dma_start(
                        out=w, in_=wout_all.ap()[k // NT_D, (k % NT_D) * 128:(k % NT_D) * 128 + 128,
                                                 hh * WOW:(hh + 1) * WOW]),
                        reads=[bB("wout_all")], writes=wo_b, after=[ev_ag_wout])
                    yt = yts[k % 4].bitcast(BF16)[:, 0:256]
                    yt_bu = ytb[k % 4]
                    ev_yt = (DMA if k % 2 == 0 else SDMA)(lambda s, ph=ph, k=k, tpair=tpair, y=yt: s.dma_start(
                        out=y, in_=y_my[ph].ap()[k // NT_D, (k % NT_D) * 128:(k % NT_D) * 128 + 128,
                                                 tpair * 256:(tpair + 1) * 256]),
                        reads=[bB(("y_my", ph))], writes=yt_bu)

                    def mmop(te, yt=yt, wo=wo_ap, k=k):
                        for tt in range(2):
                            for hc in range(NHC):
                                inst = te.matmul(pb[tt * NHC + hc][:, 0:512],
                                                 yt[:, tt * 128:(tt + 1) * 128],
                                                 wo[:, hc * 512:(hc + 1) * 512],
                                                 start=(k == 0), stop=(k == KD - 1))
                        return inst
                    ev_mmo = TE(mmop, reads=wo_b + yt_bu,
                                writes=[bpb[tt * NHC + hc] for tt in range(2) for hc in range(NHC)],
                                after=[ev_wo, ev_yt])
                for tt in range(2):
                    x2t = phA if tt == 0 else phE
                    x2_ap, x2_b = x2t.f(0, H)
                    xi_ap, xi_b = xin_t[tt].f(0, H)
                    for hc in range(NHC):
                        co = hh * WOW + hc * 512
                        bank = tt * NHC + hc
                        VE(lambda v, bank=bank, co=co, o=x2_ap: v.tensor_tensor(
                            o[:, co:co + 512], pb[bank][:, 0:512], ga_ap[:, co:co + 512], ALU.mult),
                            reads=[bpb[bank]] + ga_b, writes=x2_b, after=[ev_ga, ev_mmo])
                        VE(lambda v, co=co, o=x2_ap, xi=xi_ap: v.tensor_add(
                            o[:, co:co + 512], o[:, co:co + 512], xi[:, co:co + 512]),
                            reads=xi_b + x2_b, writes=x2_b)
            for tt in range(2):
                ttg = tpair * 2 + tt
                x2t = phA if tt == 0 else phE
                x2_ap, x2_b = x2t.f(0, H)
                GDMA(lambda s, ph=ph, ttg=ttg, i=x2_ap: s.dma_start(
                    out=x2_sp.ap()[ph, ttg * 128:(ttg + 1) * 128, :], in_=i),
                    reads=x2_b, writes=[bB(("x2_sp", ph, ttg))])
                VE(lambda v, i=x2_ap: v.reduce_sum(s1_t, i, AX.X), reads=x2_b, writes=bs1)
                dum_ap, dum_b = phD.f(0, H)
                SC(lambda s, i=x2_ap, d=dum_ap: s.activation(d, i, AF.Square, accum_out=s2_t),
                   reads=x2_b, writes=bs2 + dum_b)
                VE(lambda v: v.tensor_scalar_mul(mu_t, s1_t, 1.0 / H), reads=bs1, writes=bmu)
                VE(lambda v: v.tensor_tensor(var_t, mu_t, mu_t, ALU.mult), reads=bmu, writes=bvar)
                VE(lambda v: v.tensor_scalar(junk_sb[:, 0:1], s2_t, 1.0 / H, None, ALU.mult),
                   reads=bs2, writes=bjunk)
                VE(lambda v: v.tensor_sub(var_t, junk_sb[:, 0:1], var_t),
                   reads=bjunk + bvar, writes=bvar)
                VE(lambda v: v.tensor_scalar_add(var_t, var_t, c["EPS_LN"]),
                   reads=bvar, writes=bvar)
                SC(lambda s: s.activation(rstd_t, var_t, AF.Ln),
                   reads=bvar, writes=brstd)
                SC(lambda s: s.activation(rstd_t, rstd_t, AF.Exp, scale=-0.5),
                   reads=brstd, writes=brstd)
                hn_ap, hn_b = phD.f(0, H)
                VE(lambda v, i=x2_ap, o=hn_ap: v.tensor_scalar(o, i, mu_t, rstd_t,
                                                               ALU.subtract, ALU.mult),
                   reads=x2_b + bmu + brstd, writes=hn_b)
                h2_ap, h2_b = phC.f(0, H)
                for hf in range(2):
                    o0 = hf * (H // 2)
                    cb_ap, cb_b = phI.f(0, H // 2)
                    ev_cb = bcast_mod(phI, 0, ph, 4, o0, o0 + H // 2)
                    VE(lambda v, a=cb_ap: v.tensor_scalar_add(a, a, 1.0),
                       reads=cb_b, writes=cb_b, after=[ev_cb])
                    sb_ap, sb_b = phJ.f(0, H // 2)
                    ev_sb = bcast_mod(phJ, 0, ph, 3, o0, o0 + H // 2)
                    VE(lambda v, o0=o0, h2=h2_ap, hn=hn_ap, a=cb_ap: v.tensor_mul(
                        h2[:, o0:o0 + H // 2], hn[:, o0:o0 + H // 2], a),
                        reads=hn_b + cb_b, writes=h2_b)
                    VE(lambda v, o0=o0, h2=h2_ap, a=sb_ap: v.tensor_add(
                        h2[:, o0:o0 + H // 2], h2[:, o0:o0 + H // 2], a),
                        reads=sb_b + h2_b, writes=h2_b, after=[ev_sb])
                GDMA(lambda s, ph=ph, ttg=ttg, i=h2_ap: s.dma_start(
                    out=h2_sp.ap()[ph, ttg * 128:(ttg + 1) * 128, :], in_=i),
                    reads=h2_b, writes=[bB(("h2_sp", ph, ttg))])
                SC(lambda s, i=h2_ap, d=dum_ap: s.activation(d, i, AF.Square, accum_out=s2_t),
                   reads=h2_b, writes=bs2 + dum_b)
                VE(lambda v: v.tensor_scalar(rstd_t, s2_t, 1.0 / H, c["EPS_RMS_F"],
                                             ALU.mult, ALU.add),
                   reads=bs2, writes=brstd)
                SC(lambda s: s.activation(rstd_t, rstd_t, AF.Ln),
                   reads=brstd, writes=brstd)
                SC(lambda s: s.activation(rstd_t, rstd_t, AF.Exp, scale=-0.5),
                   reads=brstd, writes=brstd)
                xn_ap, xn_b = phD.f(0, H)
                VE(lambda v, i=h2_ap, o=xn_ap: v.tensor_scalar(o, i, rstd_t, None, ALU.mult),
                   reads=h2_b + brstd, writes=xn_b)
                xnT_ap, xnT_b = phB.bf(0, KH * 128)
                xnT = xnT_ap.rearrange("p (cc q) -> p cc q", q=128)
                for kc in range(KH):
                    ev_tp = TE(lambda te, kc=kc, i=xn_ap: te.transpose(
                        pb[7][:, 0:128], i[:, kc * 128:(kc + 1) * 128], eyef_t[:, :]),
                        reads=xn_b + [beyef], writes=[bpb[7]], after=[ev_eyef])
                    SC(lambda s, kc=kc: s.copy(xnT[:, kc, :], pb[7][:, 0:128]),
                       reads=[bpb[7]], writes=xnT_b, after=[ev_tp])

                def mmlg(te, xnT=xnT):
                    for kc in range(KH):
                        inst = te.matmul(pb[6][:, 0:24], xnT[:, kc, :], wres_tiles[:, kc, :],
                                         start=(kc == 0), stop=(kc == KH - 1))
                    return inst
                ev_lg = TE(mmlg, reads=xnT_b + bwrs, writes=[bpb[6]], after=[ev_wres_ld])
                gidx = ph * NT_Q + ttg
                VE(lambda v, gidx=gidx: v.tensor_tensor(
                    logits_sb.rearrange("p (g e) -> p g e", e=24)[:, gidx, :],
                    pb[6][:, 0:24], base_bc, ALU.add),
                    reads=[bpb[6]] + bbase, writes=blg, after=[ev_base, ev_lg])

    lg3 = logits_sb.rearrange("p (g e) -> p g e", e=24)
    Z4 = Z_sb.rearrange("p (g i j) -> p g i j", i=4, j=4)
    ZT4 = ZT_sb.rearrange("p (g j i) -> p g j i", j=4, i=4)
    u4 = u_sb.rearrange("p (g i) -> p g i", i=4)
    v4 = v_sb.rearrange("p (g j) -> p g j", j=4)
    m4 = mx_sb.rearrange("p (g i) -> p g i", i=4)
    s4 = snk_sb.rearrange("p (g i j) -> p g i j", i=4, j=4)
    VE(lambda v: v.tensor_scalar_mul(Z_sb.rearrange("p (g e) -> p g e", e=16),
                                     lg3[:, :, 0:16], 1.0 / TAU),
       reads=blg, writes=bZ)
    VE(lambda v: v.tensor_copy(ZT4, Z4.transpose([0, 1, 3, 2])),
       reads=bZ, writes=bZT)
    VE(lambda v: v.memset(u_sb, 0.0), writes=bu)
    VE(lambda v: v.memset(v_sb, 0.0), writes=bv)
    for it in range(ITERS):
        for (src_b, src_sb, dst_b, dst_sb, oth4) in (
                (bZ, Z_sb, bu, u_sb, v4), (bZT, ZT_sb, bv, v_sb, u4)):
            oth_b = bv if dst_sb is u_sb else bu
            VE(lambda v, s_=src_sb, o4=oth4: v.tensor_tensor(
                s4, s_.rearrange("p (g a b) -> p g a b", a=4, b=4),
                o4.unsqueeze(2).broadcast_to([128, G, 4, 4]), ALU.add),
                reads=src_b + oth_b, writes=bsnk)
            VE(lambda v: v.reduce_max(mx_sb, s4, AX.X), reads=bsnk, writes=bmx)
            VE(lambda v: v.tensor_tensor(s4, s4,
                                         m4.unsqueeze(3).broadcast_to([128, G, 4, 4]),
                                         ALU.subtract),
               reads=bsnk + bmx, writes=bsnk)
            SC(lambda s: s.activation(snk_sb, snk_sb, AF.Exp), reads=bsnk, writes=bsnk)
            VE(lambda v: v.reduce_sum(junk_sb[:, 0:G * 4], s4, AX.X),
               reads=bsnk, writes=bjunk)
            SC(lambda s: s.activation(junk_sb[:, 0:G * 4], junk_sb[:, 0:G * 4], AF.Ln),
               reads=bjunk, writes=bjunk)
            VE(lambda v: v.tensor_add(junk_sb[:, 0:G * 4], junk_sb[:, 0:G * 4], mx_sb),
               reads=bjunk + bmx, writes=bjunk)
            VE(lambda v, d=dst_sb: v.tensor_scalar_mul(d, junk_sb[:, 0:G * 4], -1.0),
               reads=bjunk, writes=dst_b)
    VE(lambda v: v.tensor_tensor(Hres_sb.rearrange("p (g a b) -> p g a b", a=4, b=4),
                                 Z4, u4.unsqueeze(3).broadcast_to([128, G, 4, 4]), ALU.add),
       reads=bZ + bu, writes=bHr)
    VE(lambda v: v.tensor_tensor(Hres_sb.rearrange("p (g a b) -> p g a b", a=4, b=4),
                                 Hres_sb.rearrange("p (g a b) -> p g a b", a=4, b=4),
                                 v4.unsqueeze(2).broadcast_to([128, G, 4, 4]), ALU.add),
       reads=bHr + bv, writes=bHr)
    SC(lambda s: s.activation(Hres_sb, Hres_sb, AF.Exp), reads=bHr, writes=bHr)
    VE(lambda v: v.tensor_scalar(Hres_sb, Hres_sb, g_bc[:, 0:1], None, ALU.mult),
       reads=bHr + bg, writes=bHr, after=[ev_g5])
    ev_hres = VE(lambda v: v.scalar_tensor_tensor(
        Hres_sb.rearrange("p (g e) -> p g e", e=NST),
        eye4_bc.unsqueeze(1).broadcast_to([128, G, NST]),
        gm1_bc[:, 0:1], Hres_sb.rearrange("p (g e) -> p g e", e=NST), ALU.mult, ALU.add),
        reads=bHr + beye4 + bgm1, writes=bHr, after=[ev_eye4, ev_g6])
    for (lo, dst_sb, dst_b) in ((16, Hpre_sb, bHp), (20, Hpost_sb, bHq)):
        s3 = snk_sb[:, 0:G * 4].rearrange("p (g e) -> p g e", e=4)
        VE(lambda v, lo=lo: v.tensor_copy(
            snk_sb[:, 0:G * 4].rearrange("p (g e) -> p g e", e=4), lg3[:, :, lo:lo + 4]),
           reads=blg, writes=bsnk)
        VE(lambda v, s3=s3: v.reduce_max(mx_sb[:, 0:G], s3, AX.X), reads=bsnk, writes=bmx)
        VE(lambda v: v.tensor_tensor(
            snk_sb[:, 0:G * 4].rearrange("p (g e) -> p g e", e=4),
            snk_sb[:, 0:G * 4].rearrange("p (g e) -> p g e", e=4),
            mx_sb[:, 0:G].rearrange("p (g o) -> p g o", o=1).broadcast_to([128, G, 4]),
            ALU.subtract),
           reads=bsnk + bmx, writes=bsnk)
        SC(lambda s: s.activation(snk_sb[:, 0:G * 4], snk_sb[:, 0:G * 4], AF.Exp),
           reads=bsnk, writes=bsnk)
        VE(lambda v, s3=s3: v.reduce_sum(t4_sb[:, 0:G], s3, AX.X), reads=bsnk, writes=bt4)
        VE(lambda v: v.reciprocal(t4_sb[:, G:2 * G], t4_sb[:, 0:G]), reads=bt4, writes=bt4)
        VE(lambda v, d=dst_sb: v.tensor_tensor(
            d.rearrange("p (g e) -> p g e", e=4),
            snk_sb[:, 0:G * 4].rearrange("p (g e) -> p g e", e=4),
            t4_sb[:, G:2 * G].rearrange("p (g o) -> p g o", o=1).broadcast_to([128, G, 4]),
            ALU.mult),
           reads=bsnk + bt4, writes=dst_b)

    Hr3 = Hres_sb.rearrange("p (g e) -> p g e", e=NST)
    Hp3 = Hpre_sb.rearrange("p (g e) -> p g e", e=4)
    Hq3 = Hpost_sb.rearrange("p (g e) -> p g e", e=4)
    for ph in range(2):
        ev_gb0 = bcast_mod(phG, 0, ph, 5, 0, H)
        gb_ap, gb_b = phG.f(0, H)
        ev_gb = clamp_ga(phG, gb_b, ev_gb0)
        for ttg in range(NT_Q):
            gidx = ph * NT_Q + ttg
            h2_ap, h2_b = phB.f(0, H)
            DMA(lambda s, ph=ph, ttg=ttg, d=h2_ap: s.dma_start(
                out=d, in_=h2_sp.ap()[ph, ttg * 128:(ttg + 1) * 128, :]),
                reads=[bB(("h2_sp", ph, ttg))], writes=h2_b)
            x2_ap, x2_b = phB2.f(0, H)
            DMA(lambda s, ph=ph, ttg=ttg, d=x2_ap: s.dma_start(
                out=d, in_=x2_sp.ap()[ph, ttg * 128:(ttg + 1) * 128, :]),
                reads=[bB(("x2_sp", ph, ttg))], writes=x2_b)
            xp0_ap, xp0_b = phD.f(2048, 2048 + CD)
            xp1_ap, xp1_b = phD.f(2048 + CD, 2048 + 2 * CD) if CD <= 1024 else phD.f(2048, 2048 + CD)
            VE(lambda v, gidx=gidx: v.tensor_scalar(xp0_ap, h2_ap[:, 0:CD],
                                                    Hp3[:, gidx, 0:1], None, ALU.mult),
               reads=h2_b + bHp, writes=xp0_b)
            cur_ap, cur_b, oth_ap, oth_b = xp0_ap, xp0_b, xp1_ap, xp1_b
            for n in range(1, NSTR):
                VE(lambda v, n=n, cu=cur_ap, ot=oth_ap, gidx=gidx: v.scalar_tensor_tensor(
                    ot, h2_ap[:, n * CD:(n + 1) * CD], Hp3[:, gidx, n:n + 1], cu,
                    ALU.mult, ALU.add),
                    reads=h2_b + bHp + cur_b, writes=oth_b)
                cur_ap, cur_b, oth_ap, oth_b = oth_ap, oth_b, cur_ap, cur_b
            xm_ap, xm_b = phC.f(0, H)
            for i in range(NSTR):
                VE(lambda v, i=i, gidx=gidx: v.tensor_scalar(
                    xm_ap[:, i * CD:(i + 1) * CD], h2_ap[:, 0:CD],
                    Hr3[:, gidx, i * 4:i * 4 + 1], None, ALU.mult),
                    reads=h2_b + bHr, writes=xm_b, after=[ev_hres])
                for jj in range(1, NSTR):
                    VE(lambda v, i=i, jj=jj, gidx=gidx: v.scalar_tensor_tensor(
                        xm_ap[:, i * CD:(i + 1) * CD], h2_ap[:, jj * CD:(jj + 1) * CD],
                        Hr3[:, gidx, i * 4 + jj:i * 4 + jj + 1],
                        xm_ap[:, i * CD:(i + 1) * CD], ALU.mult, ALU.add),
                        reads=h2_b + bHr + xm_b, writes=xm_b)
            xpT_ap, xpT_b = phD.bf(0, KCD * 128)
            xpT = xpT_ap.rearrange("p (k q) -> p k q", q=128)
            for kk in range(KCD):
                ev_tp = TE(lambda te, kk=kk, cu=cur_ap: te.transpose(
                    pb[7][:, 0:128], cu[:, kk * 128:(kk + 1) * 128], eyef_t[:, :]),
                    reads=cur_b + [beyef], writes=[bpb[7]], after=[ev_eyef])
                SC(lambda s, kk=kk: s.copy(xpT[:, kk, :], pb[7][:, 0:128]),
                   reads=[bpb[7]], writes=xpT_b, after=[ev_tp])
            ybr_ap, ybr_b = phE.f(0, CD)
            nwc = (CD + WBN - 1) // WBN
            for cc2 in range(nwc):
                w0 = cc2 * WBN
                wid = min(WBN, CD - w0)
                bank = 4 + cc2 % 2
                wb_ap, wb_b = phF.bf(0, KCD * wid)
                wb_tiles = wb_ap.rearrange("p (k w) -> p k w", w=wid)
                ev_wbl = DMA(lambda s, w0=w0, wid=wid, wt=wb_tiles: s.dma_start(
                    out=wt, in_=wb_bf.ap()[:, w0:w0 + wid].rearrange("(k p) w -> p k w", p=128)),
                    reads=[bB("wb_bf")], writes=wb_b, after=[ev_wb])

                def mmwb(te, w0=w0, wid=wid, bank=bank, xpT=xpT, wt=wb_tiles):
                    for kk in range(KCD):
                        inst = te.matmul(pb[bank][:, 0:wid], xpT[:, kk, :],
                                         wt[:, kk, :],
                                         start=(kk == 0), stop=(kk == KCD - 1))
                    return inst
                ev_mw = TE(mmwb, reads=xpT_b + wb_b, writes=[bpb[bank]], after=[ev_wbl])
                VE(lambda v, w0=w0, wid=wid, bank=bank: v.tensor_copy(
                    ybr_ap[:, w0:w0 + wid], pb[bank][:, 0:wid]),
                    reads=[bpb[bank]], writes=ybr_b, after=[ev_mw])
            for n in range(NSTR):
                VE(lambda v, n=n, gidx=gidx: v.scalar_tensor_tensor(
                    xm_ap[:, n * CD:(n + 1) * CD], ybr_ap, Hq3[:, gidx, n:n + 1],
                    xm_ap[:, n * CD:(n + 1) * CD], ALU.mult, ALU.add),
                    reads=ybr_b + bHq + xm_b, writes=xm_b)
            o_ap, o_b = phA.f(0, H)
            VE(lambda v: v.tensor_tensor(o_ap, xm_ap, gb_ap, ALU.mult),
               reads=xm_b + gb_b, writes=o_b, after=[ev_gb])
            ev_o = VE(lambda v: v.tensor_add(o_ap, o_ap, x2_ap), reads=x2_b + o_b, writes=o_b)
            DMA(lambda s, ph=ph, ttg=ttg, i=o_ap: s.dma_start(
                out=out_my.ap()[ph, ttg * 128:(ttg + 1) * 128, :], in_=i),
                reads=o_b, writes=[bB("out_dma")], after=[ev_o])

    if cfg.get("TAPS2"):
        def tapcp(name, t, shape, deps):
            o = nc.dram_tensor(name, list(shape), t.dtype if hasattr(t, "dtype") else F32,
                               kind="ExternalOutput")
            n = 1
            for s_ in shape:
                n *= s_
            DMA(lambda s, o=o, t=t: s.dma_start(out=o.ap().rearrange(
                " ".join("abcde"[:len(shape)]) + " -> (" + " ".join("abcde"[:len(shape)]) + ")"),
                in_=t.ap().rearrange(
                " ".join("abcde"[:len(shape)]) + " -> (" + " ".join("abcde"[:len(shape)]) + ")")),
                reads=[bB(k) for k in deps], writes=[bB(name)])
        tapcp("o_x2", x2_sp, (2, QT, H), [("x2_sp", ph_, t_) for ph_ in range(2) for t_ in range(NT_Q)])
        tapcp("o_h2", h2_sp, (2, QT, H), [("h2_sp", ph_, t_) for ph_ in range(2) for t_ in range(NT_Q)])
        tapcp("o_ssm0", ssm_out[0], (T, E), [("ssm_out", 0)])
        tapcp("o_dt", dt_sp, (B, DSL, T), [("dt_sp", b_, d_, th_) for b_ in range(B)
                                           for d_ in range(NT_D) for th_ in range(2)])
        tapcp("o_xc", xc_sp, (B, DSL, T), [("xc_sp", b_, d_, th_) for b_ in range(B)
                                           for d_ in range(NT_D) for th_ in range(2)])
        tapcp("o_xsp", x_sp, (B, DSL, T), [("x_sp", b_, d_, tc_) for b_ in range(B)
                                           for d_ in range(NT_D) for tc_ in range(NTC)])
        tapcp("o_zsp", z_sp, (B, DSL, T), [("z_sp", b_, d_, tc_) for b_ in range(B)
                                           for d_ in range(NT_D) for tc_ in range(NTC)])
        tapcp("o_Bt", Bt_i, (B, NST, T), [("Bt_i", b_) for b_ in range(B)])
        tapcp("o_Ct", Ct_i, (B, NST, T), [("Ct_i", b_) for b_ in range(B)])
        tapcp("o_ymyA", y_my[0], (NC, DSL, QT), [("y_my", 0)])
        tapcp("o_mod", mod_my, (NC, 2, MODC), ["mod_my"])
        tapcp("o_hT", hT_all, (NC, H, 2 * QT), ["hT_all"])

    if cfg.get("DEBUG_TAPS"):
        dbg = nc.dram_tensor("dbg", [128, G * 24 + 3 * G * NST + 3 * CD + 2 * H], F32,
                             kind="ExternalOutput")
        o0 = 0
        xpf_ap, xpf_b = phD.f(2048, 2048 + CD)
        xpf2_ap, xpf2_b = phD.f(2048 + CD, 2048 + 2 * CD)
        ybr2_ap, ybr2_b = phE.f(0, CD)
        xm2_ap, xm2_b = phC.f(0, H)
        for (ap_, bufs_, w_) in ((logits_sb, blg, G * 24), (Hres_sb, bHr, G * NST),
                                 (Hpre_sb, bHp, G * 4), (Hpost_sb, bHq, G * 4),
                                 (xpf_ap, xpf_b, CD), (xpf2_ap, xpf2_b, CD),
                                 (ybr2_ap, ybr2_b, CD), (xm2_ap, xm2_b, H),
                                 (phG.f(0, H)[0], phG.f(0, H)[1], H)):
            DMA(lambda s, a=ap_, o=o0, w=w_: s.dma_start(out=dbg.ap()[:, o:o + w], in_=a),
                reads=bufs_, writes=[bB("dbg")])
            o0 += w_

    fin = bB("out_dma").ready
    P.op("gp", lambda g: g.engine_nop(), after=[fin])

    P.check_deadlock()
    global _LASTPROG
    _LASTPROG = P
    semnames = sorted(P.cnt.keys())
    with contextlib.ExitStack() as sctx:
        sems = {s: sctx.enter_context(nc.semaphore(f"sem_{s}")) for s in semnames}
        with nc.Block() as block:
            with allow_ici_all_to_all():
                P.realize(sems, block)
    ctx.close()
    return nc


def _stage_inputs(inputs, c, k):
    d = derived(c)
    B, T, H, DIN, NST, KCV, R = c["B"], c["T"], c["H"], c["DIN"], c["NST"], c["KCV"], c["R"]
    DSL, MODC, QT, RT, RP, NCP = d["DSL"], d["MODC"], d["QT"], d["RT"], d["RP"], d["NCP"]
    f32 = np.float32
    x = np.asarray(inputs["L0_attn_hidden_states"], f32)
    q = k % 4
    bA, bBv = k // 4, 2 + k // 4
    x_myv = np.stack([x[bA, q * QT:(q + 1) * QT, :], x[bBv, q * QT:(q + 1) * QT, :]])
    win = np.asarray(inputs["L0_attn_in_proj_w"], f32)
    win_cat = np.concatenate([win[:, k * DSL:(k + 1) * DSL],
                              win[:, DIN + k * DSL:DIN + (k + 1) * DSL]], axis=1)
    win_tv = np.ascontiguousarray(
        win_cat.reshape(H // 128, 128, NCP, 256).transpose(2, 0, 1, 3))
    wmod = np.asarray(inputs["L0_wmod"], f32)
    return {
        "x_my": np.ascontiguousarray(x_myv),
        "cT_in": np.ascontiguousarray(np.asarray(inputs["L0_c"], f32).T),
        "wmod_sl": np.ascontiguousarray(wmod[:, k * MODC:(k + 1) * MODC]),
        "win_t": win_tv,
        "xpwT_sl": np.ascontiguousarray(
            np.asarray(inputs["L0_attn_x_proj_w"], f32)[:, k * DSL:(k + 1) * DSL].T),
        "dtpwT_sl": np.ascontiguousarray(
            np.asarray(inputs["L0_attn_dt_proj_w"], f32)[k * DSL:(k + 1) * DSL, :].T
            .reshape(RT, RP, DSL)),
        "dtb_sl": np.ascontiguousarray(
            np.asarray(inputs["L0_attn_dt_bias"], f32)[k * DSL:(k + 1) * DSL]),
        "A_sl": np.ascontiguousarray(
            np.asarray(inputs["L0_attn_A_log"], f32)[k * DSL:(k + 1) * DSL]),
        "D_sl": np.ascontiguousarray(
            np.asarray(inputs["L0_attn_D_param"], f32)[k * DSL:(k + 1) * DSL]),
        "convw_sl": np.ascontiguousarray(
            np.asarray(inputs["L0_attn_conv_w"], f32)[k * DSL:(k + 1) * DSL]),
        "convb_sl": np.ascontiguousarray(
            np.asarray(inputs["L0_attn_conv_b"], f32)[k * DSL:(k + 1) * DSL]),
        "wout_sl": np.ascontiguousarray(
            np.asarray(inputs["L0_attn_out_proj_w"], f32).T[k * DSL:(k + 1) * DSL, :]),
        "dt_ln": np.ascontiguousarray(np.asarray(inputs["L0_attn_dt_ln_w"], f32)),
        "B_ln": np.ascontiguousarray(np.asarray(inputs["L0_attn_B_ln_w"], f32)),
        "C_ln": np.ascontiguousarray(np.asarray(inputs["L0_attn_C_ln_w"], f32)),
        "wres_t": np.ascontiguousarray(np.concatenate(
            [np.asarray(inputs["L0_ffn_W_res"], f32),
             np.asarray(inputs["L0_ffn_W_pre"], f32),
             np.asarray(inputs["L0_ffn_W_post"], f32)], axis=0).T),
        "base_cat": np.ascontiguousarray(np.concatenate(
            [np.asarray(inputs["L0_ffn_H_res_base"], f32).ravel(),
             np.asarray(inputs["L0_ffn_H_pre_base"], f32).ravel(),
             np.asarray(inputs["L0_ffn_H_post_base"], f32).ravel()])),
        "wb_in": np.ascontiguousarray(np.asarray(inputs["L0_ffn_W_branch"], f32)),
        "gate_in": np.ascontiguousarray(np.asarray(inputs["L0_ffn_gate"], f32).reshape(1, 1)),
        "eye_f": np.eye(128, dtype=f32),
        "eye4_in": np.ascontiguousarray(np.eye(4, dtype=f32).ravel()),
    }


def kernel(**inputs):
    c = real_cfg()
    d = derived(c)
    nc = build_nc(c)
    in_maps = [_stage_inputs(inputs, c, k) for k in range(c["NC"])]
    with allow_ici_all_to_all():
        res = run_bass_kernel_spmd(nc, in_maps, core_ids=list(range(c["NC"])))
    B, T, H, QT = c["B"], c["T"], c["H"], d["QT"]
    out = np.zeros((B, T, H), np.float32)
    for k in range(c["NC"]):
        o = np.asarray(res.results[k]["out_my"])
        q = k % 4
        out[k // 4, q * QT:(q + 1) * QT, :] = o[0]
        out[2 + k // 4, q * QT:(q + 1) * QT, :] = o[1]
    return out
